# version 08
# baseline (speedup 1.0000x reference)
import numpy as np
from contextlib import ExitStack
import concourse.bass as bass
import concourse.mybir as mybir
import ml_dtypes

F32 = mybir.dt.float32
F32R = mybir.dt.float32r
BF16 = mybir.dt.bfloat16
AF = mybir.ActivationFunctionType
ALU = mybir.AluOpType
AX = mybir.AxisListType

S, D, H, DH, WIN = 2048, 1024, 16, 64, 256
NCH = S // WIN
DFF, LH = 4096, 512
TT, DT, FT, LFT = 16, 8, 32, 4
TOKC, NTC = 512, 4
EPS_LN = 1e-5
SHIFT = 80.0
NEG = -1.0e9
N_ITERS = 50
RCONST = float(np.float32(0.1) * np.log(np.float32(8.0), dtype=np.float32))
FTZ_DN = float(2.0 ** -24)
FTZ_UP = float(2.0 ** 24)
SQRT_D = float(np.sqrt(np.float32(1024.0)))
SBASE = 16512
VW = 65


def _np_masks():
    j = np.arange(128)[:, None]
    q = np.arange(256)[None, :]
    out = {}
    for s in (0, 1, 4, 5):
        jj = j + 128 * s
        valid = (q <= jj) & (jj <= q + 512)
        out[s] = np.where(valid, np.float32(0.0), np.float32(NEG)).astype(np.float32)
    return out


def subtiles_of_chunk(c):
    if c == 0:
        return [2, 3, 4, 5]
    if c == NCH - 1:
        return [0, 1, 2, 3]
    return [0, 1, 2, 3, 4, 5]


def masked_pairs(c):
    subs = subtiles_of_chunk(c)
    out = []
    if 0 in subs:
        out.append("01")
    if 4 in subs:
        out.append("45")
    return out


def pv_pairs(c):
    subs = subtiles_of_chunk(c)
    out = []
    for s_ in subs:
        for qh in (0, 1):
            if 128 * s_ + 127 >= 128 * qh and 128 * s_ <= 128 * qh + 127 + 512:
                out.append((s_, qh))
    return out


def build(n_rows, taps_on=False):
    nc = bass.Bass("TRN2", target_bir_lowering=False)
    es = ExitStack()

    semn = ("dX", "dM", "dG", "dW1a", "dW1b", "dW2a", "dW2b", "dRa", "dRb",
            "dOa", "dOb", "dF", "dN", "dI", "dInit", "dInitP", "dT", "dV",
            "tE", "vE", "sE", "pE")
    sems = {s_: es.enter_context(nc.semaphore(s_)) for s_ in semn}
    cnt = {s_: 0 for s_ in semn}

    def inc(instr, name, n=1):
        instr.then_inc(sems[name], n)
        cnt[name] += n
        return cnt[name]

    def wt(eng, name, val=None):
        v = cnt[name] if val is None else val
        if v > 0:
            eng.wait_ge(sems[name], v)

    x_in = nc.dram_tensor("x", [n_rows, S, D], F32, kind="ExternalInput").ap()
    out_d = nc.dram_tensor("out", [n_rows, S, D], F32, kind="ExternalOutput").ap()

    def din(name, shape):
        return nc.dram_tensor(name, shape, F32, kind="ExternalInput").ap()
    ln1_g, ln1_b = din("ln1_g", [D]), din("ln1_b", [D])
    ln2_g, ln2_b = din("ln2_g", [D]), din("ln2_b", [D])
    W1d, b1d = din("W1", [D, DFF]), din("b1", [DFF])
    W2d, b2d = din("W2", [DFF, D]), din("b2", [D])
    rtd = din("rt", [D])
    gam_l, w1ld, b1ld = din("gam_l", [D]), din("w1l", [D, LH]), din("b1l", [LH])
    w2ld, b2ld = din("w2l", [LH, D]), din("b2l", [D])
    gam_h, w1hd, b1hd = din("gam_h", [D]), din("w1h", [D, DFF]), din("b1h", [DFF])
    w2hd, b2hd = din("w2h", [DFF, D]), din("b2h", [D])

    W1bf = nc.dram_tensor("W1bf", [D, DFF], BF16, kind="Internal").ap()
    W2bf = nc.dram_tensor("W2bf", [DFF, D], BF16, kind="Internal").ap()
    w2hbf = nc.dram_tensor("w2hbf", [DFF, D], BF16, kind="Internal").ap()
    w2lbf = nc.dram_tensor("w2lbf", [LH, D], BF16, kind="Internal").ap()
    w1l_f = nc.dram_tensor("w1l_f", [D, LH], BF16, kind="Internal").ap()
    w1h_f = nc.dram_tensor("w1h_f", [D, DFF], BF16, kind="Internal").ap()

    taps = {}
    if taps_on:
        for tname in ("t_attn", "t_x1", "t_ff", "t_y", "t_z"):
            taps[tname] = nc.dram_tensor(tname, [n_rows, S, D], F32, kind="ExternalOutput").ap()
        for tname in ("t_s", "t_r", "t_m", "t_nrm"):
            taps[tname] = nc.dram_tensor(tname, [n_rows, 128, TT], F32, kind="ExternalOutput").ap()

    mk = _np_masks()
    ident_d = nc.inline_tensor(np.eye(128, dtype=np.float32), "ident_c").ap()
    mask01_d = nc.inline_tensor(np.concatenate([mk[0], mk[1]], 1), "mask01_c").ap()
    mask45_d = nc.inline_tensor(np.concatenate([mk[4], mk[5]], 1), "mask45_c").ap()
    ones_row_d = nc.inline_tensor(np.ones((1, 128), np.float32), "onesr_c").ap()
    ones_rowbf_d = nc.inline_tensor(np.ones((1, 128), ml_dtypes.bfloat16), "onesrb_c").ap()
    ones_col_d = nc.inline_tensor(np.ones((128, 1), np.float32), "onesc_c").ap()
    iota_d = nc.inline_tensor(
        np.broadcast_to(np.arange(S, dtype=np.float32)[None, :], (128, S)).copy(), "iota_c").ap()
    nshift_d = nc.inline_tensor(np.full((128, 1), -SHIFT, np.float32), "nshift_c").ap()
    jidx_d = nc.inline_tensor(
        (np.arange(TT, dtype=np.float32)[None, :] * 128
         + np.arange(128, dtype=np.float32)[:, None]).astype(np.float32), "jidx_c").ap()

    off = [SBASE]

    def sba(name, shape, dtype, at=None):
        per = int(np.prod(shape[1:])) * mybir.dt.size(dtype)
        if at is None:
            at = off[0]
            off[0] = at + ((per + 31) // 32) * 32
        h = nc.alloc_sbuf_tensor_at(name, shape, dtype, offset=at)
        return h.ap(), at

    Bt, _ = sba("Bt", [128, TT * D], F32)
    XT_at = off[0]
    XTA, _ = sba("XTA", [128, DT * 8 * 128], F32R)
    XTF, _ = sba("XTF", [128, DT * 8 * 128], BF16, at=XT_at)
    HT_at = off[0]
    HT, _ = sba("HT", [128, FT * TOKC], BF16)
    SCW, _ = sba("SCW", [128, 1664], F32, at=HT_at)
    CC_at = off[0]
    X0V, _ = sba("X0V", [128, 8 * H * VW], BF16, at=CC_at)
    FLS, _ = sba("FLS", [1, S], F32, at=XT_at + 16384)
    FLN, _ = sba("FLN", [1, S], F32, at=XT_at + 24576)
    PT, _ = sba("PT", [128, 2 * 6 * 256], BF16, at=CC_at + 16640)
    SQJ, _ = sba("SQJ", [128, D], F32, at=CC_at + 16640)
    NRMB0, _ = sba("NRMB0", [128, TOKC], F32, at=CC_at + 16640 + 4096)
    off[0] = CC_at + 16640 + 6144
    W1S, _ = sba("W1S", [128, 2 * DT * 128], BF16)
    W2S, _ = sba("W2S", [128, 2 * D], BF16)
    W2SL, _ = sba("W2SL", [128, (LH // 128) * D], BF16)
    M01, _ = sba("M01", [128, 512], F32, at=HT_at + 8192)
    M45, _ = sba("M45", [128, 512], F32, at=HT_at + 12288)
    IDT, _ = sba("IDT", [128, 128], F32)
    ONE_R, _ = sba("ONE_R", [1, 128], F32)
    ONE_RB, _ = sba("ONE_RB", [1, 128], BF16)
    ONE_C, _ = sba("ONE_C", [128, 1], F32)
    B2ROW, _ = sba("B2ROW", [1, D], BF16)
    B2LROW, _ = sba("B2LROW", [1, D], BF16)
    B2HROW, _ = sba("B2HROW", [1, D], BF16)
    B1C, _ = sba("B1C", [128, FT], F32)
    B1LC, _ = sba("B1LC", [128, LFT], F32)
    B1HC, _ = sba("B1HC", [128, FT], F32)
    RTB, _ = sba("RTB", [128, D], F32)
    GAML, _ = sba("GAML", [128, DT], F32)
    GAMH, _ = sba("GAMH", [128, DT], F32)
    GB0, _ = sba("GB0", [128, D], F32)
    GB1, _ = sba("GB1", [128, D], F32)
    X0S_at = off[0]
    X0S, _ = sba("X0S", [128, 2 * D], F32)
    RBC, _ = sba("RBC", [128, 512], F32, at=X0S_at)
    IOTC, _ = sba("IOTC", [128, 512], F32, at=X0S_at + 2048)
    GTS, _ = sba("GTS", [128, 512], F32, at=X0S_at + 4096)
    GTS2, _ = sba("GTS2", [128, 512], F32, at=X0S_at + 6144)
    OUT_at = off[0]
    OUTST, _ = sba("OUTST", [128, 2 * D], F32)
    LTM, _ = sba("LTM", [128, 512], F32, at=OUT_at)
    FL2, _ = sba("FL2", [16, 128], F32)
    FL2N, _ = sba("FL2N", [4, 4 * 128], F32)
    NRMB1, _ = sba("NRMB1", [128, TOKC], F32)
    SC16, _ = sba("SC16", [128, TT], F32)
    A16, _ = sba("A16", [128, TT], F32)
    RL16, _ = sba("RL16", [128, TT], F32)
    R16, _ = sba("R16", [128, TT], F32)
    M16, _ = sba("M16", [128, TT], F32)
    NRM, _ = sba("NRM", [128, TT], F32)
    G16, _ = sba("G16", [128, TT], F32)
    RNK, _ = sba("RNK", [128, TT], F32)
    GPART, _ = sba("GPART", [128, TT * 4], F32)
    EPART, _ = sba("EPART", [128, TT * 4], F32)
    JIDX, _ = sba("JIDX", [128, TT], F32)
    STAT, _ = sba("STAT", [128, 40], F32)
    ACOL, _ = sba("ACOL", [128, 1], F32)
    MXC, _ = sba("MXC", [128, 1], F32)
    RSC, _ = sba("RSC", [128, 4], F32)
    SEQS, _ = sba("SEQS", [1, 16], F32)
    NSH, _ = sba("NSH", [128, 1], F32)
    assert off[0] <= 229344, f"sbuf layout overflow: {off[0]}"

    pbank = [es.enter_context(nc.psum_tensor(f"pb{i}", [128, 512], F32)) for i in range(2)]
    psc = [es.enter_context(nc.psum_tensor(f"psc{i}", [128, 1536], F32)) for i in range(2)]
    pool_free = {0: None, 1: None}
    pool_next = [0]

    def pool_acquire(eng):
        k = pool_next[0]
        pool_next[0] = 1 - k
        fa = pool_free[k]
        if fa is not None:
            wt(eng, fa[0], fa[1])
            pool_free[k] = None
        return k

    def pool_release(k, semname, c_):
        pool_free[k] = (semname, c_)

    blk = es.enter_context(nc.Block())
    ck = {}
    pump_all_ref = [None]

    @blk.sync
    def _(se):
        for dst, src in ((IDT[:], ident_d),
                         (ONE_R[:], ones_row_d), (ONE_RB[:], ones_rowbf_d),
                         (ONE_C[:], ones_col_d), (JIDX[:], jidx_d), (NSH[:], nshift_d)):
            inc(se.dma_start(out=dst, in_=src), "dInit", 16)
        inc(se.dma_start(out=SCW[0:32, 0:128], in_=b1d.rearrange("(a b) -> a b", b=128)), "dInit", 16)
        inc(se.dma_start(out=SCW[0:32, 128:256], in_=b1hd.rearrange("(a b) -> a b", b=128)), "dInit", 16)
        inc(se.dma_start(out=SCW[0:4, 256:384], in_=b1ld.rearrange("(a b) -> a b", b=128)), "dInit", 16)
        inc(se.dma_start(out=SCW[0:8, 384:512], in_=gam_l.rearrange("(a b) -> a b", b=128)), "dInit", 16)
        inc(se.dma_start(out=SCW[0:8, 512:640], in_=gam_h.rearrange("(a b) -> a b", b=128)), "dInit", 16)
        inc(se.dma_start(out=SCW[0:1, 640:640 + D], in_=rtd[None, :]), "dInit", 16)

    @blk.gpsimd
    def _(ge):
        inc(ge.dma_start(out=W1bf[:], in_=W1d[:]), "dInitP", 16)
        inc(ge.dma_start(out=W2bf[:], in_=W2d[:]), "dInitP", 16)
        inc(ge.dma_start(out=w2hbf[:], in_=w2hd[:]), "dInitP", 16)
        inc(ge.dma_start(out=w2lbf[:], in_=w2ld[:]), "dInitP", 16)
        inc(ge.dma_start(out=B2ROW[:], in_=b2d[None, :]), "dInitP", 16)
        inc(ge.dma_start(out=B2LROW[:], in_=b2ld[None, :]), "dInitP", 16)
        inc(ge.dma_start(out=B2HROW[:], in_=b2hd[None, :]), "dInitP", 16)
    ck["precast_dP"] = cnt["dInitP"]

    @blk.sync
    def _(se):
        wt(se, "dInitP", ck["precast_dP"])
        src = w2lbf.rearrange("(fs p) c -> p fs c", p=128)
        inc(se.dma_start(out=W2SL[:].rearrange("p (fs c) -> p fs c", c=D), in_=src), "dInit", 16)
    ck["w2sl_dS"] = cnt["dInit"]

    @blk.tensor
    def _(te):
        wt(te, "dInit")
        pt = pbank[0]
        te.transpose(pt[:, 0:32], SCW[0:32, 0:128], IDT[0:32, 0:32])
        te.transpose(pt[:, 32:64], SCW[0:32, 128:256], IDT[0:32, 0:32])
        te.transpose(pt[:, 64:68], SCW[0:4, 256:384], IDT[0:4, 0:4])
        te.transpose(pt[:, 68:76], SCW[0:8, 384:512], IDT[0:8, 0:8])
        i = te.transpose(pt[:, 76:84], SCW[0:8, 512:640], IDT[0:8, 0:8])
        inc(i, "tE")
    ck["setup_t1"] = cnt["tE"]

    @blk.vector
    def _(ve):
        wt(ve, "tE", ck["setup_t1"])
        pt = pbank[0]
        ve.tensor_copy(B1C[:], pt[:, 0:32])
        ve.tensor_copy(B1HC[:], pt[:, 32:64])
        ve.tensor_copy(B1LC[:], pt[:, 64:68])
        ve.tensor_copy(GAML[:], pt[:, 68:76])
        i = ve.tensor_copy(GAMH[:], pt[:, 76:84])
        inc(i, "vE")
        i = ve.memset(X0V[:].rearrange("p (sh e) -> p sh e", e=VW)[:, :, 64:65], 1.0)
        inc(i, "vE")
    ck["setup_v"] = cnt["vE"]
    ck["x0v_ones_setup"] = cnt["vE"]
    pool_free[0] = ("vE", ck["setup_v"])

    for half in range(2):
        @blk.tensor
        def _(te, half=half):
            k = pool_acquire(te)
            ck["_bk"] = k
            i = te.matmul(pbank[k][:, 0:512], ONE_R[:],
                          SCW[0:1, 640 + half * 512:640 + half * 512 + 512], start=True, stop=True)
            inc(i, "tE")
        k = ck["_bk"]

        @blk.vector
        def _(ve, half=half, k=k):
            wt(ve, "tE")
            i = ve.tensor_copy(RTB[:, half * 512:half * 512 + 512], pbank[k][:, 0:512])
            inc(i, "vE")
        pool_release(k, "vE", cnt["vE"])
    ck["rtb_v"] = cnt["vE"]

    fold_jobs = [(w1ld, w1l_f, GAML, dt_i, 0, LH) for dt_i in range(DT)]
    for dt_i in range(DT):
        for cc in range(4):
            fold_jobs.append((w1hd, w1h_f, GAMH, dt_i, cc * 1024, 1024))
    for (wsrc, wdst, gt, dt_i, c0, wdt) in fold_jobs:
        @blk.sync
        def _(se, wsrc=wsrc, dt_i=dt_i, c0=c0, wdt=wdt):
            wt(se, "dInitP")
            wt(se, "vE", ck["setup_v"])
            i = se.dma_start(out=SCW[:, 0:wdt], in_=wsrc[dt_i * 128:(dt_i + 1) * 128, c0:c0 + wdt])
            inc(i, "dInit", 16)

        @blk.vector
        def _(ve, gt=gt, dt_i=dt_i, wdt=wdt):
            wt(ve, "dInit")
            i = ve.tensor_scalar(SCW[:, 0:wdt], SCW[:, 0:wdt], gt[:, dt_i:dt_i + 1], None, ALU.mult)
            inc(i, "vE")

        @blk.gpsimd
        def _(ge, wdst=wdst, dt_i=dt_i, c0=c0, wdt=wdt):
            wt(ge, "vE")
            i = ge.dma_start(out=wdst[dt_i * 128:(dt_i + 1) * 128, c0:c0 + wdt], in_=SCW[:, 0:wdt])
            inc(i, "dInitP", 16)
    ck["fold_dP"] = cnt["dInitP"]

    def rc(ds, tile):
        return ds * (8 * 128) + (tile % 8) * 128

    def emit_xt_ring(t, gates, dst):
        for ds in range(DT):
            @blk.tensor
            def _(te, t=t, ds=ds, gates=gates):
                for g in gates:
                    wt(te, g[0], g[1])
                k = pool_acquire(te)
                i = te.transpose(pbank[k][:, 0:128],
                                 Bt[:, t * D + ds * 128: t * D + ds * 128 + 128], IDT[:])
                inc(i, "tE")
                ck["_tpk"] = k
            k = ck["_tpk"]

            @blk.vector
            def _(ve, t=t, ds=ds, k=k, dst=dst):
                wt(ve, "tE")
                i = ve.tensor_copy(dst[:, rc(ds, t):rc(ds, t) + 128], pbank[k][:, 0:128])
                inc(i, "vE")
            pool_release(k, "vE", cnt["vE"])

    def emit_gb_bcast(col0, dst, gate_ds, v_gate=0):
        for half in range(2):
            @blk.tensor
            def _(te, half=half, col0=col0, gate_ds=gate_ds):
                wt(te, "dG", gate_ds)
                k = pool_acquire(te)
                i = te.matmul(pbank[k][:, 0:512], ONE_R[:],
                              FLN[0:1, col0 + half * 512:col0 + half * 512 + 512],
                              start=True, stop=True)
                inc(i, "tE")
                ck["_bk"] = k
            k = ck["_bk"]

            @blk.vector
            def _(ve, half=half, dst=dst, k=k, v_gate=v_gate):
                wt(ve, "tE")
                if v_gate:
                    wt(ve, "vE", v_gate)
                i = ve.tensor_copy(dst[:, half * 512:half * 512 + 512], pbank[k][:, 0:512])
                inc(i, "vE")
            pool_release(k, "vE", cnt["vE"])

    def emit_nrmb(tcq):
        @blk.tensor
        def _(te, tcq=tcq, dn_now=None):
            wt(te, "dN", cnt["dN"])
            wt(te, "vE", ck.get(("nrmbu", tcq - 2), ck.get(("nrmbu", tcq + 2), 0)))
            kk = pool_acquire(te)
            ck["_nbk"] = kk
            i = te.matmul(pbank[kk][:, 0:512], ONE_R[:],
                          FLN[0:1, tcq * 512:(tcq + 1) * 512], start=True, stop=True)
            inc(i, "tE")
        ck[("nrmbp", tcq)] = cnt["tE"]
        kk = ck["_nbk"]

        @blk.vector
        def _(ve, tcq=tcq, kk=kk):
            wt(ve, "tE", ck[("nrmbp", tcq)])
            i = ve.tensor_copy((NRMB0 if tcq % 2 == 0 else NRMB1)[:],
                               pbank[kk][:, 0:512])
            inc(i, "vE")
        pool_release(kk, "vE", cnt["vE"])
        ck[("nrmb_v", tcq)] = cnt["vE"]

    def emit_ln_apply(t, gpar, i_, kind, extra_gates=(), osl=None, osl_key="osl_free"):
        c0 = gpar * 16
        bsl = Bt[:, t * D:(t + 1) * D]
        if osl is None:
            osl = OUTST[:, (t % 2) * D:(t % 2) * D + D]
        sc2c = STAT[:, c0 + 12 + i_:c0 + 13 + i_]
        rsdc = STAT[:, c0 + 8 + i_:c0 + 9 + i_]
        if kind == "ln_aff":
            @blk.scalar
            def _(sa, t=t, sc2c=sc2c, rsdc=rsdc, bsl=bsl, osl=osl, gpar=gpar):
                wt(sa, "vE", ck[("lng_v8", gpar)])
                wt(sa, "vE", ck.get(("osl_free", t % 2), 0))
                for g in extra_gates:
                    wt(sa, g[0], g[1])
                i = sa.activation(osl, bsl, AF.Identity, bias=sc2c, scale=rsdc)
                inc(i, "sE")
                ck["_l9"] = cnt["sE"]

            @blk.vector
            def _(ve, t=t, osl=osl, bsl=bsl):
                wt(ve, "sE", ck["_l9"])
                i = ve.tensor_mul(osl, osl, GB0[:])
                inc(i, "vE")
                ve.wait_ge(sems["vE"], cnt["vE"])
                i = ve.tensor_add(bsl, osl, GB1[:])
                inc(i, "vE")
            ck[("ln_v", t)] = cnt["vE"]
            ck[("osl_free", t % 2)] = cnt["vE"]
            ck[("lng_gate_s", gpar)] = ck["_l9"]
        else:
            dst = bsl if kind == "plain" else osl

            @blk.scalar
            def _(sa, t=t, dst=dst, kind=kind, sc2c=sc2c, rsdc=rsdc, bsl=bsl, gpar=gpar,
                  osl_key=osl_key):
                wt(sa, "vE", ck[("lng_v8", gpar)])
                if kind == "plain_out":
                    wt(sa, "vE", ck.get((osl_key, t % 2), 0))
                for g in extra_gates:
                    wt(sa, g[0], g[1])
                i = sa.activation(dst, bsl, AF.Identity, bias=sc2c, scale=rsdc)
                inc(i, "sE")
            ck[("ln_s", t)] = cnt["sE"]
            ck[("lng_gate_s", gpar)] = cnt["sE"]

    def emit_ln_group(tlist, kind, gpar, gates=(), tile_gates=None, applies=True):
        tlist = list(tlist)
        n = len(tlist)
        c0 = gpar * 16
        ngm = STAT[:, c0:c0 + n]
        var = STAT[:, c0 + 4:c0 + 4 + n]
        rsd = STAT[:, c0 + 8:c0 + 8 + n]
        sc2 = STAT[:, c0 + 12:c0 + 12 + n]

        @blk.vector
        def _(ve, gates=gates, gpar=gpar, tlist=tlist):
            wt(ve, "vE", ck.get(("lng_gate_v", gpar), 0))
            wt(ve, "sE", ck.get(("lng_gate_s", gpar), 0))
            for g in gates:
                wt(ve, g[0], g[1])
            for i_, t in enumerate(tlist):
                if tile_gates is not None:
                    for g in tile_gates(t):
                        wt(ve, g[0], g[1])
                bsl = Bt[:, t * D:(t + 1) * D]
                i = ve.reduce_sum(STAT[:, c0 + i_:c0 + i_ + 1], bsl, axis=AX.X, negate=True)
                inc(i, "vE")
                ve.wait_ge(sems["vE"], cnt["vE"])
                i = ve.scalar_tensor_tensor(SQJ[:], bsl, 1.0, bsl, ALU.mult, ALU.mult,
                                            accum_out=STAT[:, c0 + 4 + i_:c0 + 5 + i_])
                inc(i, "vE")
                ve.wait_ge(sems["vE"], cnt["vE"])
            i = ve.tensor_scalar(ngm, ngm, 1.0 / D, None, ALU.mult)
            inc(i, "vE")
            ve.wait_ge(sems["vE"], cnt["vE"])
            i = ve.tensor_scalar(var, var, 1.0 / D, EPS_LN, ALU.mult, ALU.add)
            inc(i, "vE")
            ve.wait_ge(sems["vE"], cnt["vE"])
            i = ve.tensor_tensor(sc2, ngm, ngm, ALU.mult)
            inc(i, "vE")
            ve.wait_ge(sems["vE"], cnt["vE"])
            i = ve.tensor_tensor(var, var, sc2, ALU.subtract)
            inc(i, "vE")
            ck["_g4"] = cnt["vE"]

        @blk.scalar
        def _(sa, gpar=gpar):
            wt(sa, "vE", ck["_g4"])
            i = sa.activation(rsd, var, AF.Ln)
            inc(i, "sE")
            sa.wait_ge(sems["sE"], cnt["sE"])
            i = sa.activation(rsd, rsd, AF.Exp, bias=0.0, scale=-0.5)
            inc(i, "sE")
            ck[("lng_s7", gpar)] = cnt["sE"]

        @blk.vector
        def _(ve, gpar=gpar):
            wt(ve, "sE", ck[("lng_s7", gpar)])
            i = ve.tensor_tensor(sc2, ngm, rsd, ALU.mult)
            inc(i, "vE")
        ck[("lng_v8", gpar)] = cnt["vE"]
        ck[("lng_gate_v", gpar)] = cnt["vE"]

        if applies:
            for i_, t in enumerate(tlist):
                emit_ln_apply(t, gpar, i_, kind)

    def pump(n=1):
        q = ck.get("_ROUT")
        for _ in range(n):
            if q:
                q.pop(0)()

    def pump_all():
        q = ck.get("_ROUT")
        while q:
            q.pop(0)()
    pump_all_ref[0] = pump_all

    def ffn_pass(tc, w1src, b1col, w2src, b2row, act, masked, nft, tag, hbase=0):
        slot0 = (4 * tc) % 8
        vE0 = ck.get("prev_yev_v", 0)
        sE0 = ck.get("prev_exp_s", 0)
        pheng, phval = ck.get("prev_hev", ("sE", 0))
        heng = "vE" if act == "relu" else "sE"
        for ft in range(nft):
            @blk.sync
            def _(se, ft=ft, tc=tc, w1src=w1src, tag=tag):
                wt(se, "tE", ck.get(("hmm", tag, tc, ft - 2), ck.get("ffn_prev_t", 0)))
                wt(se, "dInitP", ck["fold_dP"])
                src = w1src[:, ft * 128:(ft + 1) * 128].rearrange("(ds p) c -> p ds c", p=128)
                sem1 = "dW1a" if ft % 2 == 0 else "dW1b"
                i = se.dma_start(out=W1S[:, (ft % 2) * 1024:(ft % 2) * 1024 + 1024]
                                 .rearrange("p (ds c) -> p ds c", c=128), in_=src)
                inc(i, sem1, 16)
            ck[("w1s", tag, tc, ft)] = ("dW1a" if ft % 2 == 0 else "dW1b",
                                        cnt["dW1a" if ft % 2 == 0 else "dW1b"])

            @blk.tensor
            def _(te, ft=ft, tc=tc, slot0=slot0, tag=tag, sE0=sE0, vE0=vE0, act=act, hbase=hbase,
                  pheng=pheng, phval=phval, heng=heng):
                wt(te, ck[("w1s", tag, tc, ft)][0], ck[("w1s", tag, tc, ft)][1])
                wt(te, "vE", ck.get(("ring_ok", tc), 0))
                if ft < 2:
                    wt(te, "vE", vE0)
                    wt(te, pheng, phval)
                wt(te, "sE", sE0)
                wt(te, heng, ck.get(("hev", tag, tc, ft - 2), 0))
                if act == "gelu":
                    wt(te, "vE", max(ck.get("prev_hpre_v", 0) if ft < 2 else 0,
                                     ck.get(("hpre", tag, tc, ft - 2), 0)))
                pt = psc[0][:, (ft % 2) * 512:(ft % 2) * 512 + 512]
                last = None
                for ds in range(DT):
                    last = te.matmul(
                        pt, W1S[:, (ft % 2) * 1024 + ds * 128:(ft % 2) * 1024 + ds * 128 + 128],
                        XTF[:, ds * 1024 + slot0 * 128: ds * 1024 + slot0 * 128 + 512],
                        start=(ds == 0), stop=(ds == DT - 1))
                inc(last, "tE")
            ck[("hmm", tag, tc, ft)] = cnt["tE"]

            if act == "relu":
                @blk.vector
                def _(ve, ft=ft, tc=tc, b1col=b1col, tag=tag, hbase=hbase):
                    wt(ve, "tE", ck[("hmm", tag, tc, ft)])
                    ve.wait_ge(sems["vE"], cnt["vE"])
                    i = ve.tensor_scalar(HT[:, (hbase + ft) * TOKC:(hbase + ft + 1) * TOKC],
                                         psc[0][:, (ft % 2) * 512:(ft % 2) * 512 + 512],
                                         b1col[:, ft:ft + 1], 0.0, ALU.add, ALU.max)
                    inc(i, "vE")
                ck[("hev", tag, tc, ft)] = cnt["vE"]
            else:
                @blk.vector
                def _(ve, ft=ft, tc=tc, tag=tag, sE0=sE0, pheng=pheng, phval=phval):
                    wt(ve, "tE", ck[("hmm", tag, tc, ft)])
                    wt(ve, "vE", ck.get(("nrmb_v", tc), 0))
                    if ft < 2:
                        wt(ve, pheng, phval)
                    wt(ve, "sE", max(sE0, ck.get(("hev", tag, tc, ft - 2), 0)))
                    i = ve.tensor_tensor(OUTST[:, (ft % 2) * 512 + 1024:(ft % 2) * 512 + 1536],
                                         psc[0][:, (ft % 2) * 512:(ft % 2) * 512 + 512],
                                         (NRMB0 if tc % 2 == 0 else NRMB1)[:], ALU.mult)
                    inc(i, "vE")
                ck[("hpre", tag, tc, ft)] = cnt["vE"]

                @blk.scalar
                def _(sa, ft=ft, tc=tc, b1col=b1col, tag=tag, hbase=hbase):
                    wt(sa, "vE", ck[("hpre", tag, tc, ft)])
                    i = sa.activation(HT[:, (hbase + ft) * TOKC:(hbase + ft + 1) * TOKC],
                                      OUTST[:, (ft % 2) * 512 + 1024:(ft % 2) * 512 + 1536],
                                      AF.Gelu, bias=b1col[:, ft:ft + 1], scale=1.0)
                    inc(i, "sE")
                ck[("hev", tag, tc, ft)] = cnt["sE"]
            if ft % 2 == 1:
                pump(1)

        if masked:
            pump_all()
        @blk.tensor
        def _(te, tc=tc, tag=tag, nft=nft, heng=heng):
            wt(te, heng, ck[("hev", tag, tc, nft - 1)])
            wt(te, heng, ck.get(("hev", tag, tc, nft - 2), 0))
            k6 = pool_acquire(te)
            k7 = pool_acquire(te)
            ck["_yk67"] = (k6, k7)
        k6, k7 = ck["_yk67"]
        ybank = {}
        for tt_ in range(4):
            for dh in range(2):
                idx = 2 * tt_ + dh
                if idx < 6:
                    ybank[(tt_, dh)] = psc[idx // 3][:, (idx % 3) * 512:(idx % 3) * 512 + 512]
                else:
                    ybank[(tt_, dh)] = pbank[(k6, k7)[idx - 6]][:]

        use_stream = w2src is not None
        for fs in range(nft):
            if use_stream:
                @blk.sync
                def _(se, fs=fs, tc=tc, tag=tag, w2src=w2src):
                    wt(se, "tE", ck.get(("ymm", tag, tc, fs - 2), 0))
                    wt(se, "dInitP", ck["precast_dP"])
                    src = w2src[fs * 128:(fs + 1) * 128, :]
                    sem2 = "dW2a" if fs % 2 == 0 else "dW2b"
                    i = se.dma_start(out=W2S[:, (fs % 2) * D:(fs % 2) * D + D], in_=src)
                    inc(i, sem2, 16)
                ck[("w2s", tag, tc, fs)] = ("dW2a" if fs % 2 == 0 else "dW2b",
                                            cnt["dW2a" if fs % 2 == 0 else "dW2b"])

            @blk.tensor
            def _(te, fs=fs, tc=tc, tag=tag, use_stream=use_stream, vE0=vE0, ybank=ybank, hbase=hbase):
                if use_stream:
                    wt(te, ck[("w2s", tag, tc, fs)][0], ck[("w2s", tag, tc, fs)][1])
                else:
                    wt(te, "dInit", ck["w2sl_dS"])
                if fs == 0:
                    wt(te, "vE", max(vE0, ck.get(("yev_all", tag, tc), 0)))
                last = None
                for tt_ in range(4):
                    for dh in range(2):
                        pt = ybank[(tt_, dh)]
                        rhs = (W2S[:, (fs % 2) * D + dh * 512:(fs % 2) * D + dh * 512 + 512]
                               if use_stream
                               else W2SL[:, fs * D + dh * 512:fs * D + dh * 512 + 512])
                        last = te.matmul(pt,
                                         HT[:, (hbase + fs) * TOKC + tt_ * 128:
                                            (hbase + fs) * TOKC + tt_ * 128 + 128],
                                         rhs, start=(fs == 0), stop=False)
                inc(last, "tE")
            ck[("ymm", tag, tc, fs)] = cnt["tE"]

        @blk.tensor
        def _(te, tc=tc, tag=tag, b2row=b2row, ybank=ybank):
            last = None
            for tt_ in range(4):
                for dh in range(2):
                    last = te.matmul(ybank[(tt_, dh)], ONE_RB[:],
                                     b2row[0:1, dh * 512:dh * 512 + 512],
                                     start=False, stop=True)
            inc(last, "tE")
        ck[("yb2", tag, tc)] = cnt["tE"]

        for tt_ in range(4):
            t_abs = tc * 4 + tt_
            for dh in range(2):
                @blk.vector
                def _(ve, tt_=tt_, dh=dh, tc=tc, t_abs=t_abs, masked=masked, tag=tag, ybank=ybank):
                    wt(ve, "tE", ck[("yb2", tag, tc)])
                    wt(ve, "dT", ck.get("hold", 0))
                    pt = ybank[(tt_, dh)]
                    bsl = Bt[:, t_abs * D + dh * 512: t_abs * D + dh * 512 + 512]
                    if masked:
                        wt(ve, "vE", ck.get("mask_v", 0))
                        i = ve.scalar_tensor_tensor(bsl, pt, M16[:, t_abs:t_abs + 1], bsl,
                                                    ALU.mult, ALU.add)
                    else:
                        i = ve.tensor_add(bsl, bsl, pt)
                    inc(i, "vE")
                ck[("yev", tag, tc, tt_, dh)] = cnt["vE"]
        ck[("yev_all", tag, tc)] = cnt["vE"]
        pool_release(k6, "vE", cnt["vE"])
        pool_release(k7, "vE", cnt["vE"])
        ck["ffn_prev_t"] = cnt["tE"]
        ck["prev_yev_v"] = cnt["vE"]
        ck["prev_hev"] = (heng, ck[("hev", tag, tc, nft - 1)])
        if act == "gelu":
            ck["prev_hpre_v"] = ck[("hpre", tag, tc, nft - 1)]

    for r in range(n_rows):
        @blk.sync
        def _(se, r=r):
            wt(se, "dOa", ck.get(("row_st_a", r - 1), 0))
            wt(se, "dOb", ck.get(("row_st_b", r - 1), 0))
            wt(se, "sE", ck.get(("row_lastln", r - 1), 0))
            src = x_in[r].rearrange("(t p) c -> p t c", p=128)
            i = se.dma_start(out=Bt[:].rearrange("p (t c) -> p t c", c=D), in_=src)
            inc(i, "dX", 16)
            wt(se, "tE", ck.get(("hv_done", r - 1), 0))
            inc(se.dma_start(out=M01[:], in_=mask01_d), "dM", 16)
            inc(se.dma_start(out=M45[:], in_=mask45_d), "dM", 16)
        ck[("x0", r)] = cnt["dX"]
        ck[("msk", r)] = cnt["dM"]

        def emit_x0v(t, extra_gate_t):
            sl = t % 8

            @blk.gpsimd
            def _(ge, t=t, r=r, sl=sl, extra_gate_t=extra_gate_t):
                wt(ge, "dX", ck[("x0", r)])
                if extra_gate_t:
                    wt(ge, "tE", extra_gate_t)
                src = Bt[:, t * D:(t + 1) * D].rearrange("p (h d) -> p h d", d=64)
                blkv = X0V[:, sl * H * VW:(sl + 1) * H * VW].rearrange("p (h e) -> p h e", e=VW)
                i = ge.dma_start(out=blkv[:, :, 0:64], in_=src)
                inc(i, "dV", 16)
            ck[("x0v", r, t)] = cnt["dV"]

        for t in range(4):
            emit_x0v(t, ck.get(("hv_done", r - 1), 0) if t == 0 else 0)
        row0_vE = cnt["vE"]
        row0_sE = cnt["sE"]
        for t in range(4):
            emit_xt_ring(t, [("dX", ck[("x0", r)]),
                             ("tE", ck.get(("hv_done", r - 1), 0))], XTA)
            ck[("rxt", r, t)] = cnt["vE"]

        def scores_emit(c, h):
            par = h % 2
            subs = subtiles_of_chunk(c)
            dt_i = h // 2
            r0 = (h % 2) * 64
            pk = ("exp", c, h - 2) if h >= 2 else ("exp", c - 1, H - 2 + h)

            @blk.tensor
            def _(te, c=c, h=h, par=par, subs=subs, dt_i=dt_i, r0=r0, pk=pk, r=r):
                wt(te, "sE", ck.get(pk, 0))
                if c == 0 and h < 2:
                    wt(te, "vE", row0_vE)
                    wt(te, "sE", row0_sE)
                wt(te, "vE", ck.get(("ring_rdy", c), 0))
                last = None
                for s_ in subs:
                    ktile = 2 * c - 2 + s_
                    col = rc(dt_i, ktile)
                    q0 = rc(dt_i, 2 * c)
                    last = te.matmul(psc[par][:, s_ * 256:s_ * 256 + 256],
                                     XTA[r0:r0 + 64, col:col + 128],
                                     XTA[r0:r0 + 64, q0:q0 + 256],
                                     start=True, stop=True, tile_position=(r0, 0))
                inc(last, "tE")
            ck[("scores", c, h)] = cnt["tE"]

        def mask_emit(c, h):
            par = h % 2
            mp = masked_pairs(c)

            @blk.vector
            def _(ve, c=c, h=h, par=par, mp=mp, r=r):
                wt(ve, "tE", ck[("scores", c, h)])
                wt(ve, "dM", ck[("msk", r)])
                last = None
                for pair in mp:
                    off = 0 if pair == "01" else 1024
                    msk = M01 if pair == "01" else M45
                    last = ve.tensor_tensor(psc[par][:, off:off + 512],
                                            psc[par][:, off:off + 512], msk[:], ALU.add)
                inc(last, "vE")
            ck[("mask", c, h)] = cnt["vE"]

        def exp_emit(c, h):
            par = h % 2
            subs = subtiles_of_chunk(c)
            pvk = ("pv", c, h - 2) if h >= 2 else ("pv", c - 1, H - 2 + h)

            @blk.scalar
            def _(sa, c=c, h=h, par=par, subs=subs, pvk=pvk):
                wt(sa, "vE", ck[("mask", c, h)])
                wt(sa, "tE", ck[("scores", c, h)])
                wt(sa, "tE", ck.get(pvk, 0))
                w0 = min(subs) * 256
                w1 = (max(subs) + 1) * 256
                i = sa.activation(PT[:, par * 1536 + w0: par * 1536 + w1],
                                  psc[par][:, w0:w1], AF.Exp, bias=NSH[:, 0:1], scale=1.0)
                inc(i, "sE")
            ck[("exp", c, h)] = cnt["sE"]

        def pv_emit(c, h):
            par = h % 2
            pairs = pv_pairs(c)

            @blk.tensor
            def _(te, c=c, h=h, par=par, pairs=pairs, r=r):
                wt(te, "sE", ck[("exp", c, h)])
                tk = min(2 * c + 3 if h < 4 else (2 * c + 4 if h < 8 else 2 * c + 5), TT - 1)
                wt(te, "dV", ck[("x0v", r, tk)])
                wt(te, "vE", ck["x0v_ones_setup"])
                k = pool_acquire(te)
                np_ = {q_: sum(1 for s2, q2 in pairs if q2 == q_) for q_ in (0, 1)}
                done = {0: 0, 1: 0}
                lastq = {}
                for qh in (0, 1):
                    for s_, q_ in pairs:
                        if q_ != qh:
                            continue
                        ktile = 2 * c - 2 + s_
                        sl = ktile % 8
                        lhs = PT[:, par * 1536 + s_ * 256 + qh * 128:
                                 par * 1536 + s_ * 256 + qh * 128 + 128]
                        rhs = X0V[:, sl * H * VW + h * VW: sl * H * VW + h * VW + VW]
                        done[qh] += 1
                        lastq[qh] = te.matmul(pbank[k][:, qh * 128:qh * 128 + VW], lhs, rhs,
                                              start=(done[qh] == 1), stop=(done[qh] == np_[qh]))
                for qh in (0, 1):
                    inc(lastq[qh], "tE")
                    ck[("pvq", c, h, qh)] = cnt["tE"]
                ck[("pvk", c, h)] = k
            ck[("pv", c, h)] = cnt["tE"]

        def pvevac_emit(c, h):
            k = ck[("pvk", c, h)]
            for qh in (0, 1):
                @blk.vector
                def _(ve, c=c, h=h, qh=qh, k=k):
                    wt(ve, "tE", ck[("pvq", c, h, 1)])
                    i = ve.reciprocal(RSC[:, qh:qh + 1], pbank[k][:, qh * 128 + 64:qh * 128 + 65])
                    inc(i, "vE")
                    ve.wait_ge(sems["vE"], cnt["vE"])
                    t_abs = 2 * c + qh
                    bsl = Bt[:, t_abs * D + h * 64: t_abs * D + h * 64 + 64]
                    i = ve.scalar_tensor_tensor(bsl, pbank[k][:, qh * 128:qh * 128 + 64],
                                                RSC[:, qh:qh + 1],
                                                bsl, ALU.mult, ALU.add)
                    inc(i, "vE")
                ck[("pvadd", c, h, qh)] = cnt["vE"]
            pool_release(k, "vE", ck[("pvadd", c, h, 1)])

        for c in range(NCH):
            ck[("ring_rdy", c)] = ck[("rxt", r, min(2 * c + 3, TT - 1))]
            for h in range(H):
                if h == 0:
                    scores_emit(c, 0)
                    mask_emit(c, 0)
                    exp_emit(c, 0)
                if h + 1 < H:
                    scores_emit(c, h + 1)
                    mask_emit(c, h + 1)
                    exp_emit(c, h + 1)
                if h in (4, 8):
                    t_new = 2 * c + 4 + (1 if h == 8 else 0)
                    if t_new < TT:
                        emit_x0v(t_new, ck.get(("pv", c, 3 if h == 4 else 7), 0))
                        emit_xt_ring(t_new, [("dX", ck[("x0", r)])], XTA)
                        ck[("rxt", r, t_new)] = cnt["vE"]
                pv_emit(c, h)
                pvevac_emit(c, h)
        ck[("pv_all", r)] = cnt["tE"]
        ck[("attn_v", r)] = cnt["vE"]
        ck["prev_exp_s"] = cnt["sE"]
        ck["prev_yev_v"] = cnt["vE"]
        ck["prev_hev"] = ("sE", cnt["sE"])
        if taps_on:
            @blk.gpsimd
            def _(ge, r=r):
                wt(ge, "vE", ck[("attn_v", r)])
                i = ge.dma_start(out=taps["t_attn"][r].rearrange("(t p) c -> p t c", p=128),
                                 in_=Bt[:].rearrange("p (t c) -> p t c", c=D))
                inc(i, "dT", 16)
            ck["hold"] = cnt["dT"]

        @blk.sync
        def _(se, r=r):
            wt(se, "vE", ck[("attn_v", r)])
            wt(se, "tE", ck[("pv_all", r)])
            if taps_on:
                wt(se, "dT", ck.get("hold", 0))
            i = se.dma_start(out=FLN[0:1, 0:D], in_=ln1_g[None, :])
            inc(i, "dG", 16)
            i = se.dma_start(out=FLN[0:1, D:2 * D], in_=ln1_b[None, :])
            inc(i, "dG", 16)
        ck["gbrow"] = cnt["dG"]
        emit_gb_bcast(0, GB0, ck["gbrow"])
        emit_gb_bcast(D, GB1, ck["gbrow"])
        ck["gb_v"] = cnt["vE"]
        ck["gb1_t"] = cnt["tE"]

        for g4 in range(4):
            emit_ln_group(range(g4 * 4, g4 * 4 + 4), "ln_aff", g4 % 2,
                          gates=[("vE", ck["gb_v"]), ("vE", ck[("attn_v", r)])])
        ck["ln1_all_v"] = cnt["vE"]
        for t in range(8):
            emit_xt_ring(t, [("vE", ck[("ln_v", t)]), ("tE", ck.get("ffn_prev_t", 0))], XTF)
        ck[("ring_ok", 0)] = cnt["vE"]
        ck[("ring_ok", 1)] = cnt["vE"]

        @blk.sync
        def _(se, r=r):
            wt(se, "tE", ck["gb1_t"])
            wt(se, "vE", ck[("attn_v", r)])
            if taps_on:
                wt(se, "dT", ck.get("hold", 0))
            i = se.dma_start(out=FLN[0:1, 0:D], in_=ln2_g[None, :])
            inc(i, "dG", 16)
            i = se.dma_start(out=FLN[0:1, D:2 * D], in_=ln2_b[None, :])
            inc(i, "dG", 16)
        ck["gbrow2"] = cnt["dG"]
        emit_gb_bcast(0, GB0, ck["gbrow2"], v_gate=ck["ln1_all_v"])
        emit_gb_bcast(D, GB1, ck["gbrow2"], v_gate=ck["ln1_all_v"])
        ck["gb2_v"] = cnt["vE"]
        ck["gb2_t"] = cnt["tE"]

        def p4_chunk(tcq):
            ts4 = list(range(tcq * 4, tcq * 4 + 4))
            gpar = tcq % 2
            emit_ln_group(ts4, "ln_aff", gpar,
                          gates=[("vE", ck["gb2_v"]), ("vE", ck[("yev_all", "a", tcq)])])

            for t in ts4:
                par = t % 2

                @blk.sync
                def _(se, t=t, r=r, par=par, tcq=tcq):
                    wt(se, "vE", ck.get(("resid", t - 2), 0))
                    wt(se, "vE", ck.get(("sdot", tcq * 4 - 1), 0))
                    semr = "dRa" if par == 0 else "dRb"
                    i = se.dma_start(out=X0S[:, par * D:(par + 1) * D],
                                     in_=x_in[r][t * 128:(t + 1) * 128, :])
                    inc(i, semr, 16)
                ck[("x0rr", t)] = ("dRa" if par == 0 else "dRb",
                                  cnt["dRa" if par == 0 else "dRb"])

                @blk.vector
                def _(ve, t=t, par=par):
                    wt(ve, ck[("x0rr", t)][0], ck[("x0rr", t)][1])
                    wt(ve, "vE", ck[("ln_v", t)])
                    bsl = Bt[:, t * D:(t + 1) * D]
                    i = ve.tensor_add(bsl, bsl, X0S[:, par * D:(par + 1) * D])
                    inc(i, "vE")
                ck[("resid", t)] = cnt["vE"]

            emit_ln_group(ts4, "plain", gpar,
                          tile_gates=lambda t: [("vE", ck[("resid", t)])])

            c0g = gpar * 16
            @blk.vector
            def _(ve, tcq=tcq, c0g=c0g, gpar=gpar):
                wt(ve, "sE", ck[("lng_s7", gpar)])
                g1 = G16[:, tcq * 4:tcq * 4 + 4]
                i = ve.tensor_tensor(g1, STAT[:, c0g + 8:c0g + 12],
                                     STAT[:, c0g + 8:c0g + 12], ALU.mult)
                inc(i, "vE")
                ve.wait_ge(sems["vE"], cnt["vE"])
                i = ve.tensor_scalar(NRM[:, tcq * 4:tcq * 4 + 4], g1,
                                     -float(D) * EPS_LN, float(D), ALU.mult, ALU.add)
                inc(i, "vE")
                ve.wait_ge(sems["vE"], cnt["vE"])
                i = ve.tensor_scalar(NRM[:, tcq * 4:tcq * 4 + 4], NRM[:, tcq * 4:tcq * 4 + 4],
                                     1e-24, None, ALU.max)
                inc(i, "vE")
            ck[("nq0", tcq)] = cnt["vE"]
            ck[("lng_gate_v", gpar)] = cnt["vE"]

            for t in ts4:
                @blk.vector
                def _(ve, t=t, tcq=tcq):
                    wt(ve, "sE", ck[("ln_s", t)])
                    wt(ve, "vE", ck["rtb_v"])
                    wt(ve, ck[("x0rr", tcq * 4 + 3)][0], ck[("x0rr", tcq * 4 + 3)][1])
                    ve.wait_ge(sems["vE"], cnt["vE"])
                    i = ve.scalar_tensor_tensor(GTS[:, 0:512], Bt[:, t * D:t * D + 512], 1.0,
                                                RTB[:, 0:512], ALU.mult, ALU.mult,
                                                accum_out=SC16[:, t:t + 1])
                    inc(i, "vE")
                    i = ve.scalar_tensor_tensor(GTS2[:, 0:512], Bt[:, t * D + 512:t * D + D], 1.0,
                                                RTB[:, 512:D], ALU.mult, ALU.mult,
                                                accum_out=A16[:, t:t + 1])
                    inc(i, "vE")
                ck[("sdot", t)] = cnt["vE"]

            @blk.scalar
            def _(sa, tcq=tcq):
                wt(sa, "vE", ck[("nq0", tcq)])
                g1 = G16[:, tcq * 4:tcq * 4 + 4]
                i = sa.activation(g1, NRM[:, tcq * 4:tcq * 4 + 4], AF.Ln)
                inc(i, "sE")
                sa.wait_ge(sems["sE"], cnt["sE"])
                i = sa.activation(g1, g1, AF.Exp, bias=0.0, scale=-0.5)
                inc(i, "sE")
            ck[("nq1", tcq)] = cnt["sE"]

            @blk.vector
            def _(ve, tcq=tcq):
                wt(ve, "sE", ck[("nq1", tcq)])
                i = ve.tensor_scalar(NRM[:, tcq * 4:tcq * 4 + 4],
                                     G16[:, tcq * 4:tcq * 4 + 4], SQRT_D, None, ALU.mult)
                inc(i, "vE")
            ck[("nq2", tcq)] = cnt["vE"]

            @blk.tensor
            def _(te, tcq=tcq):
                wt(te, "vE", ck[("nq2", tcq)])
                k = pool_acquire(te)
                ck["_nqk"] = k
                i = te.transpose(pbank[k][0:4, 0:128], NRM[:, tcq * 4:tcq * 4 + 4], IDT[:])
                inc(i, "tE")
            ck[("nq3", tcq)] = cnt["tE"]
            k = ck["_nqk"]

            @blk.vector
            def _(ve, tcq=tcq, k=k):
                wt(ve, "tE", ck[("nq3", tcq)])
                i = ve.tensor_copy(FL2N[0:4, tcq * 128:tcq * 128 + 128], pbank[k][0:4, 0:128])
                inc(i, "vE")
            pool_release(k, "vE", cnt["vE"])
            ck[("nq4", tcq)] = cnt["vE"]

            @blk.sync
            def _(se, tcq=tcq):
                wt(se, "vE", ck[("nq4", tcq)])
                wt(se, "tE", ck["gb2_t"])
                for j4 in range(4):
                    i = se.dma_start(out=FLN[0:1, tcq * 512 + j4 * 128:tcq * 512 + j4 * 128 + 128],
                                     in_=FL2N[j4:j4 + 1, tcq * 128:tcq * 128 + 128])
                    inc(i, "dN", 16)
            ck[("nq5", tcq)] = cnt["dN"]

        ffn_pass(0, W1bf, B1C, W2bf, B2ROW, "relu", False, FT, "a")
        p4_chunk(0)
        for t in range(8, 12):
            emit_xt_ring(t, [("vE", ck[("ln_v", t)]), ("tE", ck[("hmm", "a", 0, FT - 1)])], XTF)
        ck[("ring_ok", 2)] = cnt["vE"]
        ffn_pass(1, W1bf, B1C, W2bf, B2ROW, "relu", False, FT, "a")
        p4_chunk(1)
        for t in range(12, 16):
            emit_xt_ring(t, [("vE", ck[("ln_v", t)]), ("tE", ck[("hmm", "a", 1, FT - 1)])], XTF)
        ck[("ring_ok", 3)] = cnt["vE"]
        ffn_pass(2, W1bf, B1C, W2bf, B2ROW, "relu", False, FT, "a")
        p4_chunk(2)
        for t in range(0, 4):
            emit_xt_ring(t, [("sE", ck[("ln_s", t)]), ("tE", ck[("hmm", "a", 2, FT - 1)])], XTF)
            ck[("yrg", t)] = cnt["vE"]
        ck[("ring_ok", 0)] = ck[("yrg", 3)]
        emit_nrmb(0)
        ffn_pass(0, w1l_f, B1LC, None, B2LROW, "gelu", False, LFT, "l", hbase=0)
        ffn_pass(3, W1bf, B1C, W2bf, B2ROW, "relu", False, FT, "a")
        p4_chunk(3)
        for t in range(4, 8):
            emit_xt_ring(t, [("sE", ck[("ln_s", t)]), ("tE", ck[("hmm", "a", 3, FT - 1)])], XTF)
            ck[("yrg", t)] = cnt["vE"]
        ck[("ring_ok", 1)] = ck[("yrg", 7)]
        emit_nrmb(1)
        ffn_pass(1, w1l_f, B1LC, None, B2LROW, "gelu", False, LFT, "l", hbase=4)
        ck["affn_v"] = cnt["vE"]
        if taps_on:
            @blk.gpsimd
            def _(ge, r=r):
                wt(ge, "vE", ck["affn_v"])
                i = ge.dma_start(out=taps["t_ff"][r].rearrange("(t p) c -> p t c", p=128),
                                 in_=Bt[:].rearrange("p (t c) -> p t c", c=D))
                inc(i, "dT", 16)
            ck["hold"] = cnt["dT"]

        @blk.vector
        def _(ve):
            ve.wait_ge(sems["vE"], ck[("sdot", TT - 1)])
            i = ve.tensor_tensor(SC16[:], SC16[:], A16[:], ALU.add)
            inc(i, "vE")
        ck["s_p"] = cnt["vE"]
        if taps_on:
            @blk.gpsimd
            def _(ge, r=r):
                wt(ge, "sE", ck[("ln_s", TT - 1)])
                wt(ge, "vE", ck[("resid", TT - 1)])
                i = ge.dma_start(out=taps["t_y"][r].rearrange("(t p) c -> p t c", p=128),
                                 in_=Bt[:].rearrange("p (t c) -> p t c", c=D))
                inc(i, "dT", 16)
            ck["hold"] = cnt["dT"]
            ck["bt_hold_dT"] = cnt["dT"]


        ROUT = []

        def _first_step():
            @blk.vector
            def _(ve):
                wt(ve, "vE", ck["s_p"])
                i = ve.tensor_scalar(RL16[:], SC16[:], 0.0, None, ALU.max)
                inc(i, "vE")
                ve.wait_ge(sems["vE"], cnt["vE"])
                i = ve.memset(ACOL[:], 0.0)
                inc(i, "vE")
                i = ve.memset(MXC[:], 0.0)
                inc(i, "vE")
            ck["dsc_v"] = cnt["vE"]
            ck["dsc_rl"] = 0
            ck["dsc_acol_v"] = cnt["vE"]
        ROUT.append(_first_step)

        def _iter_step(it):
            @blk.vector
            def _(ve, it=it):
                wt(ve, "vE", ck["dsc_v"] if it == 0 else ck["dsc_acol_v"])
                if it > 0:
                    wt(ve, "sE", ck["dsc_rl"])
                i = ve.scalar_tensor_tensor(A16[:], SC16[:], 1.0, RL16[:], ALU.mult, ALU.subtract)
                inc(i, "vE")
                ck["_d1"] = cnt["vE"]

            @blk.scalar
            def _(sa, it=it):
                wt(sa, "vE", ck["_d1"])
                i = sa.activation(GTS[:, 0:TT], A16[:], AF.Exp, bias=MXC[:, 0:1], scale=10.0,
                                  accum_out=STAT[:, 33:34])
                inc(i, "sE")
                ck["_d8"] = cnt["sE"]

            @blk.tensor
            def _(te):
                wt(te, "sE", ck["_d8"])
                k3 = pool_acquire(te)
                ck["_dk3"] = k3
                i = te.matmul(pbank[k3][0:1, 0:1], STAT[:, 33:34], ONE_C[:, 0:1],
                              start=True, stop=True)
                inc(i, "tE")
                ck["_d9"] = cnt["tE"]
            k3 = ck["_dk3"]

            @blk.scalar
            def _(sa, k3=k3):
                wt(sa, "tE", ck["_d9"])
                i = sa.activation(SEQS[0:1, 1:2], pbank[k3][0:1, 0:1], AF.Ln)
                inc(i, "sE")
                ck["_d10"] = cnt["sE"]
            pool_release(k3, "sE", ck["_d10"])

            @blk.vector
            def _(ve, it=it):
                wt(ve, "sE", ck["_d10"])
                i = ve.tensor_scalar(SEQS[0:1, 2:3], SEQS[0:1, 1:2], -0.1, RCONST,
                                     ALU.mult, ALU.add)
                inc(i, "vE")
                ve.wait_ge(sems["vE"], cnt["vE"])
                if it > 0:
                    i = ve.tensor_tensor(SEQS[0:1, 3:4], SEQS[0:1, 2:3], SEQS[0:1, 4:5], ALU.add)
                else:
                    i = ve.tensor_copy(SEQS[0:1, 3:4], SEQS[0:1, 2:3])
                inc(i, "vE")
                ve.wait_ge(sems["vE"], cnt["vE"])
                i = ve.tensor_copy(SEQS[0:1, 4:5], SEQS[0:1, 3:4])
                inc(i, "vE")
                ck["_d11"] = cnt["vE"]

            @blk.tensor
            def _(te):
                wt(te, "vE", ck["_d11"])
                k4 = pool_acquire(te)
                ck["_dk4"] = k4
                i = te.matmul(pbank[k4][:, 0:1], ONE_R[:], SEQS[0:1, 3:4], start=True, stop=True)
                inc(i, "tE")
                ck["_d13"] = cnt["tE"]
            k4 = ck["_dk4"]

            @blk.vector
            def _(ve, k4=k4):
                wt(ve, "tE", ck["_d13"])
                i = ve.tensor_copy(ACOL[:], pbank[k4][:, 0:1])
                inc(i, "vE")
                ve.wait_ge(sems["vE"], cnt["vE"])
                i = ve.tensor_scalar(MXC[:], ACOL[:], 10.0, None, ALU.mult)
                inc(i, "vE")
                ck["dsc_acol_v"] = cnt["vE"]
            pool_release(k4, "vE", ck["dsc_acol_v"])

            @blk.scalar
            def _(sa):
                wt(sa, "vE", ck["dsc_acol_v"])
                i = sa.activation(RL16[:], SC16[:], AF.Relu, bias=ACOL[:, 0:1], scale=1.0)
                inc(i, "sE")
                ck["dsc_rl"] = cnt["sE"]
        for it in range(N_ITERS):
            ROUT.append(lambda it=it: _iter_step(it))

        def _scores_step():
            @blk.vector
            def _(ve):
                wt(ve, "sE", ck["dsc_rl"])
                i = ve.scalar_tensor_tensor(A16[:], SC16[:], ACOL[:, 0:1], RL16[:],
                                            ALU.add, ALU.subtract)
                inc(i, "vE")
            ck["r1"] = cnt["vE"]

            @blk.scalar
            def _(sa):
                wt(sa, "vE", ck["r1"])
                i = sa.activation(R16[:], A16[:], AF.Exp, bias=0.0, scale=10.0)
                inc(i, "sE")
            ck["r2"] = cnt["sE"]

            @blk.vector
            def _(ve):
                wt(ve, "sE", ck["r2"])
                i = ve.tensor_scalar(R16[:], R16[:], FTZ_DN, FTZ_UP, ALU.mult, ALU.mult)
                inc(i, "vE")
            ck["r3"] = cnt["vE"]

            @blk.tensor
            def _(te):
                wt(te, "vE", ck["r3"])
                k = pool_acquire(te)
                ck["_rk"] = k
                i = te.transpose(pbank[k][0:16, 0:128], R16[:], IDT[:])
                inc(i, "tE")
            ck["r4"] = cnt["tE"]
            k = ck["_rk"]

            @blk.vector
            def _(ve, k=k):
                wt(ve, "tE", ck["r4"])
                i = ve.tensor_copy(FL2[:], pbank[k][0:16, 0:128])
                inc(i, "vE")
            pool_release(k, "vE", cnt["vE"])
            ck["r5"] = cnt["vE"]

            @blk.sync
            def _(se, r=r):
                wt(se, "vE", ck["r5"])
                wt(se, "tE", ck[("pv_all", r)])
                for i16 in range(16):
                    i = se.dma_start(out=FLS[0:1, i16 * 128:(i16 + 1) * 128],
                                     in_=FL2[i16:i16 + 1, :])
                    inc(i, "dF", 16)
            ck["r6"] = cnt["dF"]
        ROUT.append(_scores_step)

        def _rank_prep(c4):
            @blk.tensor
            def _(te, c4=c4):
                wt(te, "dF", ck["r6"])
                wt(te, "vE", ck.get(("rbu", c4 - 1), 0))
                k = pool_acquire(te)
                ck["_rbk"] = k
                i = te.matmul(pbank[k][:, 0:512], ONE_R[:], FLS[0:1, c4 * 512:(c4 + 1) * 512],
                              start=True, stop=True)
                inc(i, "tE")
            ck[("rb", c4)] = cnt["tE"]
            k = ck["_rbk"]

            @blk.vector
            def _(ve, c4=c4, k=k):
                wt(ve, "tE", ck[("rb", c4)])
                xr = ck.get(("x0rr", TT - 1), None)
                if xr is not None:
                    wt(ve, xr[0], xr[1])
                wt(ve, "vE", ck[("sdot", TT - 1)])
                i = ve.tensor_copy(RBC[:], pbank[k][:, 0:512])
                inc(i, "vE")
            pool_release(k, "vE", cnt["vE"])
            ck[("rbc", c4)] = cnt["vE"]

            @blk.scalar
            def _(sa, c4=c4):
                iu = ck.get(("iotu", c4 - 1), None)
                if iu is not None:
                    wt(sa, iu[0], iu[1])
                wt(sa, "vE", ck[("sdot", TT - 1)])
                i = sa.dma_start(out=IOTC[:], in_=iota_d[:, c4 * 512:(c4 + 1) * 512])
                inc(i, "dI", 16)
            ck[("iot", c4)] = cnt["dI"]

        def _rank_step(c4, it):
            @blk.vector
            def _(ve, c4=c4, it=it):
                wt(ve, "dI", ck[("iot", c4)])
                wt(ve, "vE", ck[("rbc", c4)])
                wt(ve, "vE", ck.get(("eqv", c4, it - 1),
                                    ck.get(("eqv", c4 - 1, TT - 1), 0)))
                i = ve.tensor_scalar(LTM[:], IOTC[:], JIDX[:, it:it + 1], None, ALU.is_lt)
                inc(i, "vE")
                i = ve.tensor_scalar(GTS[:], RBC[:], R16[:, it:it + 1], None, ALU.is_gt,
                                     ALU.add,
                                     accum_out=GPART[:, it * 4 + c4:it * 4 + c4 + 1])
                inc(i, "vE")
                ve.wait_ge(sems["vE"], cnt["vE"])
                i = ve.scalar_tensor_tensor(GTS2[:], RBC[:], R16[:, it:it + 1], LTM[:],
                                            ALU.is_equal, ALU.mult,
                                            accum_out=EPART[:, it * 4 + c4:it * 4 + c4 + 1])
                inc(i, "vE")
            ck[("eqv", c4, it)] = cnt["vE"]
            if it == TT - 1:
                ck[("rbu", c4)] = ck[("eqv", c4, TT - 1)]
                ck[("iotu", c4)] = ("vE", ck[("eqv", c4, TT - 1)])

        for c4 in range(4):
            ROUT.append(lambda c4=c4: _rank_prep(c4))
            for it2 in range(0, TT, 2):
                ROUT.append(lambda c4=c4, it2=it2: (_rank_step(c4, it2), _rank_step(c4, it2 + 1)))

        def _mask_step():
            @blk.vector
            def _(ve):
                wt(ve, "vE", ck[("eqv", 3, TT - 1)])
                i = ve.reduce_sum(G16[:], GPART[:].rearrange("p (a b) -> p a b", b=4), axis=AX.X)
                inc(i, "vE")
                i = ve.reduce_sum(RNK[:], EPART[:].rearrange("p (a b) -> p a b", b=4), axis=AX.X)
                inc(i, "vE")
            ck["m1"] = cnt["vE"]

            @blk.vector
            def _(ve):
                ve.wait_ge(sems["vE"], ck["m1"])
                i = ve.tensor_tensor(RNK[:], RNK[:], G16[:], ALU.add)
                inc(i, "vE")
                ve.wait_ge(sems["vE"], cnt["vE"])
                i = ve.tensor_scalar(M16[:], RNK[:], 1024.0, None, ALU.is_lt)
                inc(i, "vE")
            ck["mask_v"] = cnt["vE"]
        ROUT.append(_mask_step)
        ck["_ROUT"] = ROUT

        ck["mask_v"] = cnt["vE"]
        if taps_on:
            pump_all_ref[0]()
            @blk.gpsimd
            def _(ge, r=r):
                wt(ge, "vE", ck["mask_v"])
                for tn, tt2 in (("t_m", M16), ("t_r", R16), ("t_nrm", NRM), ("t_s", SC16)):
                    i = ge.dma_start(out=taps[tn][r], in_=tt2[:])
                    inc(i, "dT", 16)
            ck["hold"] = cnt["dT"]

        ck[("ring_ok", 0)] = ck[("yrg", 3)]
        ck[("ring_ok", 1)] = ck[("yrg", 7)]
        ffn_pass(0, w1h_f, B1HC, w2hbf, B2HROW, "gelu", True, FT, "h")
        ck[("nrmbu", 0)] = cnt["vE"]
        for t in range(8, 12):
            emit_xt_ring(t, [("sE", ck[("ln_s", t)]),
                             ("tE", ck[("hmm", "h", 0, FT - 1)])], XTF)
        ck[("ring_ok", 2)] = cnt["vE"]
        emit_nrmb(2)
        ffn_pass(2, w1l_f, B1LC, None, B2LROW, "gelu", False, LFT, "l", hbase=8)
        ffn_pass(1, w1h_f, B1HC, w2hbf, B2HROW, "gelu", True, FT, "h")
        ck[("nrmbu", 1)] = cnt["vE"]
        for t in range(12, 16):
            emit_xt_ring(t, [("sE", ck[("ln_s", t)]),
                             ("tE", ck[("hmm", "h", 1, FT - 1)])], XTF)
        ck[("ring_ok", 3)] = cnt["vE"]
        emit_nrmb(3)
        ffn_pass(3, w1l_f, B1LC, None, B2LROW, "gelu", False, LFT, "l", hbase=12)
        ffn_pass(2, w1h_f, B1HC, w2hbf, B2HROW, "gelu", True, FT, "h")
        ck[("nrmbu", 2)] = cnt["vE"]
        ffn_pass(3, w1h_f, B1HC, w2hbf, B2HROW, "gelu", True, FT, "h")
        ck[("nrmbu", 3)] = cnt["vE"]
        pump_all()
        ck[("hv_done", r)] = cnt["tE"]
        ck["z_v"] = cnt["vE"]
        if taps_on:
            @blk.gpsimd
            def _(ge, r=r):
                wt(ge, "vE", ck["z_v"])
                i = ge.dma_start(out=taps["t_z"][r].rearrange("(t p) c -> p t c", p=128),
                                 in_=Bt[:].rearrange("p (t c) -> p t c", c=D))
                inc(i, "dT", 16)
            ck["hold"] = cnt["dT"]

        for _p in range(2):
            ck[("lng_gate_v", _p)] = 0
            ck[("lng_gate_s", _p)] = 0
        ck[("osl_free", 0)] = 0
        ck[("osl_free", 1)] = 0
        for tcq in range(4):
            ts4 = list(range(tcq * 4, tcq * 4 + 4))
            gpar = tcq % 2
            gts = [("vE", ck["z_v"])]
            if taps_on:
                gts.append(("dT", ck.get("hold", 0)))
            emit_ln_group(ts4, "plain_out", gpar, gates=gts, applies=False)
            for i_, t in enumerate(ts4):
                par = t % 2
                sto = ck.get(("st", t - 2), None)
                emit_ln_apply(t, gpar, i_, "plain_out",
                              extra_gates=([sto] if sto is not None else []))

                @blk.scalar
                def _(sa, t=t, r=r, par=par):
                    wt(sa, "sE", ck[("ln_s", t)])
                    semo = "dOa" if par == 0 else "dOb"
                    i = sa.dma_start(out=out_d[r][t * 128:(t + 1) * 128, :],
                                     in_=OUTST[:, par * D:(par + 1) * D])
                    inc(i, semo, 16)
                ck[("st", t)] = ("dOa" if par == 0 else "dOb",
                                cnt["dOa" if par == 0 else "dOb"])
        ck[("row_st_a", r)] = cnt["dOa"]
        ck[("row_st_b", r)] = cnt["dOb"]
        ck[("row_lastln", r)] = ck[("ln_s", TT - 1)]

    @blk.sync
    def _(se):
        for s_ in ("dOa", "dOb", "dX", "dM", "dG", "dW1a", "dW1b", "dW2a", "dW2b",
                   "dRa", "dRb", "dF", "dN", "dI", "dInit", "dInitP", "dT", "dV"):
            wt(se, s_)
    es.close()
    return nc


_N_CORES = 8
_ROWS_PER_CORE = 4
_FEED = {"ln1_g": "L0_attn_ln1_g", "ln1_b": "L0_attn_ln1_b", "ln2_g": "L0_attn_ln2_g",
         "ln2_b": "L0_attn_ln2_b", "W1": "L0_attn_W1", "b1": "L0_attn_b1",
         "W2": "L0_attn_W2", "b2": "L0_attn_b2", "rt": "L0_ffn_routing_token",
         "gam_l": "L0_ffn_gamma_light", "w1l": "L0_ffn_w1_light", "b1l": "L0_ffn_b1_light",
         "w2l": "L0_ffn_w2_light", "b2l": "L0_ffn_b2_light",
         "gam_h": "L0_ffn_gamma_heavy", "w1h": "L0_ffn_w1_heavy", "b1h": "L0_ffn_b1_heavy",
         "w2h": "L0_ffn_w2_heavy", "b2h": "L0_ffn_b2_heavy"}


def kernel(**inputs):
    from concourse.bass_utils import run_bass_kernel_spmd
    x = np.ascontiguousarray(np.asarray(inputs["L0_attn_src"], dtype=np.float32))
    w = {k: np.ascontiguousarray(np.asarray(inputs[v], dtype=np.float32))
         for k, v in _FEED.items()}
    nc = build(_ROWS_PER_CORE, taps_on=False)
    in_maps = []
    for c in range(_N_CORES):
        m = {"x": x[c * _ROWS_PER_CORE:(c + 1) * _ROWS_PER_CORE]}
        m.update(w)
        in_maps.append(m)
    res = run_bass_kernel_spmd(nc, in_maps, core_ids=list(range(_N_CORES)))
    out = np.concatenate([np.asarray(r["out"]) for r in res.results], axis=0)
    return out.astype(np.float32, copy=False)


# version 09
# speedup vs baseline: 1.2192x; 1.2192x over previous
import numpy as np
from contextlib import ExitStack
import concourse.bass as bass
import concourse.mybir as mybir
import ml_dtypes

F32 = mybir.dt.float32
F32R = mybir.dt.float32r
BF16 = mybir.dt.bfloat16
AF = mybir.ActivationFunctionType
ALU = mybir.AluOpType
AX = mybir.AxisListType

S, D, H, DH, WIN = 2048, 1024, 16, 64, 256
NCH = S // WIN
DFF, LH = 4096, 512
TT, DT, FT, LFT = 16, 8, 32, 4
TOKC, NTC = 512, 4
EPS_LN = 1e-5
SHIFT = 80.0
NEG = -1.0e9
N_ITERS = 50
RCONST = float(np.float32(0.1) * np.log(np.float32(8.0), dtype=np.float32))
FTZ_DN = float(2.0 ** -24)
FTZ_UP = float(2.0 ** 24)
SQRT_D = float(np.sqrt(np.float32(1024.0)))
SBASE = 16512
VW = 65


def _np_masks():
    j = np.arange(128)[:, None]
    q = np.arange(256)[None, :]
    out = {}
    for s in (0, 1, 4, 5):
        jj = j + 128 * s
        valid = (q <= jj) & (jj <= q + 512)
        out[s] = np.where(valid, np.float32(0.0), np.float32(NEG)).astype(np.float32)
    return out


def subtiles_of_chunk(c):
    if c == 0:
        return [2, 3, 4, 5]
    if c == NCH - 1:
        return [0, 1, 2, 3]
    return [0, 1, 2, 3, 4, 5]


def masked_pairs(c):
    subs = subtiles_of_chunk(c)
    out = []
    if 0 in subs:
        out.append("01")
    if 4 in subs:
        out.append("45")
    return out


def pv_pairs(c):
    subs = subtiles_of_chunk(c)
    out = []
    for s_ in subs:
        for qh in (0, 1):
            if 128 * s_ + 127 >= 128 * qh and 128 * s_ <= 128 * qh + 127 + 512:
                out.append((s_, qh))
    return out


def build(n_rows, taps_on=False):
    nc = bass.Bass("TRN2", target_bir_lowering=False)
    es = ExitStack()

    semn = ("dX", "dM", "dG", "dW1a", "dW1b", "dW2a", "dW2b", "dRa", "dRb",
            "dOa", "dOb", "dF", "dN", "dI", "dInit", "dInitP", "dT", "dV",
            "tE", "vE", "sE", "pE")
    sems = {s_: es.enter_context(nc.semaphore(s_)) for s_ in semn}
    cnt = {s_: 0 for s_ in semn}

    def inc(instr, name, n=1):
        instr.then_inc(sems[name], n)
        cnt[name] += n
        return cnt[name]

    def wt(eng, name, val=None):
        v = cnt[name] if val is None else val
        if v > 0:
            eng.wait_ge(sems[name], v)

    x_in = nc.dram_tensor("x", [n_rows, S, D], F32, kind="ExternalInput").ap()
    out_d = nc.dram_tensor("out", [n_rows, S, D], F32, kind="ExternalOutput").ap()

    def din(name, shape):
        return nc.dram_tensor(name, shape, F32, kind="ExternalInput").ap()
    ln1_g, ln1_b = din("ln1_g", [D]), din("ln1_b", [D])
    ln2_g, ln2_b = din("ln2_g", [D]), din("ln2_b", [D])
    W1d, b1d = din("W1", [D, DFF]), din("b1", [DFF])
    W2d, b2d = din("W2", [DFF, D]), din("b2", [D])
    rtd = din("rt", [D])
    gam_l, w1ld, b1ld = din("gam_l", [D]), din("w1l", [D, LH]), din("b1l", [LH])
    w2ld, b2ld = din("w2l", [LH, D]), din("b2l", [D])
    gam_h, w1hd, b1hd = din("gam_h", [D]), din("w1h", [D, DFF]), din("b1h", [DFF])
    w2hd, b2hd = din("w2h", [DFF, D]), din("b2h", [D])

    W1bf = nc.dram_tensor("W1bf", [D, DFF], BF16, kind="Internal").ap()
    W2bf = nc.dram_tensor("W2bf", [DFF, D], BF16, kind="Internal").ap()
    w2hbf = nc.dram_tensor("w2hbf", [DFF, D], BF16, kind="Internal").ap()
    w2lbf = nc.dram_tensor("w2lbf", [LH, D], BF16, kind="Internal").ap()
    w1l_f = nc.dram_tensor("w1l_f", [D, LH], BF16, kind="Internal").ap()
    w1h_f = nc.dram_tensor("w1h_f", [D, DFF], BF16, kind="Internal").ap()

    taps = {}
    if taps_on:
        for tname in ("t_attn", "t_x1", "t_ff", "t_y", "t_z"):
            taps[tname] = nc.dram_tensor(tname, [n_rows, S, D], F32, kind="ExternalOutput").ap()
        for tname in ("t_s", "t_r", "t_m", "t_nrm"):
            taps[tname] = nc.dram_tensor(tname, [n_rows, 128, TT], F32, kind="ExternalOutput").ap()

    mk = _np_masks()
    ident_d = nc.inline_tensor(np.eye(128, dtype=np.float32), "ident_c").ap()
    mask01_d = nc.inline_tensor(np.concatenate([mk[0], mk[1]], 1), "mask01_c").ap()
    mask45_d = nc.inline_tensor(np.concatenate([mk[4], mk[5]], 1), "mask45_c").ap()
    ones_row_d = nc.inline_tensor(np.ones((1, 128), np.float32), "onesr_c").ap()
    ones_rowbf_d = nc.inline_tensor(np.ones((1, 128), ml_dtypes.bfloat16), "onesrb_c").ap()
    ones_col_d = nc.inline_tensor(np.ones((128, 1), np.float32), "onesc_c").ap()
    iota_d = nc.inline_tensor(
        np.broadcast_to(np.arange(S, dtype=np.float32)[None, :], (128, S)).copy(), "iota_c").ap()
    nshift_d = nc.inline_tensor(np.full((128, 1), -SHIFT, np.float32), "nshift_c").ap()
    jidx_d = nc.inline_tensor(
        (np.arange(TT, dtype=np.float32)[None, :] * 128
         + np.arange(128, dtype=np.float32)[:, None]).astype(np.float32), "jidx_c").ap()

    off = [SBASE]

    def sba(name, shape, dtype, at=None):
        per = int(np.prod(shape[1:])) * mybir.dt.size(dtype)
        if at is None:
            at = off[0]
            off[0] = at + ((per + 31) // 32) * 32
        h = nc.alloc_sbuf_tensor_at(name, shape, dtype, offset=at)
        return h.ap(), at

    Bt, _ = sba("Bt", [128, TT * D], F32)
    XT_at = off[0]
    XTA, _ = sba("XTA", [128, DT * 8 * 128], F32R)
    XTF, _ = sba("XTF", [128, DT * 8 * 128], BF16, at=XT_at)
    HT_at = off[0]
    HT, _ = sba("HT", [128, FT * TOKC], BF16)
    SCW, _ = sba("SCW", [128, 1664], F32, at=HT_at)
    CC_at = off[0]
    X0V, _ = sba("X0V", [128, 8 * H * VW], BF16, at=CC_at)
    FLS, _ = sba("FLS", [1, S], F32, at=XT_at + 16384)
    FLN, _ = sba("FLN", [1, S], F32, at=XT_at + 24576)
    PT, _ = sba("PT", [128, 2 * 6 * 256], BF16, at=CC_at + 16640)
    SQJ, _ = sba("SQJ", [128, D], F32, at=CC_at + 16640)
    NRMB0, _ = sba("NRMB0", [128, TOKC], F32, at=CC_at + 16640 + 4096)
    off[0] = CC_at + 16640 + 6144
    W1S, _ = sba("W1S", [128, 2 * DT * 128], BF16)
    W2S, _ = sba("W2S", [128, 2 * D], BF16)
    W2SL, _ = sba("W2SL", [128, (LH // 128) * D], BF16)
    M01, _ = sba("M01", [128, 512], F32, at=HT_at + 8192)
    M45, _ = sba("M45", [128, 512], F32, at=HT_at + 12288)
    IDT, _ = sba("IDT", [128, 128], F32)
    ONE_R, _ = sba("ONE_R", [1, 128], F32)
    ONE_RB, _ = sba("ONE_RB", [1, 128], BF16)
    ONE_C, _ = sba("ONE_C", [128, 1], F32)
    B2ROW, _ = sba("B2ROW", [1, D], BF16)
    B2LROW, _ = sba("B2LROW", [1, D], BF16)
    B2HROW, _ = sba("B2HROW", [1, D], BF16)
    B1C, _ = sba("B1C", [128, FT], F32)
    B1LC, _ = sba("B1LC", [128, LFT], F32)
    B1HC, _ = sba("B1HC", [128, FT], F32)
    RTB, _ = sba("RTB", [128, D], F32)
    GAML, _ = sba("GAML", [128, DT], F32)
    GAMH, _ = sba("GAMH", [128, DT], F32)
    GB0, _ = sba("GB0", [128, D], F32)
    GB1, _ = sba("GB1", [128, D], F32)
    X0S_at = off[0]
    X0S, _ = sba("X0S", [128, 2 * D], F32)
    RBC, _ = sba("RBC", [128, 512], F32, at=X0S_at)
    IOTC, _ = sba("IOTC", [128, 512], F32, at=X0S_at + 2048)
    GTS, _ = sba("GTS", [128, 512], F32, at=X0S_at + 4096)
    GTS2, _ = sba("GTS2", [128, 512], F32, at=X0S_at + 6144)
    OUT_at = off[0]
    OUTST, _ = sba("OUTST", [128, 2 * D], F32)
    LTM, _ = sba("LTM", [128, 512], F32, at=OUT_at)
    FL2, _ = sba("FL2", [16, 128], F32)
    FL2N, _ = sba("FL2N", [4, 4 * 128], F32)
    NRMB1, _ = sba("NRMB1", [128, TOKC], F32)
    SC16, _ = sba("SC16", [128, TT], F32)
    A16, _ = sba("A16", [128, TT], F32)
    RL16, _ = sba("RL16", [128, TT], F32)
    R16, _ = sba("R16", [128, TT], F32)
    M16, _ = sba("M16", [128, TT], F32)
    NRM, _ = sba("NRM", [128, TT], F32)
    G16, _ = sba("G16", [128, TT], F32)
    RNK, _ = sba("RNK", [128, TT], F32)
    GPART, _ = sba("GPART", [128, TT * 4], F32)
    EPART, _ = sba("EPART", [128, TT * 4], F32)
    JIDX, _ = sba("JIDX", [128, TT], F32)
    STAT, _ = sba("STAT", [128, 40], F32)
    ACOL, _ = sba("ACOL", [128, 1], F32)
    MXC, _ = sba("MXC", [128, 1], F32)
    RSC, _ = sba("RSC", [128, 4], F32)
    SEQS, _ = sba("SEQS", [1, 16], F32)
    NSH, _ = sba("NSH", [128, 1], F32)
    BIASC, _ = sba("BIASC", [128, 1], F32)
    RCONC, _ = sba("RCONC", [128, 1], F32)
    assert off[0] <= 229344, f"sbuf layout overflow: {off[0]}"

    pbank = [es.enter_context(nc.psum_tensor(f"pb{i}", [128, 512], F32)) for i in range(2)]
    psc = [es.enter_context(nc.psum_tensor(f"psc{i}", [128, 1536], F32)) for i in range(2)]
    pool_free = {0: None, 1: None}
    pool_next = [0]

    def pool_acquire(eng):
        k = pool_next[0]
        pool_next[0] = 1 - k
        fa = pool_free[k]
        if fa is not None:
            wt(eng, fa[0], fa[1])
            pool_free[k] = None
        return k

    def pool_release(k, semname, c_):
        pool_free[k] = (semname, c_)

    blk = es.enter_context(nc.Block())
    ck = {}
    pump_all_ref = [None]

    @blk.sync
    def _(se):
        for dst, src in ((IDT[:], ident_d),
                         (ONE_R[:], ones_row_d), (ONE_RB[:], ones_rowbf_d),
                         (ONE_C[:], ones_col_d), (JIDX[:], jidx_d), (NSH[:], nshift_d)):
            inc(se.dma_start(out=dst, in_=src), "dInit", 16)
        inc(se.dma_start(out=SCW[0:32, 0:128], in_=b1d.rearrange("(a b) -> a b", b=128)), "dInit", 16)
        inc(se.dma_start(out=SCW[0:32, 128:256], in_=b1hd.rearrange("(a b) -> a b", b=128)), "dInit", 16)
        inc(se.dma_start(out=SCW[0:4, 256:384], in_=b1ld.rearrange("(a b) -> a b", b=128)), "dInit", 16)
        inc(se.dma_start(out=SCW[0:8, 384:512], in_=gam_l.rearrange("(a b) -> a b", b=128)), "dInit", 16)
        inc(se.dma_start(out=SCW[0:8, 512:640], in_=gam_h.rearrange("(a b) -> a b", b=128)), "dInit", 16)
        inc(se.dma_start(out=SCW[0:1, 640:640 + D], in_=rtd[None, :]), "dInit", 16)

    @blk.gpsimd
    def _(ge):
        inc(ge.dma_start(out=W1bf[:], in_=W1d[:]), "dInitP", 16)
        inc(ge.dma_start(out=W2bf[:], in_=W2d[:]), "dInitP", 16)
        inc(ge.dma_start(out=w2hbf[:], in_=w2hd[:]), "dInitP", 16)
        inc(ge.dma_start(out=w2lbf[:], in_=w2ld[:]), "dInitP", 16)
        inc(ge.dma_start(out=B2ROW[:], in_=b2d[None, :]), "dInitP", 16)
        inc(ge.dma_start(out=B2LROW[:], in_=b2ld[None, :]), "dInitP", 16)
        inc(ge.dma_start(out=B2HROW[:], in_=b2hd[None, :]), "dInitP", 16)
    ck["precast_dP"] = cnt["dInitP"]

    @blk.sync
    def _(se):
        wt(se, "dInitP", ck["precast_dP"])
        src = w2lbf.rearrange("(fs p) c -> p fs c", p=128)
        inc(se.dma_start(out=W2SL[:].rearrange("p (fs c) -> p fs c", c=D), in_=src), "dInit", 16)
    ck["w2sl_dS"] = cnt["dInit"]

    @blk.tensor
    def _(te):
        wt(te, "dInit")
        pt = pbank[0]
        te.transpose(pt[:, 0:32], SCW[0:32, 0:128], IDT[0:32, 0:32])
        te.transpose(pt[:, 32:64], SCW[0:32, 128:256], IDT[0:32, 0:32])
        te.transpose(pt[:, 64:68], SCW[0:4, 256:384], IDT[0:4, 0:4])
        te.transpose(pt[:, 68:76], SCW[0:8, 384:512], IDT[0:8, 0:8])
        i = te.transpose(pt[:, 76:84], SCW[0:8, 512:640], IDT[0:8, 0:8])
        inc(i, "tE")
    ck["setup_t1"] = cnt["tE"]

    @blk.vector
    def _(ve):
        wt(ve, "tE", ck["setup_t1"])
        pt = pbank[0]
        ve.tensor_copy(B1C[:], pt[:, 0:32])
        ve.tensor_copy(B1HC[:], pt[:, 32:64])
        ve.tensor_copy(B1LC[:], pt[:, 64:68])
        ve.tensor_copy(GAML[:], pt[:, 68:76])
        i = ve.tensor_copy(GAMH[:], pt[:, 76:84])
        inc(i, "vE")
        i = ve.memset(X0V[:].rearrange("p (sh e) -> p sh e", e=VW)[:, :, 64:65], 1.0)
        inc(i, "vE")
    ck["setup_v"] = cnt["vE"]
    ck["x0v_ones_setup"] = cnt["vE"]
    pool_free[0] = ("vE", ck["setup_v"])

    for half in range(2):
        @blk.tensor
        def _(te, half=half):
            k = pool_acquire(te)
            ck["_bk"] = k
            i = te.matmul(pbank[k][:, 0:512], ONE_R[:],
                          SCW[0:1, 640 + half * 512:640 + half * 512 + 512], start=True, stop=True)
            inc(i, "tE")
        k = ck["_bk"]

        @blk.vector
        def _(ve, half=half, k=k):
            wt(ve, "tE")
            i = ve.tensor_copy(RTB[:, half * 512:half * 512 + 512], pbank[k][:, 0:512])
            inc(i, "vE")
        pool_release(k, "vE", cnt["vE"])
    ck["rtb_v"] = cnt["vE"]

    fold_jobs = [(w1ld, w1l_f, GAML, dt_i, 0, LH) for dt_i in range(DT)]
    for dt_i in range(DT):
        for cc in range(4):
            fold_jobs.append((w1hd, w1h_f, GAMH, dt_i, cc * 1024, 1024))
    for (wsrc, wdst, gt, dt_i, c0, wdt) in fold_jobs:
        @blk.sync
        def _(se, wsrc=wsrc, dt_i=dt_i, c0=c0, wdt=wdt):
            wt(se, "dInitP")
            wt(se, "vE", ck["setup_v"])
            i = se.dma_start(out=SCW[:, 0:wdt], in_=wsrc[dt_i * 128:(dt_i + 1) * 128, c0:c0 + wdt])
            inc(i, "dInit", 16)

        @blk.vector
        def _(ve, gt=gt, dt_i=dt_i, wdt=wdt):
            wt(ve, "dInit")
            i = ve.tensor_scalar(SCW[:, 0:wdt], SCW[:, 0:wdt], gt[:, dt_i:dt_i + 1], None, ALU.mult)
            inc(i, "vE")

        @blk.gpsimd
        def _(ge, wdst=wdst, dt_i=dt_i, c0=c0, wdt=wdt):
            wt(ge, "vE")
            i = ge.dma_start(out=wdst[dt_i * 128:(dt_i + 1) * 128, c0:c0 + wdt], in_=SCW[:, 0:wdt])
            inc(i, "dInitP", 16)
    ck["fold_dP"] = cnt["dInitP"]

    def rc(ds, tile):
        return ds * (8 * 128) + (tile % 8) * 128

    def emit_xt_ring(t, gates, dst):
        for half in range(2):
            @blk.tensor
            def _(te, t=t, half=half, gates=gates):
                for g in gates:
                    wt(te, g[0], g[1])
                k = pool_acquire(te)
                last = None
                for j in range(4):
                    ds = half * 4 + j
                    last = te.transpose(pbank[k][:, j * 128:j * 128 + 128],
                                        Bt[:, t * D + ds * 128: t * D + ds * 128 + 128],
                                        IDT[:])
                inc(last, "tE")
                ck["_tpk"] = k
            k = ck["_tpk"]

            @blk.vector
            def _(ve, t=t, half=half, k=k, dst=dst):
                wt(ve, "tE")
                dv = dst[:].rearrange("p (a c) -> p a c", c=8 * 128)[
                    :, half * 4:half * 4 + 4, (t % 8) * 128:(t % 8) * 128 + 128]
                sv = pbank[k][:, 0:512].rearrange("p (a c) -> p a c", c=128)
                i = ve.tensor_copy(dv, sv)
                inc(i, "vE")
            pool_release(k, "vE", cnt["vE"])

    def emit_gb_bcast(col0, dst, gate_ds, v_gate=0):
        for half in range(2):
            @blk.tensor
            def _(te, half=half, col0=col0, gate_ds=gate_ds):
                wt(te, "dG", gate_ds)
                k = pool_acquire(te)
                i = te.matmul(pbank[k][:, 0:512], ONE_R[:],
                              FLN[0:1, col0 + half * 512:col0 + half * 512 + 512],
                              start=True, stop=True)
                inc(i, "tE")
                ck["_bk"] = k
            k = ck["_bk"]

            @blk.vector
            def _(ve, half=half, dst=dst, k=k, v_gate=v_gate):
                wt(ve, "tE")
                if v_gate:
                    wt(ve, "vE", v_gate)
                i = ve.tensor_copy(dst[:, half * 512:half * 512 + 512], pbank[k][:, 0:512])
                inc(i, "vE")
            pool_release(k, "vE", cnt["vE"])

    def emit_nrmb(tcq):
        @blk.tensor
        def _(te, tcq=tcq, dn_now=None):
            wt(te, "dN", cnt["dN"])
            wt(te, "vE", ck.get(("nrmbu", tcq - 2), ck.get(("nrmbu", tcq + 2), 0)))
            kk = pool_acquire(te)
            ck["_nbk"] = kk
            i = te.matmul(pbank[kk][:, 0:512], ONE_R[:],
                          FLN[0:1, tcq * 512:(tcq + 1) * 512], start=True, stop=True)
            inc(i, "tE")
        ck[("nrmbp", tcq)] = cnt["tE"]
        kk = ck["_nbk"]

        @blk.vector
        def _(ve, tcq=tcq, kk=kk):
            wt(ve, "tE", ck[("nrmbp", tcq)])
            i = ve.tensor_copy((NRMB0 if tcq % 2 == 0 else NRMB1)[:],
                               pbank[kk][:, 0:512])
            inc(i, "vE")
        pool_release(kk, "vE", cnt["vE"])
        ck[("nrmb_v", tcq)] = cnt["vE"]

    def emit_ln_apply(t, gpar, i_, kind, extra_gates=(), osl=None, osl_key="osl_free"):
        c0 = gpar * 16
        bsl = Bt[:, t * D:(t + 1) * D]
        if osl is None:
            osl = OUTST[:, (t % 2) * D:(t % 2) * D + D]
        sc2c = STAT[:, c0 + 12 + i_:c0 + 13 + i_]
        rsdc = STAT[:, c0 + 8 + i_:c0 + 9 + i_]
        if kind == "ln_aff":
            @blk.scalar
            def _(sa, t=t, sc2c=sc2c, rsdc=rsdc, bsl=bsl, osl=osl, gpar=gpar):
                wt(sa, "vE", ck[("lng_v8", gpar)])
                wt(sa, "vE", ck.get(("osl_free", t % 2), 0))
                for g in extra_gates:
                    wt(sa, g[0], g[1])
                i = sa.activation(osl, bsl, AF.Identity, bias=sc2c, scale=rsdc)
                inc(i, "sE")
                ck["_l9"] = cnt["sE"]

            @blk.vector
            def _(ve, t=t, osl=osl, bsl=bsl):
                wt(ve, "sE", ck["_l9"])
                i = ve.tensor_mul(osl, osl, GB0[:])
                inc(i, "vE")
                ve.wait_ge(sems["vE"], cnt["vE"])
                i = ve.tensor_add(bsl, osl, GB1[:])
                inc(i, "vE")
            ck[("ln_v", t)] = cnt["vE"]
            ck[("osl_free", t % 2)] = cnt["vE"]
            ck[("lng_gate_s", gpar)] = ck["_l9"]
        else:
            dst = bsl if kind == "plain" else osl

            @blk.scalar
            def _(sa, t=t, dst=dst, kind=kind, sc2c=sc2c, rsdc=rsdc, bsl=bsl, gpar=gpar,
                  osl_key=osl_key):
                wt(sa, "vE", ck[("lng_v8", gpar)])
                if kind == "plain_out":
                    wt(sa, "vE", ck.get((osl_key, t % 2), 0))
                for g in extra_gates:
                    wt(sa, g[0], g[1])
                i = sa.activation(dst, bsl, AF.Identity, bias=sc2c, scale=rsdc)
                inc(i, "sE")
            ck[("ln_s", t)] = cnt["sE"]
            ck[("lng_gate_s", gpar)] = cnt["sE"]

    def emit_ln_group(tlist, kind, gpar, gates=(), tile_gates=None, applies=True):
        tlist = list(tlist)
        n = len(tlist)
        c0 = gpar * 16
        ngm = STAT[:, c0:c0 + n]
        var = STAT[:, c0 + 4:c0 + 4 + n]
        rsd = STAT[:, c0 + 8:c0 + 8 + n]
        sc2 = STAT[:, c0 + 12:c0 + 12 + n]

        @blk.vector
        def _(ve, gates=gates, gpar=gpar, tlist=tlist):
            wt(ve, "vE", ck.get(("lng_gate_v", gpar), 0))
            wt(ve, "sE", ck.get(("lng_gate_s", gpar), 0))
            for g in gates:
                wt(ve, g[0], g[1])
            for i_, t in enumerate(tlist):
                if tile_gates is not None:
                    for g in tile_gates(t):
                        wt(ve, g[0], g[1])
                bsl = Bt[:, t * D:(t + 1) * D]
                i = ve.reduce_sum(STAT[:, c0 + i_:c0 + i_ + 1], bsl, axis=AX.X, negate=True)
                inc(i, "vE")
                ve.wait_ge(sems["vE"], cnt["vE"])
                i = ve.scalar_tensor_tensor(SQJ[:], bsl, 1.0, bsl, ALU.mult, ALU.mult,
                                            accum_out=STAT[:, c0 + 4 + i_:c0 + 5 + i_])
                inc(i, "vE")
                ve.wait_ge(sems["vE"], cnt["vE"])
            i = ve.tensor_scalar(ngm, ngm, 1.0 / D, None, ALU.mult)
            inc(i, "vE")
            ve.wait_ge(sems["vE"], cnt["vE"])
            i = ve.tensor_scalar(var, var, 1.0 / D, EPS_LN, ALU.mult, ALU.add)
            inc(i, "vE")
            ve.wait_ge(sems["vE"], cnt["vE"])
            i = ve.tensor_tensor(sc2, ngm, ngm, ALU.mult)
            inc(i, "vE")
            ve.wait_ge(sems["vE"], cnt["vE"])
            i = ve.tensor_tensor(var, var, sc2, ALU.subtract)
            inc(i, "vE")
            ck["_g4"] = cnt["vE"]

        @blk.scalar
        def _(sa, gpar=gpar):
            wt(sa, "vE", ck["_g4"])
            i = sa.activation(rsd, var, AF.Ln)
            inc(i, "sE")
            sa.wait_ge(sems["sE"], cnt["sE"])
            i = sa.activation(rsd, rsd, AF.Exp, bias=0.0, scale=-0.5)
            inc(i, "sE")
            ck[("lng_s7", gpar)] = cnt["sE"]

        @blk.vector
        def _(ve, gpar=gpar):
            wt(ve, "sE", ck[("lng_s7", gpar)])
            i = ve.tensor_tensor(sc2, ngm, rsd, ALU.mult)
            inc(i, "vE")
        ck[("lng_v8", gpar)] = cnt["vE"]
        ck[("lng_gate_v", gpar)] = cnt["vE"]

        if applies:
            for i_, t in enumerate(tlist):
                emit_ln_apply(t, gpar, i_, kind)

    def pump(n=1):
        q = ck.get("_ROUT")
        for _ in range(n):
            if q:
                q.pop(0)()

    def pump_all():
        q = ck.get("_ROUT")
        while q:
            q.pop(0)()
    pump_all_ref[0] = pump_all

    def ffn_pass(tc, w1src, b1col, w2src, b2row, act, masked, nft, tag, hbase=0):
        slot0 = (4 * tc) % 8
        vE0 = ck.get("prev_yev_v", 0)
        sE0 = ck.get("prev_exp_s", 0)
        pheng, phval = ck.get("prev_hev", ("sE", 0))
        heng = "vE" if act == "relu" else "sE"
        for ft in range(nft):
            @blk.sync
            def _(se, ft=ft, tc=tc, w1src=w1src, tag=tag):
                wt(se, "tE", ck.get(("hmm", tag, tc, ft - 2), ck.get("ffn_prev_t", 0)))
                wt(se, "dInitP", ck["fold_dP"])
                src = w1src[:, ft * 128:(ft + 1) * 128].rearrange("(ds p) c -> p ds c", p=128)
                sem1 = "dW1a" if ft % 2 == 0 else "dW1b"
                i = se.dma_start(out=W1S[:, (ft % 2) * 1024:(ft % 2) * 1024 + 1024]
                                 .rearrange("p (ds c) -> p ds c", c=128), in_=src)
                inc(i, sem1, 16)
            ck[("w1s", tag, tc, ft)] = ("dW1a" if ft % 2 == 0 else "dW1b",
                                        cnt["dW1a" if ft % 2 == 0 else "dW1b"])

            @blk.tensor
            def _(te, ft=ft, tc=tc, slot0=slot0, tag=tag, sE0=sE0, vE0=vE0, act=act, hbase=hbase,
                  pheng=pheng, phval=phval, heng=heng):
                wt(te, ck[("w1s", tag, tc, ft)][0], ck[("w1s", tag, tc, ft)][1])
                wt(te, "vE", ck.get(("ring_ok", tc), 0))
                if ft < 2:
                    wt(te, "vE", vE0)
                    wt(te, pheng, phval)
                wt(te, "sE", sE0)
                wt(te, heng, ck.get(("hev", tag, tc, ft - 2), 0))
                if act == "gelu":
                    wt(te, "vE", max(ck.get("prev_hpre_v", 0) if ft < 2 else 0,
                                     ck.get(("hpre", tag, tc, ft - 2), 0)))
                pt = psc[0][:, (ft % 2) * 512:(ft % 2) * 512 + 512]
                last = None
                for ds in range(DT):
                    last = te.matmul(
                        pt, W1S[:, (ft % 2) * 1024 + ds * 128:(ft % 2) * 1024 + ds * 128 + 128],
                        XTF[:, ds * 1024 + slot0 * 128: ds * 1024 + slot0 * 128 + 512],
                        start=(ds == 0), stop=(ds == DT - 1))
                inc(last, "tE")
            ck[("hmm", tag, tc, ft)] = cnt["tE"]

            if act == "relu":
                @blk.vector
                def _(ve, ft=ft, tc=tc, b1col=b1col, tag=tag, hbase=hbase):
                    wt(ve, "tE", ck[("hmm", tag, tc, ft)])
                    ve.wait_ge(sems["vE"], cnt["vE"])
                    i = ve.tensor_scalar(HT[:, (hbase + ft) * TOKC:(hbase + ft + 1) * TOKC],
                                         psc[0][:, (ft % 2) * 512:(ft % 2) * 512 + 512],
                                         b1col[:, ft:ft + 1], 0.0, ALU.add, ALU.max)
                    inc(i, "vE")
                ck[("hev", tag, tc, ft)] = cnt["vE"]
            else:
                @blk.vector
                def _(ve, ft=ft, tc=tc, tag=tag, sE0=sE0, pheng=pheng, phval=phval):
                    wt(ve, "tE", ck[("hmm", tag, tc, ft)])
                    wt(ve, "vE", ck.get(("nrmb_v", tc), 0))
                    if ft < 2:
                        wt(ve, pheng, phval)
                    wt(ve, "sE", max(sE0, ck.get(("hev", tag, tc, ft - 2), 0)))
                    i = ve.tensor_tensor(OUTST[:, (ft % 2) * 512 + 1024:(ft % 2) * 512 + 1536],
                                         psc[0][:, (ft % 2) * 512:(ft % 2) * 512 + 512],
                                         (NRMB0 if tc % 2 == 0 else NRMB1)[:], ALU.mult)
                    inc(i, "vE")
                ck[("hpre", tag, tc, ft)] = cnt["vE"]

                @blk.scalar
                def _(sa, ft=ft, tc=tc, b1col=b1col, tag=tag, hbase=hbase):
                    wt(sa, "vE", ck[("hpre", tag, tc, ft)])
                    i = sa.activation(HT[:, (hbase + ft) * TOKC:(hbase + ft + 1) * TOKC],
                                      OUTST[:, (ft % 2) * 512 + 1024:(ft % 2) * 512 + 1536],
                                      AF.Gelu, bias=b1col[:, ft:ft + 1], scale=1.0)
                    inc(i, "sE")
                ck[("hev", tag, tc, ft)] = cnt["sE"]
            if ft % 2 == 1:
                pump(1)

        if masked:
            pump_all()
        @blk.tensor
        def _(te, tc=tc, tag=tag, nft=nft, heng=heng):
            wt(te, heng, ck[("hev", tag, tc, nft - 1)])
            wt(te, heng, ck.get(("hev", tag, tc, nft - 2), 0))
            k6 = pool_acquire(te)
            k7 = pool_acquire(te)
            ck["_yk67"] = (k6, k7)
        k6, k7 = ck["_yk67"]
        ybank = {}
        for tt_ in range(4):
            for dh in range(2):
                idx = 2 * tt_ + dh
                if idx < 6:
                    ybank[(tt_, dh)] = psc[idx // 3][:, (idx % 3) * 512:(idx % 3) * 512 + 512]
                else:
                    ybank[(tt_, dh)] = pbank[(k6, k7)[idx - 6]][:]

        use_stream = w2src is not None
        for fs in range(nft):
            if use_stream:
                @blk.sync
                def _(se, fs=fs, tc=tc, tag=tag, w2src=w2src):
                    wt(se, "tE", ck.get(("ymm", tag, tc, fs - 2), 0))
                    wt(se, "dInitP", ck["precast_dP"])
                    src = w2src[fs * 128:(fs + 1) * 128, :]
                    sem2 = "dW2a" if fs % 2 == 0 else "dW2b"
                    i = se.dma_start(out=W2S[:, (fs % 2) * D:(fs % 2) * D + D], in_=src)
                    inc(i, sem2, 16)
                ck[("w2s", tag, tc, fs)] = ("dW2a" if fs % 2 == 0 else "dW2b",
                                            cnt["dW2a" if fs % 2 == 0 else "dW2b"])

            @blk.tensor
            def _(te, fs=fs, tc=tc, tag=tag, use_stream=use_stream, vE0=vE0, ybank=ybank, hbase=hbase):
                if use_stream:
                    wt(te, ck[("w2s", tag, tc, fs)][0], ck[("w2s", tag, tc, fs)][1])
                else:
                    wt(te, "dInit", ck["w2sl_dS"])
                if fs == 0:
                    wt(te, "vE", max(vE0, ck.get(("yev_all", tag, tc), 0)))
                last = None
                for tt_ in range(4):
                    for dh in range(2):
                        pt = ybank[(tt_, dh)]
                        rhs = (W2S[:, (fs % 2) * D + dh * 512:(fs % 2) * D + dh * 512 + 512]
                               if use_stream
                               else W2SL[:, fs * D + dh * 512:fs * D + dh * 512 + 512])
                        last = te.matmul(pt,
                                         HT[:, (hbase + fs) * TOKC + tt_ * 128:
                                            (hbase + fs) * TOKC + tt_ * 128 + 128],
                                         rhs, start=(fs == 0), stop=False)
                inc(last, "tE")
            ck[("ymm", tag, tc, fs)] = cnt["tE"]

        @blk.tensor
        def _(te, tc=tc, tag=tag, b2row=b2row, ybank=ybank):
            last = None
            for tt_ in range(4):
                for dh in range(2):
                    last = te.matmul(ybank[(tt_, dh)], ONE_RB[:],
                                     b2row[0:1, dh * 512:dh * 512 + 512],
                                     start=False, stop=True)
            inc(last, "tE")
        ck[("yb2", tag, tc)] = cnt["tE"]

        for tt_ in range(4):
            t_abs = tc * 4 + tt_
            for dh in range(2):
                @blk.vector
                def _(ve, tt_=tt_, dh=dh, tc=tc, t_abs=t_abs, masked=masked, tag=tag, ybank=ybank):
                    wt(ve, "tE", ck[("yb2", tag, tc)])
                    wt(ve, "dT", ck.get("hold", 0))
                    pt = ybank[(tt_, dh)]
                    bsl = Bt[:, t_abs * D + dh * 512: t_abs * D + dh * 512 + 512]
                    if masked:
                        wt(ve, "vE", ck.get("mask_v", 0))
                        i = ve.scalar_tensor_tensor(bsl, pt, M16[:, t_abs:t_abs + 1], bsl,
                                                    ALU.mult, ALU.add)
                    else:
                        i = ve.tensor_add(bsl, bsl, pt)
                    inc(i, "vE")
                ck[("yev", tag, tc, tt_, dh)] = cnt["vE"]
        ck[("yev_all", tag, tc)] = cnt["vE"]
        pool_release(k6, "vE", cnt["vE"])
        pool_release(k7, "vE", cnt["vE"])
        ck["ffn_prev_t"] = cnt["tE"]
        ck["prev_yev_v"] = cnt["vE"]
        ck["prev_hev"] = (heng, ck[("hev", tag, tc, nft - 1)])
        if act == "gelu":
            ck["prev_hpre_v"] = ck[("hpre", tag, tc, nft - 1)]

    for r in range(n_rows):
        @blk.sync
        def _(se, r=r):
            wt(se, "dOa", ck.get(("row_st_a", r - 1), 0))
            wt(se, "dOb", ck.get(("row_st_b", r - 1), 0))
            wt(se, "sE", ck.get(("row_lastln", r - 1), 0))
            src = x_in[r].rearrange("(t p) c -> p t c", p=128)
            i = se.dma_start(out=Bt[:].rearrange("p (t c) -> p t c", c=D), in_=src)
            inc(i, "dX", 16)
            wt(se, "tE", ck.get(("hv_done", r - 1), 0))
            inc(se.dma_start(out=M01[:], in_=mask01_d), "dM", 16)
            inc(se.dma_start(out=M45[:], in_=mask45_d), "dM", 16)
        ck[("x0", r)] = cnt["dX"]
        ck[("msk", r)] = cnt["dM"]

        def emit_x0v(t, extra_gate_t):
            sl = t % 8

            @blk.gpsimd
            def _(ge, t=t, r=r, sl=sl, extra_gate_t=extra_gate_t):
                wt(ge, "dX", ck[("x0", r)])
                if extra_gate_t:
                    wt(ge, "tE", extra_gate_t)
                src = Bt[:, t * D:(t + 1) * D].rearrange("p (h d) -> p h d", d=64)
                blkv = X0V[:, sl * H * VW:(sl + 1) * H * VW].rearrange("p (h e) -> p h e", e=VW)
                i = ge.dma_start(out=blkv[:, :, 0:64], in_=src)
                inc(i, "dV", 16)
            ck[("x0v", r, t)] = cnt["dV"]

        for t in range(4):
            emit_x0v(t, ck.get(("hv_done", r - 1), 0) if t == 0 else 0)
        row0_vE = cnt["vE"]
        row0_sE = cnt["sE"]
        for t in range(4):
            emit_xt_ring(t, [("dX", ck[("x0", r)]),
                             ("tE", ck.get(("hv_done", r - 1), 0))], XTA)
            ck[("rxt", r, t)] = cnt["vE"]

        def scores_emit(c, h):
            par = h % 2
            subs = subtiles_of_chunk(c)
            dt_i = h // 2
            r0 = (h % 2) * 64
            pk = ("exp", c, h - 2) if h >= 2 else ("exp", c - 1, H - 2 + h)

            @blk.tensor
            def _(te, c=c, h=h, par=par, subs=subs, dt_i=dt_i, r0=r0, pk=pk, r=r):
                wt(te, "sE", ck.get(pk, 0))
                if c == 0 and h < 2:
                    wt(te, "vE", row0_vE)
                    wt(te, "sE", row0_sE)
                wt(te, "vE", ck.get(("ring_rdy", c), 0))
                last = None
                for s_ in subs:
                    ktile = 2 * c - 2 + s_
                    col = rc(dt_i, ktile)
                    q0 = rc(dt_i, 2 * c)
                    last = te.matmul(psc[par][:, s_ * 256:s_ * 256 + 256],
                                     XTA[r0:r0 + 64, col:col + 128],
                                     XTA[r0:r0 + 64, q0:q0 + 256],
                                     start=True, stop=True, tile_position=(r0, 0))
                inc(last, "tE")
            ck[("scores", c, h)] = cnt["tE"]

        def mask_emit(c, h):
            par = h % 2
            mp = masked_pairs(c)

            @blk.vector
            def _(ve, c=c, h=h, par=par, mp=mp, r=r):
                wt(ve, "tE", ck[("scores", c, h)])
                wt(ve, "dM", ck[("msk", r)])
                last = None
                for pair in mp:
                    off = 0 if pair == "01" else 1024
                    msk = M01 if pair == "01" else M45
                    last = ve.tensor_tensor(psc[par][:, off:off + 512],
                                            psc[par][:, off:off + 512], msk[:], ALU.add)
                inc(last, "vE")
            ck[("mask", c, h)] = cnt["vE"]

        def exp_emit(c, h):
            par = h % 2
            subs = subtiles_of_chunk(c)
            pvk = ("pv", c, h - 2) if h >= 2 else ("pv", c - 1, H - 2 + h)

            @blk.scalar
            def _(sa, c=c, h=h, par=par, subs=subs, pvk=pvk):
                wt(sa, "vE", ck[("mask", c, h)])
                wt(sa, "tE", ck[("scores", c, h)])
                wt(sa, "tE", ck.get(pvk, 0))
                w0 = min(subs) * 256
                w1 = (max(subs) + 1) * 256
                i = sa.activation(PT[:, par * 1536 + w0: par * 1536 + w1],
                                  psc[par][:, w0:w1], AF.Exp, bias=NSH[:, 0:1], scale=1.0)
                inc(i, "sE")
            ck[("exp", c, h)] = cnt["sE"]

        def pv_emit(c, h):
            par = h % 2
            pairs = pv_pairs(c)

            @blk.tensor
            def _(te, c=c, h=h, par=par, pairs=pairs, r=r):
                wt(te, "sE", ck[("exp", c, h)])
                tk = min(2 * c + 3 if h < 4 else (2 * c + 4 if h < 8 else 2 * c + 5), TT - 1)
                wt(te, "dV", ck[("x0v", r, tk)])
                wt(te, "vE", ck["x0v_ones_setup"])
                k = pool_acquire(te)
                np_ = {q_: sum(1 for s2, q2 in pairs if q2 == q_) for q_ in (0, 1)}
                done = {0: 0, 1: 0}
                lastq = {}
                for qh in (0, 1):
                    for s_, q_ in pairs:
                        if q_ != qh:
                            continue
                        ktile = 2 * c - 2 + s_
                        sl = ktile % 8
                        lhs = PT[:, par * 1536 + s_ * 256 + qh * 128:
                                 par * 1536 + s_ * 256 + qh * 128 + 128]
                        rhs = X0V[:, sl * H * VW + h * VW: sl * H * VW + h * VW + VW]
                        done[qh] += 1
                        lastq[qh] = te.matmul(pbank[k][:, qh * 128:qh * 128 + VW], lhs, rhs,
                                              start=(done[qh] == 1), stop=(done[qh] == np_[qh]))
                for qh in (0, 1):
                    inc(lastq[qh], "tE")
                    ck[("pvq", c, h, qh)] = cnt["tE"]
                ck[("pvk", c, h)] = k
            ck[("pv", c, h)] = cnt["tE"]

        def pvevac_emit(c, h):
            k = ck[("pvk", c, h)]
            for qh in (0, 1):
                @blk.vector
                def _(ve, c=c, h=h, qh=qh, k=k):
                    wt(ve, "tE", ck[("pvq", c, h, 1)])
                    i = ve.reciprocal(RSC[:, qh:qh + 1], pbank[k][:, qh * 128 + 64:qh * 128 + 65])
                    inc(i, "vE")
                    ve.wait_ge(sems["vE"], cnt["vE"])
                    t_abs = 2 * c + qh
                    bsl = Bt[:, t_abs * D + h * 64: t_abs * D + h * 64 + 64]
                    i = ve.scalar_tensor_tensor(bsl, pbank[k][:, qh * 128:qh * 128 + 64],
                                                RSC[:, qh:qh + 1],
                                                bsl, ALU.mult, ALU.add)
                    inc(i, "vE")
                ck[("pvadd", c, h, qh)] = cnt["vE"]
            pool_release(k, "vE", ck[("pvadd", c, h, 1)])

        for c in range(NCH):
            ck[("ring_rdy", c)] = ck[("rxt", r, min(2 * c + 3, TT - 1))]
            for h in range(H):
                if h == 0:
                    scores_emit(c, 0)
                    mask_emit(c, 0)
                    exp_emit(c, 0)
                if h + 1 < H:
                    scores_emit(c, h + 1)
                    mask_emit(c, h + 1)
                    exp_emit(c, h + 1)
                if h in (4, 8):
                    t_new = 2 * c + 4 + (1 if h == 8 else 0)
                    if t_new < TT:
                        emit_x0v(t_new, ck.get(("pv", c, 3 if h == 4 else 7), 0))
                        emit_xt_ring(t_new, [("dX", ck[("x0", r)])], XTA)
                        ck[("rxt", r, t_new)] = cnt["vE"]
                pv_emit(c, h)
                pvevac_emit(c, h)
        ck[("pv_all", r)] = cnt["tE"]
        ck[("attn_v", r)] = cnt["vE"]
        ck["prev_exp_s"] = cnt["sE"]
        ck["prev_yev_v"] = cnt["vE"]
        ck["prev_hev"] = ("sE", cnt["sE"])
        if taps_on:
            @blk.gpsimd
            def _(ge, r=r):
                wt(ge, "vE", ck[("attn_v", r)])
                i = ge.dma_start(out=taps["t_attn"][r].rearrange("(t p) c -> p t c", p=128),
                                 in_=Bt[:].rearrange("p (t c) -> p t c", c=D))
                inc(i, "dT", 16)
            ck["hold"] = cnt["dT"]

        @blk.sync
        def _(se, r=r):
            wt(se, "vE", ck[("attn_v", r)])
            wt(se, "tE", ck[("pv_all", r)])
            if taps_on:
                wt(se, "dT", ck.get("hold", 0))
            i = se.dma_start(out=FLN[0:1, 0:D], in_=ln1_g[None, :])
            inc(i, "dG", 16)
            i = se.dma_start(out=FLN[0:1, D:2 * D], in_=ln1_b[None, :])
            inc(i, "dG", 16)
        ck["gbrow"] = cnt["dG"]
        emit_gb_bcast(0, GB0, ck["gbrow"])
        emit_gb_bcast(D, GB1, ck["gbrow"])
        ck["gb_v"] = cnt["vE"]
        ck["gb1_t"] = cnt["tE"]

        for g4 in range(4):
            emit_ln_group(range(g4 * 4, g4 * 4 + 4), "ln_aff", g4 % 2,
                          gates=[("vE", ck["gb_v"]), ("vE", ck[("attn_v", r)])])
        ck["ln1_all_v"] = cnt["vE"]
        for t in range(8):
            emit_xt_ring(t, [("vE", ck[("ln_v", t)]), ("tE", ck.get("ffn_prev_t", 0))], XTF)
        ck[("ring_ok", 0)] = cnt["vE"]
        ck[("ring_ok", 1)] = cnt["vE"]

        @blk.sync
        def _(se, r=r):
            wt(se, "tE", ck["gb1_t"])
            wt(se, "vE", ck[("attn_v", r)])
            if taps_on:
                wt(se, "dT", ck.get("hold", 0))
            i = se.dma_start(out=FLN[0:1, 0:D], in_=ln2_g[None, :])
            inc(i, "dG", 16)
            i = se.dma_start(out=FLN[0:1, D:2 * D], in_=ln2_b[None, :])
            inc(i, "dG", 16)
        ck["gbrow2"] = cnt["dG"]
        emit_gb_bcast(0, GB0, ck["gbrow2"], v_gate=ck["ln1_all_v"])
        emit_gb_bcast(D, GB1, ck["gbrow2"], v_gate=ck["ln1_all_v"])
        ck["gb2_v"] = cnt["vE"]
        ck["gb2_t"] = cnt["tE"]

        def p4_chunk(tcq):
            ts4 = list(range(tcq * 4, tcq * 4 + 4))
            gpar = tcq % 2
            emit_ln_group(ts4, "ln_aff", gpar,
                          gates=[("vE", ck["gb2_v"]), ("vE", ck[("yev_all", "a", tcq)])])

            for t in ts4:
                par = t % 2

                @blk.sync
                def _(se, t=t, r=r, par=par, tcq=tcq):
                    wt(se, "vE", ck.get(("resid", t - 2), 0))
                    wt(se, "vE", ck.get(("sdot", tcq * 4 - 1), 0))
                    semr = "dRa" if par == 0 else "dRb"
                    i = se.dma_start(out=X0S[:, par * D:(par + 1) * D],
                                     in_=x_in[r][t * 128:(t + 1) * 128, :])
                    inc(i, semr, 16)
                ck[("x0rr", t)] = ("dRa" if par == 0 else "dRb",
                                  cnt["dRa" if par == 0 else "dRb"])

                @blk.vector
                def _(ve, t=t, par=par):
                    wt(ve, ck[("x0rr", t)][0], ck[("x0rr", t)][1])
                    wt(ve, "vE", ck[("ln_v", t)])
                    bsl = Bt[:, t * D:(t + 1) * D]
                    i = ve.tensor_add(bsl, bsl, X0S[:, par * D:(par + 1) * D])
                    inc(i, "vE")
                ck[("resid", t)] = cnt["vE"]

            emit_ln_group(ts4, "plain", gpar,
                          tile_gates=lambda t: [("vE", ck[("resid", t)])])

            c0g = gpar * 16
            @blk.vector
            def _(ve, tcq=tcq, c0g=c0g, gpar=gpar):
                wt(ve, "sE", ck[("lng_s7", gpar)])
                g1 = G16[:, tcq * 4:tcq * 4 + 4]
                i = ve.tensor_tensor(g1, STAT[:, c0g + 8:c0g + 12],
                                     STAT[:, c0g + 8:c0g + 12], ALU.mult)
                inc(i, "vE")
                ve.wait_ge(sems["vE"], cnt["vE"])
                i = ve.tensor_scalar(NRM[:, tcq * 4:tcq * 4 + 4], g1,
                                     -float(D) * EPS_LN, float(D), ALU.mult, ALU.add)
                inc(i, "vE")
                ve.wait_ge(sems["vE"], cnt["vE"])
                i = ve.tensor_scalar(NRM[:, tcq * 4:tcq * 4 + 4], NRM[:, tcq * 4:tcq * 4 + 4],
                                     1e-24, None, ALU.max)
                inc(i, "vE")
            ck[("nq0", tcq)] = cnt["vE"]
            ck[("lng_gate_v", gpar)] = cnt["vE"]

            for t in ts4:
                @blk.vector
                def _(ve, t=t, tcq=tcq):
                    wt(ve, "sE", ck[("ln_s", t)])
                    wt(ve, "vE", ck["rtb_v"])
                    wt(ve, ck[("x0rr", tcq * 4 + 3)][0], ck[("x0rr", tcq * 4 + 3)][1])
                    ve.wait_ge(sems["vE"], cnt["vE"])
                    i = ve.scalar_tensor_tensor(GTS[:, 0:512], Bt[:, t * D:t * D + 512], 1.0,
                                                RTB[:, 0:512], ALU.mult, ALU.mult,
                                                accum_out=SC16[:, t:t + 1])
                    inc(i, "vE")
                    i = ve.scalar_tensor_tensor(GTS2[:, 0:512], Bt[:, t * D + 512:t * D + D], 1.0,
                                                RTB[:, 512:D], ALU.mult, ALU.mult,
                                                accum_out=A16[:, t:t + 1])
                    inc(i, "vE")
                ck[("sdot", t)] = cnt["vE"]

            @blk.scalar
            def _(sa, tcq=tcq):
                wt(sa, "vE", ck[("nq0", tcq)])
                g1 = G16[:, tcq * 4:tcq * 4 + 4]
                i = sa.activation(g1, NRM[:, tcq * 4:tcq * 4 + 4], AF.Ln)
                inc(i, "sE")
                sa.wait_ge(sems["sE"], cnt["sE"])
                i = sa.activation(g1, g1, AF.Exp, bias=0.0, scale=-0.5)
                inc(i, "sE")
            ck[("nq1", tcq)] = cnt["sE"]

            @blk.vector
            def _(ve, tcq=tcq):
                wt(ve, "sE", ck[("nq1", tcq)])
                i = ve.tensor_scalar(NRM[:, tcq * 4:tcq * 4 + 4],
                                     G16[:, tcq * 4:tcq * 4 + 4], SQRT_D, None, ALU.mult)
                inc(i, "vE")
            ck[("nq2", tcq)] = cnt["vE"]

            @blk.tensor
            def _(te, tcq=tcq):
                wt(te, "vE", ck[("nq2", tcq)])
                k = pool_acquire(te)
                ck["_nqk"] = k
                i = te.transpose(pbank[k][0:4, 0:128], NRM[:, tcq * 4:tcq * 4 + 4], IDT[:])
                inc(i, "tE")
            ck[("nq3", tcq)] = cnt["tE"]
            k = ck["_nqk"]

            @blk.vector
            def _(ve, tcq=tcq, k=k):
                wt(ve, "tE", ck[("nq3", tcq)])
                i = ve.tensor_copy(FL2N[0:4, tcq * 128:tcq * 128 + 128], pbank[k][0:4, 0:128])
                inc(i, "vE")
            pool_release(k, "vE", cnt["vE"])
            ck[("nq4", tcq)] = cnt["vE"]

            @blk.sync
            def _(se, tcq=tcq):
                wt(se, "vE", ck[("nq4", tcq)])
                wt(se, "tE", ck["gb2_t"])
                for j4 in range(4):
                    i = se.dma_start(out=FLN[0:1, tcq * 512 + j4 * 128:tcq * 512 + j4 * 128 + 128],
                                     in_=FL2N[j4:j4 + 1, tcq * 128:tcq * 128 + 128])
                    inc(i, "dN", 16)
            ck[("nq5", tcq)] = cnt["dN"]

        ffn_pass(0, W1bf, B1C, W2bf, B2ROW, "relu", False, FT, "a")
        p4_chunk(0)
        for t in range(8, 12):
            emit_xt_ring(t, [("vE", ck[("ln_v", t)]), ("tE", ck[("hmm", "a", 0, FT - 1)])], XTF)
        ck[("ring_ok", 2)] = cnt["vE"]
        ffn_pass(1, W1bf, B1C, W2bf, B2ROW, "relu", False, FT, "a")
        p4_chunk(1)
        for t in range(12, 16):
            emit_xt_ring(t, [("vE", ck[("ln_v", t)]), ("tE", ck[("hmm", "a", 1, FT - 1)])], XTF)
        ck[("ring_ok", 3)] = cnt["vE"]
        ffn_pass(2, W1bf, B1C, W2bf, B2ROW, "relu", False, FT, "a")
        p4_chunk(2)
        for t in range(0, 4):
            emit_xt_ring(t, [("sE", ck[("ln_s", t)]), ("tE", ck[("hmm", "a", 2, FT - 1)])], XTF)
            ck[("yrg", t)] = cnt["vE"]
        ck[("ring_ok", 0)] = ck[("yrg", 3)]
        emit_nrmb(0)
        ffn_pass(0, w1l_f, B1LC, None, B2LROW, "gelu", False, LFT, "l", hbase=0)
        ffn_pass(3, W1bf, B1C, W2bf, B2ROW, "relu", False, FT, "a")
        p4_chunk(3)
        for t in range(4, 8):
            emit_xt_ring(t, [("sE", ck[("ln_s", t)]), ("tE", ck[("hmm", "a", 3, FT - 1)])], XTF)
            ck[("yrg", t)] = cnt["vE"]
        ck[("ring_ok", 1)] = ck[("yrg", 7)]
        emit_nrmb(1)
        ffn_pass(1, w1l_f, B1LC, None, B2LROW, "gelu", False, LFT, "l", hbase=4)
        ck["affn_v"] = cnt["vE"]
        if taps_on:
            @blk.gpsimd
            def _(ge, r=r):
                wt(ge, "vE", ck["affn_v"])
                i = ge.dma_start(out=taps["t_ff"][r].rearrange("(t p) c -> p t c", p=128),
                                 in_=Bt[:].rearrange("p (t c) -> p t c", c=D))
                inc(i, "dT", 16)
            ck["hold"] = cnt["dT"]

        @blk.vector
        def _(ve):
            ve.wait_ge(sems["vE"], ck[("sdot", TT - 1)])
            i = ve.tensor_tensor(SC16[:], SC16[:], A16[:], ALU.add)
            inc(i, "vE")
        ck["s_p"] = cnt["vE"]
        if taps_on:
            @blk.gpsimd
            def _(ge, r=r):
                wt(ge, "sE", ck[("ln_s", TT - 1)])
                wt(ge, "vE", ck[("resid", TT - 1)])
                i = ge.dma_start(out=taps["t_y"][r].rearrange("(t p) c -> p t c", p=128),
                                 in_=Bt[:].rearrange("p (t c) -> p t c", c=D))
                inc(i, "dT", 16)
            ck["hold"] = cnt["dT"]
            ck["bt_hold_dT"] = cnt["dT"]


        ROUT = []

        def _first_step():
            @blk.vector
            def _(ve):
                wt(ve, "vE", ck["s_p"])
                i = ve.tensor_scalar(RL16[:], SC16[:], 0.0, None, ALU.max)
                inc(i, "vE")
                ve.wait_ge(sems["vE"], cnt["vE"])
                i = ve.memset(ACOL[:], 0.0)
                inc(i, "vE")
                i = ve.memset(MXC[:], 0.0)
                inc(i, "vE")
                i = ve.memset(BIASC[:], RCONST)
                inc(i, "vE")
                i = ve.memset(RCONC[:], RCONST)
                inc(i, "vE")
            ck["dsc_v"] = cnt["vE"]
            ck["dsc_rl"] = 0
            ck["dsc_acol_v"] = cnt["vE"]
        ROUT.append(_first_step)

        def _iter_step(it):
            @blk.vector
            def _(ve, it=it):
                wt(ve, "vE", ck["dsc_v"] if it == 0 else ck["dsc_acol_v"])
                if it > 0:
                    wt(ve, "sE", ck["dsc_rl"])
                i = ve.scalar_tensor_tensor(A16[:], SC16[:], 1.0, RL16[:], ALU.mult, ALU.subtract)
                inc(i, "vE")
                ck["_d1"] = cnt["vE"]

            @blk.scalar
            def _(sa, it=it):
                wt(sa, "vE", ck["_d1"])
                sa.wait_ge(sems["sE"], cnt["sE"])
                i = sa.activation(GTS[:, 0:TT], A16[:], AF.Exp, bias=MXC[:, 0:1], scale=10.0,
                                  accum_out=STAT[:, 33:34])
                inc(i, "sE")
                ck["_d8"] = cnt["sE"]

            @blk.tensor
            def _(te):
                wt(te, "sE", ck["_d8"])
                k3 = pool_acquire(te)
                ck["_dk3"] = k3
                i = te.matmul(pbank[k3][0:1, 0:1], STAT[:, 33:34], ONE_C[:, 0:1],
                              start=True, stop=True)
                inc(i, "tE")
                ck["_d9"] = cnt["tE"]
            k3 = ck["_dk3"]

            @blk.scalar
            def _(sa, k3=k3):
                wt(sa, "tE", ck["_d9"])
                i = sa.activation(SEQS[0:1, 1:2], pbank[k3][0:1, 0:1], AF.Ln)
                inc(i, "sE")
                ck["_d10"] = cnt["sE"]
            pool_release(k3, "sE", ck["_d10"])

            @blk.tensor
            def _(te):
                wt(te, "sE", ck["_d10"])
                k4 = pool_acquire(te)
                ck["_dk4"] = k4
                i = te.matmul(pbank[k4][:, 0:1], ONE_R[:], SEQS[0:1, 1:2], start=True, stop=True)
                inc(i, "tE")
                ck["_d13"] = cnt["tE"]
            k4 = ck["_dk4"]

            @blk.scalar
            def _(sa, it=it, k4=k4):
                wt(sa, "tE", ck["_d13"])
                i = sa.activation(ACOL[:], pbank[k4][:, 0:1], AF.Identity,
                                  bias=BIASC[:, 0:1], scale=-0.1)
                inc(i, "sE")
                sa.wait_ge(sems["sE"], cnt["sE"])
                i = sa.activation(BIASC[:], ACOL[:], AF.Identity, bias=RCONC[:, 0:1], scale=1.0)
                inc(i, "sE")
                sa.wait_ge(sems["sE"], cnt["sE"])
                i = sa.activation(MXC[:], ACOL[:], AF.Identity, bias=0.0, scale=10.0)
                inc(i, "sE")
                sa.wait_ge(sems["sE"], cnt["sE"])
                i = sa.activation(RL16[:], SC16[:], AF.Relu, bias=ACOL[:, 0:1], scale=1.0)
                inc(i, "sE")
                ck["dsc_rl"] = cnt["sE"]
            pool_release(k4, "sE", ck["dsc_rl"])
        for it in range(N_ITERS):
            ROUT.append(lambda it=it: _iter_step(it))

        def _scores_step():
            @blk.vector
            def _(ve):
                wt(ve, "sE", ck["dsc_rl"])
                i = ve.scalar_tensor_tensor(A16[:], SC16[:], ACOL[:, 0:1], RL16[:],
                                            ALU.add, ALU.subtract)
                inc(i, "vE")
            ck["r1"] = cnt["vE"]

            @blk.scalar
            def _(sa):
                wt(sa, "vE", ck["r1"])
                i = sa.activation(R16[:], A16[:], AF.Exp, bias=0.0, scale=10.0)
                inc(i, "sE")
            ck["r2"] = cnt["sE"]

            @blk.vector
            def _(ve):
                wt(ve, "sE", ck["r2"])
                i = ve.tensor_scalar(R16[:], R16[:], FTZ_DN, FTZ_UP, ALU.mult, ALU.mult)
                inc(i, "vE")
            ck["r3"] = cnt["vE"]

            @blk.tensor
            def _(te):
                wt(te, "vE", ck["r3"])
                k = pool_acquire(te)
                ck["_rk"] = k
                i = te.transpose(pbank[k][0:16, 0:128], R16[:], IDT[:])
                inc(i, "tE")
            ck["r4"] = cnt["tE"]
            k = ck["_rk"]

            @blk.vector
            def _(ve, k=k):
                wt(ve, "tE", ck["r4"])
                i = ve.tensor_copy(FL2[:], pbank[k][0:16, 0:128])
                inc(i, "vE")
            pool_release(k, "vE", cnt["vE"])
            ck["r5"] = cnt["vE"]

            @blk.sync
            def _(se, r=r):
                wt(se, "vE", ck["r5"])
                wt(se, "tE", ck[("pv_all", r)])
                for i16 in range(16):
                    i = se.dma_start(out=FLS[0:1, i16 * 128:(i16 + 1) * 128],
                                     in_=FL2[i16:i16 + 1, :])
                    inc(i, "dF", 16)
            ck["r6"] = cnt["dF"]
        ROUT.append(_scores_step)

        def _rank_prep(c4):
            @blk.tensor
            def _(te, c4=c4):
                wt(te, "dF", ck["r6"])
                wt(te, "vE", ck.get(("rbu", c4 - 1), 0))
                k = pool_acquire(te)
                ck["_rbk"] = k
                i = te.matmul(pbank[k][:, 0:512], ONE_R[:], FLS[0:1, c4 * 512:(c4 + 1) * 512],
                              start=True, stop=True)
                inc(i, "tE")
            ck[("rb", c4)] = cnt["tE"]
            k = ck["_rbk"]

            @blk.vector
            def _(ve, c4=c4, k=k):
                wt(ve, "tE", ck[("rb", c4)])
                xr = ck.get(("x0rr", TT - 1), None)
                if xr is not None:
                    wt(ve, xr[0], xr[1])
                wt(ve, "vE", ck[("sdot", TT - 1)])
                i = ve.tensor_copy(RBC[:], pbank[k][:, 0:512])
                inc(i, "vE")
            pool_release(k, "vE", cnt["vE"])
            ck[("rbc", c4)] = cnt["vE"]

            @blk.sync
            def _(se, c4=c4):
                iu = ck.get(("iotu", c4 - 1), None)
                if iu is not None:
                    wt(se, iu[0], iu[1])
                wt(se, "vE", ck[("sdot", TT - 1)])
                i = se.dma_start(out=IOTC[:], in_=iota_d[:, c4 * 512:(c4 + 1) * 512])
                inc(i, "dI", 16)
            ck[("iot", c4)] = cnt["dI"]

        def _rank_step(c4, it):
            @blk.vector
            def _(ve, c4=c4, it=it):
                wt(ve, "dI", ck[("iot", c4)])
                wt(ve, "vE", ck[("rbc", c4)])
                wt(ve, "vE", ck.get(("eqv", c4, it - 1),
                                    ck.get(("eqv", c4 - 1, TT - 1), 0)))
                i = ve.tensor_scalar(LTM[:], IOTC[:], JIDX[:, it:it + 1], None, ALU.is_lt)
                inc(i, "vE")
                i = ve.tensor_scalar(GTS[:], RBC[:], R16[:, it:it + 1], None, ALU.is_gt,
                                     ALU.add,
                                     accum_out=GPART[:, it * 4 + c4:it * 4 + c4 + 1])
                inc(i, "vE")
                ve.wait_ge(sems["vE"], cnt["vE"])
                i = ve.scalar_tensor_tensor(GTS2[:], RBC[:], R16[:, it:it + 1], LTM[:],
                                            ALU.is_equal, ALU.mult,
                                            accum_out=EPART[:, it * 4 + c4:it * 4 + c4 + 1])
                inc(i, "vE")
            ck[("eqv", c4, it)] = cnt["vE"]
            if it == TT - 1:
                ck[("rbu", c4)] = ck[("eqv", c4, TT - 1)]
                ck[("iotu", c4)] = ("vE", ck[("eqv", c4, TT - 1)])

        for c4 in range(4):
            ROUT.append(lambda c4=c4: _rank_prep(c4))
            for it2 in range(0, TT, 2):
                ROUT.append(lambda c4=c4, it2=it2: (_rank_step(c4, it2), _rank_step(c4, it2 + 1)))

        def _mask_step():
            @blk.vector
            def _(ve):
                wt(ve, "vE", ck[("eqv", 3, TT - 1)])
                i = ve.reduce_sum(G16[:], GPART[:].rearrange("p (a b) -> p a b", b=4), axis=AX.X)
                inc(i, "vE")
                i = ve.reduce_sum(RNK[:], EPART[:].rearrange("p (a b) -> p a b", b=4), axis=AX.X)
                inc(i, "vE")
            ck["m1"] = cnt["vE"]

            @blk.vector
            def _(ve):
                ve.wait_ge(sems["vE"], ck["m1"])
                i = ve.tensor_tensor(RNK[:], RNK[:], G16[:], ALU.add)
                inc(i, "vE")
                ve.wait_ge(sems["vE"], cnt["vE"])
                i = ve.tensor_scalar(M16[:], RNK[:], 1024.0, None, ALU.is_lt)
                inc(i, "vE")
            ck["mask_v"] = cnt["vE"]
        ROUT.append(_mask_step)
        ck["_ROUT"] = ROUT

        ck["mask_v"] = cnt["vE"]
        if taps_on:
            pump_all_ref[0]()
            @blk.gpsimd
            def _(ge, r=r):
                wt(ge, "vE", ck["mask_v"])
                for tn, tt2 in (("t_m", M16), ("t_r", R16), ("t_nrm", NRM), ("t_s", SC16)):
                    i = ge.dma_start(out=taps[tn][r], in_=tt2[:])
                    inc(i, "dT", 16)
            ck["hold"] = cnt["dT"]

        ck[("ring_ok", 0)] = ck[("yrg", 3)]
        ck[("ring_ok", 1)] = ck[("yrg", 7)]
        ffn_pass(0, w1h_f, B1HC, w2hbf, B2HROW, "gelu", True, FT, "h")
        ck[("nrmbu", 0)] = cnt["vE"]
        for t in range(8, 12):
            emit_xt_ring(t, [("sE", ck[("ln_s", t)]),
                             ("tE", ck[("hmm", "h", 0, FT - 1)])], XTF)
        ck[("ring_ok", 2)] = cnt["vE"]
        emit_nrmb(2)
        ffn_pass(2, w1l_f, B1LC, None, B2LROW, "gelu", False, LFT, "l", hbase=8)
        ffn_pass(1, w1h_f, B1HC, w2hbf, B2HROW, "gelu", True, FT, "h")
        ck[("nrmbu", 1)] = cnt["vE"]
        for t in range(12, 16):
            emit_xt_ring(t, [("sE", ck[("ln_s", t)]),
                             ("tE", ck[("hmm", "h", 1, FT - 1)])], XTF)
        ck[("ring_ok", 3)] = cnt["vE"]
        emit_nrmb(3)
        ffn_pass(3, w1l_f, B1LC, None, B2LROW, "gelu", False, LFT, "l", hbase=12)
        ffn_pass(2, w1h_f, B1HC, w2hbf, B2HROW, "gelu", True, FT, "h")
        ck[("nrmbu", 2)] = cnt["vE"]
        ffn_pass(3, w1h_f, B1HC, w2hbf, B2HROW, "gelu", True, FT, "h")
        ck[("nrmbu", 3)] = cnt["vE"]
        pump_all()
        ck[("hv_done", r)] = cnt["tE"]
        ck["z_v"] = cnt["vE"]
        if taps_on:
            @blk.gpsimd
            def _(ge, r=r):
                wt(ge, "vE", ck["z_v"])
                i = ge.dma_start(out=taps["t_z"][r].rearrange("(t p) c -> p t c", p=128),
                                 in_=Bt[:].rearrange("p (t c) -> p t c", c=D))
                inc(i, "dT", 16)
            ck["hold"] = cnt["dT"]

        for _p in range(2):
            ck[("lng_gate_v", _p)] = 0
            ck[("lng_gate_s", _p)] = 0
        ck[("osl_free", 0)] = 0
        ck[("osl_free", 1)] = 0
        for tcq in range(4):
            ts4 = list(range(tcq * 4, tcq * 4 + 4))
            gpar = tcq % 2
            gts = [("vE", ck["z_v"])]
            if taps_on:
                gts.append(("dT", ck.get("hold", 0)))
            emit_ln_group(ts4, "plain_out", gpar, gates=gts, applies=False)
            for i_, t in enumerate(ts4):
                par = t % 2
                sto = ck.get(("st", t - 2), None)
                emit_ln_apply(t, gpar, i_, "plain_out",
                              extra_gates=([sto] if sto is not None else []))

                @blk.scalar
                def _(sa, t=t, r=r, par=par):
                    wt(sa, "sE", ck[("ln_s", t)])
                    semo = "dOa" if par == 0 else "dOb"
                    i = sa.dma_start(out=out_d[r][t * 128:(t + 1) * 128, :],
                                     in_=OUTST[:, par * D:(par + 1) * D])
                    inc(i, semo, 16)
                ck[("st", t)] = ("dOa" if par == 0 else "dOb",
                                cnt["dOa" if par == 0 else "dOb"])
        ck[("row_st_a", r)] = cnt["dOa"]
        ck[("row_st_b", r)] = cnt["dOb"]
        ck[("row_lastln", r)] = ck[("ln_s", TT - 1)]

    @blk.sync
    def _(se):
        for s_ in ("dOa", "dOb", "dX", "dM", "dG", "dW1a", "dW1b", "dW2a", "dW2b",
                   "dRa", "dRb", "dF", "dN", "dI", "dInit", "dInitP", "dT", "dV"):
            wt(se, s_)
    es.close()
    return nc


_N_CORES = 8
_ROWS_PER_CORE = 4
_FEED = {"ln1_g": "L0_attn_ln1_g", "ln1_b": "L0_attn_ln1_b", "ln2_g": "L0_attn_ln2_g",
         "ln2_b": "L0_attn_ln2_b", "W1": "L0_attn_W1", "b1": "L0_attn_b1",
         "W2": "L0_attn_W2", "b2": "L0_attn_b2", "rt": "L0_ffn_routing_token",
         "gam_l": "L0_ffn_gamma_light", "w1l": "L0_ffn_w1_light", "b1l": "L0_ffn_b1_light",
         "w2l": "L0_ffn_w2_light", "b2l": "L0_ffn_b2_light",
         "gam_h": "L0_ffn_gamma_heavy", "w1h": "L0_ffn_w1_heavy", "b1h": "L0_ffn_b1_heavy",
         "w2h": "L0_ffn_w2_heavy", "b2h": "L0_ffn_b2_heavy"}


def kernel(**inputs):
    from concourse.bass_utils import run_bass_kernel_spmd
    x = np.ascontiguousarray(np.asarray(inputs["L0_attn_src"], dtype=np.float32))
    w = {k: np.ascontiguousarray(np.asarray(inputs[v], dtype=np.float32))
         for k, v in _FEED.items()}
    nc = build(_ROWS_PER_CORE, taps_on=False)
    in_maps = []
    for c in range(_N_CORES):
        m = {"x": x[c * _ROWS_PER_CORE:(c + 1) * _ROWS_PER_CORE]}
        m.update(w)
        in_maps.append(m)
    res = run_bass_kernel_spmd(nc, in_maps, core_ids=list(range(_N_CORES)))
    out = np.concatenate([np.asarray(r["out"]) for r in res.results], axis=0)
    return out.astype(np.float32, copy=False)


# version 10
# speedup vs baseline: 1.2316x; 1.0102x over previous
import numpy as np
from contextlib import ExitStack
import concourse.bass as bass
import concourse.mybir as mybir
import ml_dtypes

F32 = mybir.dt.float32
F32R = mybir.dt.float32r
BF16 = mybir.dt.bfloat16
AF = mybir.ActivationFunctionType
ALU = mybir.AluOpType
AX = mybir.AxisListType

S, D, H, DH, WIN = 2048, 1024, 16, 64, 256
NCH = S // WIN
DFF, LH = 4096, 512
TT, DT, FT, LFT = 16, 8, 32, 4
TOKC, NTC = 512, 4
EPS_LN = 1e-5
SHIFT = 80.0
NEG = -1.0e9
N_ITERS = 50
RCONST = float(np.float32(0.1) * np.log(np.float32(8.0), dtype=np.float32))
FTZ_DN = float(2.0 ** -24)
FTZ_UP = float(2.0 ** 24)
SQRT_D = float(np.sqrt(np.float32(1024.0)))
SBASE = 16512
VW = 65


def _np_masks():
    j = np.arange(128)[:, None]
    q = np.arange(256)[None, :]
    out = {}
    for s in (0, 1, 4, 5):
        jj = j + 128 * s
        valid = (q <= jj) & (jj <= q + 512)
        out[s] = np.where(valid, np.float32(0.0), np.float32(NEG)).astype(np.float32)
    return out


def subtiles_of_chunk(c):
    if c == 0:
        return [2, 3, 4, 5]
    if c == NCH - 1:
        return [0, 1, 2, 3]
    return [0, 1, 2, 3, 4, 5]


def masked_pairs(c):
    subs = subtiles_of_chunk(c)
    out = []
    if 0 in subs:
        out.append("01")
    if 4 in subs:
        out.append("45")
    return out


def pv_pairs(c):
    subs = subtiles_of_chunk(c)
    out = []
    for s_ in subs:
        for qh in (0, 1):
            if 128 * s_ + 127 >= 128 * qh and 128 * s_ <= 128 * qh + 127 + 512:
                out.append((s_, qh))
    return out


def build(n_rows, taps_on=False):
    nc = bass.Bass("TRN2", target_bir_lowering=False)
    es = ExitStack()

    semn = ("dX", "dM", "dG", "dW1a", "dW1b", "dW2a", "dW2b", "dRa", "dRb",
            "dOa", "dOb", "dF", "dN", "dI", "dInit", "dInitP", "dT", "dV",
            "dInitB", "dInitC", "dInitD",
            "tE", "vE", "sE", "pE")
    sems = {s_: es.enter_context(nc.semaphore(s_)) for s_ in semn}
    cnt = {s_: 0 for s_ in semn}

    def inc(instr, name, n=1):
        instr.then_inc(sems[name], n)
        cnt[name] += n
        return cnt[name]

    def wt(eng, name, val=None):
        v = cnt[name] if val is None else val
        if v > 0:
            eng.wait_ge(sems[name], v)

    x_in = nc.dram_tensor("x", [n_rows, S, D], F32, kind="ExternalInput").ap()
    out_d = nc.dram_tensor("out", [n_rows, S, D], F32, kind="ExternalOutput").ap()

    def din(name, shape):
        return nc.dram_tensor(name, shape, F32, kind="ExternalInput").ap()
    ln1_g, ln1_b = din("ln1_g", [D]), din("ln1_b", [D])
    ln2_g, ln2_b = din("ln2_g", [D]), din("ln2_b", [D])
    W1d, b1d = din("W1", [D, DFF]), din("b1", [DFF])
    W2d, b2d = din("W2", [DFF, D]), din("b2", [D])
    rtd = din("rt", [D])
    gam_l, w1ld, b1ld = din("gam_l", [D]), din("w1l", [D, LH]), din("b1l", [LH])
    w2ld, b2ld = din("w2l", [LH, D]), din("b2l", [D])
    gam_h, w1hd, b1hd = din("gam_h", [D]), din("w1h", [D, DFF]), din("b1h", [DFF])
    w2hd, b2hd = din("w2h", [DFF, D]), din("b2h", [D])

    W1bf = nc.dram_tensor("W1bf", [D, DFF], BF16, kind="Internal").ap()
    W2bf = nc.dram_tensor("W2bf", [DFF, D], BF16, kind="Internal").ap()
    w2hbf = nc.dram_tensor("w2hbf", [DFF, D], BF16, kind="Internal").ap()
    w2lbf = nc.dram_tensor("w2lbf", [LH, D], BF16, kind="Internal").ap()
    w1l_f = nc.dram_tensor("w1l_f", [D, LH], BF16, kind="Internal").ap()
    w1h_f = nc.dram_tensor("w1h_f", [D, DFF], BF16, kind="Internal").ap()

    taps = {}
    if taps_on:
        for tname in ("t_attn", "t_x1", "t_ff", "t_y", "t_z"):
            taps[tname] = nc.dram_tensor(tname, [n_rows, S, D], F32, kind="ExternalOutput").ap()
        for tname in ("t_s", "t_r", "t_m", "t_nrm"):
            taps[tname] = nc.dram_tensor(tname, [n_rows, 128, TT], F32, kind="ExternalOutput").ap()

    mk = _np_masks()
    ident_d = nc.inline_tensor(np.eye(128, dtype=np.float32), "ident_c").ap()
    mask01_d = nc.inline_tensor(np.concatenate([mk[0], mk[1]], 1), "mask01_c").ap()
    mask45_d = nc.inline_tensor(np.concatenate([mk[4], mk[5]], 1), "mask45_c").ap()
    ones_row_d = nc.inline_tensor(np.ones((1, 128), np.float32), "onesr_c").ap()
    ones_rowbf_d = nc.inline_tensor(np.ones((1, 128), ml_dtypes.bfloat16), "onesrb_c").ap()
    ones_col_d = nc.inline_tensor(np.ones((128, 1), np.float32), "onesc_c").ap()
    iota_d = nc.inline_tensor(
        np.broadcast_to(np.arange(S, dtype=np.float32)[None, :], (128, S)).copy(), "iota_c").ap()
    nshift_d = nc.inline_tensor(np.full((128, 1), -SHIFT, np.float32), "nshift_c").ap()
    jidx_d = nc.inline_tensor(
        (np.arange(TT, dtype=np.float32)[None, :] * 128
         + np.arange(128, dtype=np.float32)[:, None]).astype(np.float32), "jidx_c").ap()

    off = [SBASE]

    def sba(name, shape, dtype, at=None):
        per = int(np.prod(shape[1:])) * mybir.dt.size(dtype)
        if at is None:
            at = off[0]
            off[0] = at + ((per + 31) // 32) * 32
        h = nc.alloc_sbuf_tensor_at(name, shape, dtype, offset=at)
        return h.ap(), at

    Bt, _ = sba("Bt", [128, TT * D], F32)
    XT_at = off[0]
    XTA, _ = sba("XTA", [128, DT * 8 * 128], F32R)
    XTF, _ = sba("XTF", [128, DT * 8 * 128], BF16, at=XT_at)
    HT_at = off[0]
    HT, _ = sba("HT", [128, FT * TOKC], BF16)
    SCW, _ = sba("SCW", [128, 2048], F32, at=HT_at)
    SCWB, _ = sba("SCWB", [128, 2048], BF16, at=HT_at + 8192)
    CC_at = off[0]
    X0V, _ = sba("X0V", [128, 8 * H * VW], BF16, at=CC_at)
    FLS, _ = sba("FLS", [1, S], F32, at=XT_at + 16384)
    FLN, _ = sba("FLN", [1, S], F32, at=XT_at + 24576)
    PT, _ = sba("PT", [128, 2 * 6 * 256], BF16, at=CC_at + 16640)
    SQJ, _ = sba("SQJ", [128, D], F32, at=CC_at + 16640)
    NRMB0, _ = sba("NRMB0", [128, TOKC], F32, at=CC_at + 16640 + 4096)
    off[0] = CC_at + 16640 + 6144
    W1S, _ = sba("W1S", [128, 2 * DT * 128], BF16)
    W2S, _ = sba("W2S", [128, 2 * D], BF16)
    W2SL, _ = sba("W2SL", [128, (LH // 128) * D], BF16)
    M01, _ = sba("M01", [128, 512], F32, at=HT_at + 8192)
    M45, _ = sba("M45", [128, 512], F32, at=HT_at + 12288)
    IDT, _ = sba("IDT", [128, 128], F32)
    ONE_R, _ = sba("ONE_R", [1, 128], F32)
    ONE_RB, _ = sba("ONE_RB", [1, 128], BF16)
    ONE_C, _ = sba("ONE_C", [128, 1], F32)
    B2ROW, _ = sba("B2ROW", [1, D], BF16)
    B2LROW, _ = sba("B2LROW", [1, D], BF16)
    B2HROW, _ = sba("B2HROW", [1, D], BF16)
    B1C, _ = sba("B1C", [128, FT], F32)
    B1LC, _ = sba("B1LC", [128, LFT], F32)
    B1HC, _ = sba("B1HC", [128, FT], F32)
    RTB, _ = sba("RTB", [128, D], F32)
    GAML, _ = sba("GAML", [128, DT], F32)
    GAMH, _ = sba("GAMH", [128, DT], F32)
    GB0, _ = sba("GB0", [128, D], F32)
    GB1, _ = sba("GB1", [128, D], F32)
    X0S_at = off[0]
    X0S, _ = sba("X0S", [128, 2 * D], F32)
    RBC, _ = sba("RBC", [128, 512], F32, at=X0S_at)
    IOTC, _ = sba("IOTC", [128, 512], F32, at=X0S_at + 2048)
    GTS, _ = sba("GTS", [128, 512], F32, at=X0S_at + 4096)
    GTS2, _ = sba("GTS2", [128, 512], F32, at=X0S_at + 6144)
    OUT_at = off[0]
    OUTST, _ = sba("OUTST", [128, 2 * D], F32)
    LTM, _ = sba("LTM", [128, 512], F32, at=OUT_at)
    FL2, _ = sba("FL2", [16, 128], F32)
    FL2N, _ = sba("FL2N", [4, 4 * 128], F32)
    NRMB1, _ = sba("NRMB1", [128, TOKC], F32)
    SC16, _ = sba("SC16", [128, TT], F32)
    A16, _ = sba("A16", [128, TT], F32)
    RL16, _ = sba("RL16", [128, TT], F32)
    R16, _ = sba("R16", [128, TT], F32)
    M16, _ = sba("M16", [128, TT], F32)
    NRM, _ = sba("NRM", [128, TT], F32)
    G16, _ = sba("G16", [128, TT], F32)
    RNK, _ = sba("RNK", [128, TT], F32)
    GPART, _ = sba("GPART", [128, TT * 4], F32)
    EPART, _ = sba("EPART", [128, TT * 4], F32)
    JIDX, _ = sba("JIDX", [128, TT], F32)
    STAT, _ = sba("STAT", [128, 40], F32)
    ACOL, _ = sba("ACOL", [128, 1], F32)
    MXC, _ = sba("MXC", [128, 1], F32)
    RSC, _ = sba("RSC", [128, 4], F32)
    SEQS, _ = sba("SEQS", [1, 16], F32)
    NSH, _ = sba("NSH", [128, 1], F32)
    BIASC, _ = sba("BIASC", [128, 1], F32)
    RCONC, _ = sba("RCONC", [128, 1], F32)
    assert off[0] <= 229344, f"sbuf layout overflow: {off[0]}"

    pbank = [es.enter_context(nc.psum_tensor(f"pb{i}", [128, 512], F32)) for i in range(2)]
    psc = [es.enter_context(nc.psum_tensor(f"psc{i}", [128, 1536], F32)) for i in range(2)]
    pool_free = {0: None, 1: None}
    pool_next = [0]

    def pool_acquire(eng):
        k = pool_next[0]
        pool_next[0] = 1 - k
        fa = pool_free[k]
        if fa is not None:
            wt(eng, fa[0], fa[1])
            pool_free[k] = None
        return k

    def pool_release(k, semname, c_):
        pool_free[k] = (semname, c_)

    blk = es.enter_context(nc.Block())
    ck = {}
    pump_all_ref = [None]

    @blk.sync
    def _(se):
        for dst, src in ((IDT[:], ident_d),
                         (ONE_R[:], ones_row_d), (ONE_RB[:], ones_rowbf_d),
                         (ONE_C[:], ones_col_d), (JIDX[:], jidx_d), (NSH[:], nshift_d)):
            inc(se.dma_start(out=dst, in_=src), "dInit", 16)
        inc(se.dma_start(out=SCW[0:32, 0:128], in_=b1d.rearrange("(a b) -> a b", b=128)), "dInit", 16)
        inc(se.dma_start(out=SCW[0:32, 128:256], in_=b1hd.rearrange("(a b) -> a b", b=128)), "dInit", 16)
        inc(se.dma_start(out=SCW[0:4, 256:384], in_=b1ld.rearrange("(a b) -> a b", b=128)), "dInit", 16)
        inc(se.dma_start(out=SCW[0:8, 384:512], in_=gam_l.rearrange("(a b) -> a b", b=128)), "dInit", 16)
        inc(se.dma_start(out=SCW[0:8, 512:640], in_=gam_h.rearrange("(a b) -> a b", b=128)), "dInit", 16)
        inc(se.dma_start(out=SCW[0:1, 640:640 + D], in_=rtd[None, :]), "dInit", 16)

    @blk.gpsimd
    def _(ge):
        inc(ge.dma_start(out=W1bf[:], in_=W1d[:]), "dInitP", 16)
        inc(ge.dma_start(out=W2bf[:], in_=W2d[:]), "dInitP", 16)
        inc(ge.dma_start(out=w2hbf[:], in_=w2hd[:]), "dInitP", 16)
        inc(ge.dma_start(out=w2lbf[:], in_=w2ld[:]), "dInitP", 16)
        inc(ge.dma_start(out=B2ROW[:], in_=b2d[None, :]), "dInitP", 16)
        inc(ge.dma_start(out=B2LROW[:], in_=b2ld[None, :]), "dInitP", 16)
        inc(ge.dma_start(out=B2HROW[:], in_=b2hd[None, :]), "dInitP", 16)
    ck["precast_dP"] = cnt["dInitP"]

    @blk.sync
    def _(se):
        wt(se, "dInitP", ck["precast_dP"])
        src = w2lbf.rearrange("(fs p) c -> p fs c", p=128)
        inc(se.dma_start(out=W2SL[:].rearrange("p (fs c) -> p fs c", c=D), in_=src), "dInit", 16)
    ck["w2sl_dS"] = cnt["dInit"]

    @blk.tensor
    def _(te):
        wt(te, "dInit")
        pt = pbank[0]
        te.transpose(pt[:, 0:32], SCW[0:32, 0:128], IDT[0:32, 0:32])
        te.transpose(pt[:, 32:64], SCW[0:32, 128:256], IDT[0:32, 0:32])
        te.transpose(pt[:, 64:68], SCW[0:4, 256:384], IDT[0:4, 0:4])
        te.transpose(pt[:, 68:76], SCW[0:8, 384:512], IDT[0:8, 0:8])
        i = te.transpose(pt[:, 76:84], SCW[0:8, 512:640], IDT[0:8, 0:8])
        inc(i, "tE")
    ck["setup_t1"] = cnt["tE"]

    @blk.vector
    def _(ve):
        wt(ve, "tE", ck["setup_t1"])
        pt = pbank[0]
        ve.tensor_copy(B1C[:], pt[:, 0:32])
        ve.tensor_copy(B1HC[:], pt[:, 32:64])
        ve.tensor_copy(B1LC[:], pt[:, 64:68])
        ve.tensor_copy(GAML[:], pt[:, 68:76])
        i = ve.tensor_copy(GAMH[:], pt[:, 76:84])
        inc(i, "vE")
        i = ve.memset(X0V[:].rearrange("p (sh e) -> p sh e", e=VW)[:, :, 64:65], 1.0)
        inc(i, "vE")
    ck["setup_v"] = cnt["vE"]
    ck["x0v_ones_setup"] = cnt["vE"]
    pool_free[0] = ("vE", ck["setup_v"])

    for half in range(2):
        @blk.tensor
        def _(te, half=half):
            k = pool_acquire(te)
            ck["_bk"] = k
            i = te.matmul(pbank[k][:, 0:512], ONE_R[:],
                          SCW[0:1, 640 + half * 512:640 + half * 512 + 512], start=True, stop=True)
            inc(i, "tE")
        k = ck["_bk"]

        @blk.vector
        def _(ve, half=half, k=k):
            wt(ve, "tE")
            i = ve.tensor_copy(RTB[:, half * 512:half * 512 + 512], pbank[k][:, 0:512])
            inc(i, "vE")
        pool_release(k, "vE", cnt["vE"])
    ck["rtb_v"] = cnt["vE"]

    fold_jobs = [(w1ld, w1l_f, GAML, dt_i, 0, LH) for dt_i in range(DT)]
    for dt_i in range(DT):
        for cc in range(4):
            fold_jobs.append((w1hd, w1h_f, GAMH, dt_i, cc * 1024, 1024))
    for jj, (wsrc, wdst, gt, dt_i, c0, wdt) in enumerate(fold_jobs):
        sl = jj % 2
        sin = "dInit" if sl == 0 else "dInitB"
        sout = "dInitC" if sl == 0 else "dInitD"

        @blk.sync
        def _(se, wsrc=wsrc, dt_i=dt_i, c0=c0, wdt=wdt, sl=sl, sin=sin, sout=sout, jj=jj):
            wt(se, sout)
            wt(se, "vE", ck["rtb_v" if jj < 2 else "setup_v"])
            i = se.dma_start(out=SCW[:, sl * 1024:sl * 1024 + wdt],
                             in_=wsrc[dt_i * 128:(dt_i + 1) * 128, c0:c0 + wdt])
            inc(i, sin, 16)

        @blk.vector
        def _(ve, gt=gt, dt_i=dt_i, wdt=wdt, sl=sl, sin=sin):
            wt(ve, sin)
            ve.wait_ge(sems["vE"], cnt["vE"])
            i = ve.tensor_scalar(SCWB[:, sl * 1024:sl * 1024 + wdt],
                                 SCW[:, sl * 1024:sl * 1024 + wdt],
                                 gt[:, dt_i:dt_i + 1], None, ALU.mult)
            inc(i, "vE")
            ck["_fv"] = cnt["vE"]

        @blk.scalar
        def _(sa, wdst=wdst, dt_i=dt_i, c0=c0, wdt=wdt, sl=sl, sout=sout):
            wt(sa, "vE", ck["_fv"])
            i = sa.dma_start(out=wdst[dt_i * 128:(dt_i + 1) * 128, c0:c0 + wdt],
                             in_=SCWB[:, sl * 1024:sl * 1024 + wdt])
            inc(i, sout, 16)
    ck["fold_dP"] = cnt["dInitP"]
    ck["fold_dC"] = cnt["dInitC"]
    ck["fold_dD"] = cnt["dInitD"]

    def rc(ds, tile):
        return ds * (8 * 128) + (tile % 8) * 128

    def emit_xt_ring(t, gates, dst):
        for half in range(2):
            @blk.tensor
            def _(te, t=t, half=half, gates=gates):
                for g in gates:
                    wt(te, g[0], g[1])
                k = pool_acquire(te)
                last = None
                for j in range(4):
                    ds = half * 4 + j
                    last = te.transpose(pbank[k][:, j * 128:j * 128 + 128],
                                        Bt[:, t * D + ds * 128: t * D + ds * 128 + 128],
                                        IDT[:])
                inc(last, "tE")
                ck["_tpk"] = k
            k = ck["_tpk"]

            @blk.vector
            def _(ve, t=t, half=half, k=k, dst=dst):
                wt(ve, "tE")
                dv = dst[:].rearrange("p (a c) -> p a c", c=8 * 128)[
                    :, half * 4:half * 4 + 4, (t % 8) * 128:(t % 8) * 128 + 128]
                sv = pbank[k][:, 0:512].rearrange("p (a c) -> p a c", c=128)
                i = ve.tensor_copy(dv, sv)
                inc(i, "vE")
            pool_release(k, "vE", cnt["vE"])

    def emit_gb_bcast(col0, dst, gate_ds, v_gate=0):
        for half in range(2):
            @blk.tensor
            def _(te, half=half, col0=col0, gate_ds=gate_ds):
                wt(te, "dG", gate_ds)
                k = pool_acquire(te)
                i = te.matmul(pbank[k][:, 0:512], ONE_R[:],
                              FLN[0:1, col0 + half * 512:col0 + half * 512 + 512],
                              start=True, stop=True)
                inc(i, "tE")
                ck["_bk"] = k
            k = ck["_bk"]

            @blk.vector
            def _(ve, half=half, dst=dst, k=k, v_gate=v_gate):
                wt(ve, "tE")
                if v_gate:
                    wt(ve, "vE", v_gate)
                i = ve.tensor_copy(dst[:, half * 512:half * 512 + 512], pbank[k][:, 0:512])
                inc(i, "vE")
            pool_release(k, "vE", cnt["vE"])

    def emit_nrmb(tcq):
        @blk.tensor
        def _(te, tcq=tcq, dn_now=None):
            wt(te, "dN", cnt["dN"])
            wt(te, "vE", ck.get(("nrmbu", tcq - 2), ck.get(("nrmbu", tcq + 2), 0)))
            kk = pool_acquire(te)
            ck["_nbk"] = kk
            i = te.matmul(pbank[kk][:, 0:512], ONE_R[:],
                          FLN[0:1, tcq * 512:(tcq + 1) * 512], start=True, stop=True)
            inc(i, "tE")
        ck[("nrmbp", tcq)] = cnt["tE"]
        kk = ck["_nbk"]

        @blk.vector
        def _(ve, tcq=tcq, kk=kk):
            wt(ve, "tE", ck[("nrmbp", tcq)])
            i = ve.tensor_copy((NRMB0 if tcq % 2 == 0 else NRMB1)[:],
                               pbank[kk][:, 0:512])
            inc(i, "vE")
        pool_release(kk, "vE", cnt["vE"])
        ck[("nrmb_v", tcq)] = cnt["vE"]

    def emit_ln_apply(t, gpar, i_, kind, extra_gates=(), osl=None, osl_key="osl_free"):
        c0 = gpar * 16
        bsl = Bt[:, t * D:(t + 1) * D]
        if osl is None:
            osl = OUTST[:, (t % 2) * D:(t % 2) * D + D]
        sc2c = STAT[:, c0 + 12 + i_:c0 + 13 + i_]
        rsdc = STAT[:, c0 + 8 + i_:c0 + 9 + i_]
        if kind == "ln_aff":
            @blk.scalar
            def _(sa, t=t, sc2c=sc2c, rsdc=rsdc, bsl=bsl, osl=osl, gpar=gpar):
                wt(sa, "vE", ck[("lng_v8", gpar)])
                wt(sa, "vE", ck.get(("osl_free", t % 2), 0))
                for g in extra_gates:
                    wt(sa, g[0], g[1])
                i = sa.activation(osl, bsl, AF.Identity, bias=sc2c, scale=rsdc)
                inc(i, "sE")
                ck["_l9"] = cnt["sE"]

            @blk.vector
            def _(ve, t=t, osl=osl, bsl=bsl):
                wt(ve, "sE", ck["_l9"])
                i = ve.tensor_mul(osl, osl, GB0[:])
                inc(i, "vE")
                ve.wait_ge(sems["vE"], cnt["vE"])
                i = ve.tensor_add(bsl, osl, GB1[:])
                inc(i, "vE")
            ck[("ln_v", t)] = cnt["vE"]
            ck[("osl_free", t % 2)] = cnt["vE"]
            ck[("lng_gate_s", gpar)] = ck["_l9"]
        else:
            dst = bsl if kind == "plain" else osl

            @blk.scalar
            def _(sa, t=t, dst=dst, kind=kind, sc2c=sc2c, rsdc=rsdc, bsl=bsl, gpar=gpar,
                  osl_key=osl_key):
                wt(sa, "vE", ck[("lng_v8", gpar)])
                if kind == "plain_out":
                    wt(sa, "vE", ck.get((osl_key, t % 2), 0))
                for g in extra_gates:
                    wt(sa, g[0], g[1])
                i = sa.activation(dst, bsl, AF.Identity, bias=sc2c, scale=rsdc)
                inc(i, "sE")
            ck[("ln_s", t)] = cnt["sE"]
            ck[("lng_gate_s", gpar)] = cnt["sE"]

    def emit_ln_group(tlist, kind, gpar, gates=(), tile_gates=None, applies=True):
        tlist = list(tlist)
        n = len(tlist)
        c0 = gpar * 16
        ngm = STAT[:, c0:c0 + n]
        var = STAT[:, c0 + 4:c0 + 4 + n]
        rsd = STAT[:, c0 + 8:c0 + 8 + n]
        sc2 = STAT[:, c0 + 12:c0 + 12 + n]

        @blk.vector
        def _(ve, gates=gates, gpar=gpar, tlist=tlist):
            wt(ve, "vE", ck.get(("lng_gate_v", gpar), 0))
            wt(ve, "sE", ck.get(("lng_gate_s", gpar), 0))
            for g in gates:
                wt(ve, g[0], g[1])
            for i_, t in enumerate(tlist):
                if tile_gates is not None:
                    for g in tile_gates(t):
                        wt(ve, g[0], g[1])
                bsl = Bt[:, t * D:(t + 1) * D]
                i = ve.reduce_sum(STAT[:, c0 + i_:c0 + i_ + 1], bsl, axis=AX.X, negate=True)
                inc(i, "vE")
                ve.wait_ge(sems["vE"], cnt["vE"])
                i = ve.scalar_tensor_tensor(SQJ[:], bsl, 1.0, bsl, ALU.mult, ALU.mult,
                                            accum_out=STAT[:, c0 + 4 + i_:c0 + 5 + i_])
                inc(i, "vE")
                ve.wait_ge(sems["vE"], cnt["vE"])
            i = ve.tensor_scalar(ngm, ngm, 1.0 / D, None, ALU.mult)
            inc(i, "vE")
            ve.wait_ge(sems["vE"], cnt["vE"])
            i = ve.tensor_scalar(var, var, 1.0 / D, EPS_LN, ALU.mult, ALU.add)
            inc(i, "vE")
            ve.wait_ge(sems["vE"], cnt["vE"])
            i = ve.tensor_tensor(sc2, ngm, ngm, ALU.mult)
            inc(i, "vE")
            ve.wait_ge(sems["vE"], cnt["vE"])
            i = ve.tensor_tensor(var, var, sc2, ALU.subtract)
            inc(i, "vE")
            ck["_g4"] = cnt["vE"]

        @blk.scalar
        def _(sa, gpar=gpar):
            wt(sa, "vE", ck["_g4"])
            i = sa.activation(rsd, var, AF.Ln)
            inc(i, "sE")
            sa.wait_ge(sems["sE"], cnt["sE"])
            i = sa.activation(rsd, rsd, AF.Exp, bias=0.0, scale=-0.5)
            inc(i, "sE")
            ck[("lng_s7", gpar)] = cnt["sE"]

        @blk.vector
        def _(ve, gpar=gpar):
            wt(ve, "sE", ck[("lng_s7", gpar)])
            i = ve.tensor_tensor(sc2, ngm, rsd, ALU.mult)
            inc(i, "vE")
        ck[("lng_v8", gpar)] = cnt["vE"]
        ck[("lng_gate_v", gpar)] = cnt["vE"]

        if applies:
            for i_, t in enumerate(tlist):
                emit_ln_apply(t, gpar, i_, kind)

    def pump(n=1):
        q = ck.get("_ROUT")
        for _ in range(n):
            if q:
                q.pop(0)()

    def pump_all():
        q = ck.get("_ROUT")
        while q:
            q.pop(0)()
    pump_all_ref[0] = pump_all

    def ffn_pass(tc, w1src, b1col, w2src, b2row, act, masked, nft, tag, hbase=0):
        slot0 = (4 * tc) % 8
        vE0 = ck.get("prev_yev_v", 0)
        sE0 = ck.get("prev_exp_s", 0)
        pheng, phval = ck.get("prev_hev", ("sE", 0))
        heng = "vE" if act == "relu" else "sE"
        for ft in range(nft):
            @blk.sync
            def _(se, ft=ft, tc=tc, w1src=w1src, tag=tag):
                wt(se, "tE", ck.get(("hmm", tag, tc, ft - 2), ck.get("ffn_prev_t", 0)))
                wt(se, "dInitP", ck["fold_dP"])
                wt(se, "dInitC", ck["fold_dC"])
                wt(se, "dInitD", ck["fold_dD"])
                src = w1src[:, ft * 128:(ft + 1) * 128].rearrange("(ds p) c -> p ds c", p=128)
                sem1 = "dW1a" if ft % 2 == 0 else "dW1b"
                i = se.dma_start(out=W1S[:, (ft % 2) * 1024:(ft % 2) * 1024 + 1024]
                                 .rearrange("p (ds c) -> p ds c", c=128), in_=src)
                inc(i, sem1, 16)
            ck[("w1s", tag, tc, ft)] = ("dW1a" if ft % 2 == 0 else "dW1b",
                                        cnt["dW1a" if ft % 2 == 0 else "dW1b"])

            @blk.tensor
            def _(te, ft=ft, tc=tc, slot0=slot0, tag=tag, sE0=sE0, vE0=vE0, act=act, hbase=hbase,
                  pheng=pheng, phval=phval, heng=heng):
                wt(te, ck[("w1s", tag, tc, ft)][0], ck[("w1s", tag, tc, ft)][1])
                wt(te, "vE", ck.get(("ring_ok", tc), 0))
                if ft < 2:
                    wt(te, "vE", vE0)
                    wt(te, pheng, phval)
                wt(te, "sE", sE0)
                wt(te, heng, ck.get(("hev", tag, tc, ft - 2), 0))
                if act == "gelu":
                    wt(te, "vE", max(ck.get("prev_hpre_v", 0) if ft < 2 else 0,
                                     ck.get(("hpre", tag, tc, ft - 2), 0)))
                pt = psc[0][:, (ft % 2) * 512:(ft % 2) * 512 + 512]
                last = None
                for ds in range(DT):
                    last = te.matmul(
                        pt, W1S[:, (ft % 2) * 1024 + ds * 128:(ft % 2) * 1024 + ds * 128 + 128],
                        XTF[:, ds * 1024 + slot0 * 128: ds * 1024 + slot0 * 128 + 512],
                        start=(ds == 0), stop=(ds == DT - 1))
                inc(last, "tE")
            ck[("hmm", tag, tc, ft)] = cnt["tE"]

            if act == "relu":
                @blk.vector
                def _(ve, ft=ft, tc=tc, b1col=b1col, tag=tag, hbase=hbase):
                    wt(ve, "tE", ck[("hmm", tag, tc, ft)])
                    ve.wait_ge(sems["vE"], cnt["vE"])
                    i = ve.tensor_scalar(HT[:, (hbase + ft) * TOKC:(hbase + ft + 1) * TOKC],
                                         psc[0][:, (ft % 2) * 512:(ft % 2) * 512 + 512],
                                         b1col[:, ft:ft + 1], 0.0, ALU.add, ALU.max)
                    inc(i, "vE")
                ck[("hev", tag, tc, ft)] = cnt["vE"]
            else:
                @blk.vector
                def _(ve, ft=ft, tc=tc, tag=tag, sE0=sE0, pheng=pheng, phval=phval):
                    wt(ve, "tE", ck[("hmm", tag, tc, ft)])
                    wt(ve, "vE", ck.get(("nrmb_v", tc), 0))
                    if ft < 2:
                        wt(ve, pheng, phval)
                    wt(ve, "sE", max(sE0, ck.get(("hev", tag, tc, ft - 2), 0)))
                    i = ve.tensor_tensor(OUTST[:, (ft % 2) * 512 + 1024:(ft % 2) * 512 + 1536],
                                         psc[0][:, (ft % 2) * 512:(ft % 2) * 512 + 512],
                                         (NRMB0 if tc % 2 == 0 else NRMB1)[:], ALU.mult)
                    inc(i, "vE")
                ck[("hpre", tag, tc, ft)] = cnt["vE"]

                @blk.scalar
                def _(sa, ft=ft, tc=tc, b1col=b1col, tag=tag, hbase=hbase):
                    wt(sa, "vE", ck[("hpre", tag, tc, ft)])
                    i = sa.activation(HT[:, (hbase + ft) * TOKC:(hbase + ft + 1) * TOKC],
                                      OUTST[:, (ft % 2) * 512 + 1024:(ft % 2) * 512 + 1536],
                                      AF.Gelu, bias=b1col[:, ft:ft + 1], scale=1.0)
                    inc(i, "sE")
                ck[("hev", tag, tc, ft)] = cnt["sE"]
            if ft % 2 == 1:
                pump(1)

        if masked:
            pump_all()
        @blk.tensor
        def _(te, tc=tc, tag=tag, nft=nft, heng=heng):
            wt(te, heng, ck[("hev", tag, tc, nft - 1)])
            wt(te, heng, ck.get(("hev", tag, tc, nft - 2), 0))
            k6 = pool_acquire(te)
            k7 = pool_acquire(te)
            ck["_yk67"] = (k6, k7)
        k6, k7 = ck["_yk67"]
        ybank = {}
        for tt_ in range(4):
            for dh in range(2):
                idx = 2 * tt_ + dh
                if idx < 6:
                    ybank[(tt_, dh)] = psc[idx // 3][:, (idx % 3) * 512:(idx % 3) * 512 + 512]
                else:
                    ybank[(tt_, dh)] = pbank[(k6, k7)[idx - 6]][:]

        use_stream = w2src is not None
        for fs in range(nft):
            if use_stream:
                @blk.sync
                def _(se, fs=fs, tc=tc, tag=tag, w2src=w2src):
                    wt(se, "tE", ck.get(("ymm", tag, tc, fs - 2), 0))
                    wt(se, "dInitP", ck["precast_dP"])
                    src = w2src[fs * 128:(fs + 1) * 128, :]
                    sem2 = "dW2a" if fs % 2 == 0 else "dW2b"
                    i = se.dma_start(out=W2S[:, (fs % 2) * D:(fs % 2) * D + D], in_=src)
                    inc(i, sem2, 16)
                ck[("w2s", tag, tc, fs)] = ("dW2a" if fs % 2 == 0 else "dW2b",
                                            cnt["dW2a" if fs % 2 == 0 else "dW2b"])

            @blk.tensor
            def _(te, fs=fs, tc=tc, tag=tag, use_stream=use_stream, vE0=vE0, ybank=ybank, hbase=hbase):
                if use_stream:
                    wt(te, ck[("w2s", tag, tc, fs)][0], ck[("w2s", tag, tc, fs)][1])
                else:
                    wt(te, "dInit", ck["w2sl_dS"])
                if fs == 0:
                    wt(te, "vE", max(vE0, ck.get(("yev_all", tag, tc), 0)))
                last = None
                for tt_ in range(4):
                    for dh in range(2):
                        pt = ybank[(tt_, dh)]
                        rhs = (W2S[:, (fs % 2) * D + dh * 512:(fs % 2) * D + dh * 512 + 512]
                               if use_stream
                               else W2SL[:, fs * D + dh * 512:fs * D + dh * 512 + 512])
                        last = te.matmul(pt,
                                         HT[:, (hbase + fs) * TOKC + tt_ * 128:
                                            (hbase + fs) * TOKC + tt_ * 128 + 128],
                                         rhs, start=(fs == 0), stop=False)
                inc(last, "tE")
            ck[("ymm", tag, tc, fs)] = cnt["tE"]

        @blk.tensor
        def _(te, tc=tc, tag=tag, b2row=b2row, ybank=ybank):
            last = None
            for tt_ in range(4):
                for dh in range(2):
                    last = te.matmul(ybank[(tt_, dh)], ONE_RB[:],
                                     b2row[0:1, dh * 512:dh * 512 + 512],
                                     start=False, stop=True)
            inc(last, "tE")
        ck[("yb2", tag, tc)] = cnt["tE"]

        for tt_ in range(4):
            t_abs = tc * 4 + tt_
            for dh in range(2):
                @blk.vector
                def _(ve, tt_=tt_, dh=dh, tc=tc, t_abs=t_abs, masked=masked, tag=tag, ybank=ybank):
                    wt(ve, "tE", ck[("yb2", tag, tc)])
                    wt(ve, "dT", ck.get("hold", 0))
                    pt = ybank[(tt_, dh)]
                    bsl = Bt[:, t_abs * D + dh * 512: t_abs * D + dh * 512 + 512]
                    if masked:
                        wt(ve, "vE", ck.get("mask_v", 0))
                        i = ve.scalar_tensor_tensor(bsl, pt, M16[:, t_abs:t_abs + 1], bsl,
                                                    ALU.mult, ALU.add)
                    else:
                        i = ve.tensor_add(bsl, bsl, pt)
                    inc(i, "vE")
                ck[("yev", tag, tc, tt_, dh)] = cnt["vE"]
        ck[("yev_all", tag, tc)] = cnt["vE"]
        pool_release(k6, "vE", cnt["vE"])
        pool_release(k7, "vE", cnt["vE"])
        ck["ffn_prev_t"] = cnt["tE"]
        ck["prev_yev_v"] = cnt["vE"]
        ck["prev_hev"] = (heng, ck[("hev", tag, tc, nft - 1)])
        if act == "gelu":
            ck["prev_hpre_v"] = ck[("hpre", tag, tc, nft - 1)]

    for r in range(n_rows):
        @blk.sync
        def _(se, r=r):
            wt(se, "dOa", ck.get(("row_st_a", r - 1), 0))
            wt(se, "dOb", ck.get(("row_st_b", r - 1), 0))
            wt(se, "sE", ck.get(("row_lastln", r - 1), 0))
            src = x_in[r].rearrange("(t p) c -> p t c", p=128)
            i = se.dma_start(out=Bt[:].rearrange("p (t c) -> p t c", c=D), in_=src)
            inc(i, "dX", 16)
            wt(se, "tE", ck.get(("hv_done", r - 1), 0))
            if r == 0:
                wt(se, "dInitC", ck["fold_dC"])
                wt(se, "dInitD", ck["fold_dD"])
            inc(se.dma_start(out=M01[:], in_=mask01_d), "dM", 16)
            inc(se.dma_start(out=M45[:], in_=mask45_d), "dM", 16)
        ck[("x0", r)] = cnt["dX"]
        ck[("msk", r)] = cnt["dM"]

        def emit_x0v(t, extra_gate_t):
            sl = t % 8

            @blk.gpsimd
            def _(ge, t=t, r=r, sl=sl, extra_gate_t=extra_gate_t):
                wt(ge, "dX", ck[("x0", r)])
                if extra_gate_t:
                    wt(ge, "tE", extra_gate_t)
                src = Bt[:, t * D:(t + 1) * D].rearrange("p (h d) -> p h d", d=64)
                blkv = X0V[:, sl * H * VW:(sl + 1) * H * VW].rearrange("p (h e) -> p h e", e=VW)
                i = ge.dma_start(out=blkv[:, :, 0:64], in_=src)
                inc(i, "dV", 16)
            ck[("x0v", r, t)] = cnt["dV"]

        for t in range(4):
            emit_x0v(t, ck.get(("hv_done", r - 1), 0) if t == 0 else 0)
        row0_vE = cnt["vE"]
        row0_sE = cnt["sE"]
        for t in range(4):
            emit_xt_ring(t, [("dX", ck[("x0", r)]),
                             ("tE", ck.get(("hv_done", r - 1), 0))], XTA)
            ck[("rxt", r, t)] = cnt["vE"]

        def scores_emit(c, h):
            par = h % 2
            subs = subtiles_of_chunk(c)
            dt_i = h // 2
            r0 = (h % 2) * 64
            pk = ("exp", c, h - 2) if h >= 2 else ("exp", c - 1, H - 2 + h)

            @blk.tensor
            def _(te, c=c, h=h, par=par, subs=subs, dt_i=dt_i, r0=r0, pk=pk, r=r):
                wt(te, "sE", ck.get(pk, 0))
                if c == 0 and h < 2:
                    wt(te, "vE", row0_vE)
                    wt(te, "sE", row0_sE)
                wt(te, "vE", ck.get(("ring_rdy", c), 0))
                last = None
                for s_ in subs:
                    ktile = 2 * c - 2 + s_
                    col = rc(dt_i, ktile)
                    q0 = rc(dt_i, 2 * c)
                    last = te.matmul(psc[par][:, s_ * 256:s_ * 256 + 256],
                                     XTA[r0:r0 + 64, col:col + 128],
                                     XTA[r0:r0 + 64, q0:q0 + 256],
                                     start=True, stop=True, tile_position=(r0, 0))
                inc(last, "tE")
            ck[("scores", c, h)] = cnt["tE"]

        def mask_emit(c, h):
            par = h % 2
            mp = masked_pairs(c)

            @blk.vector
            def _(ve, c=c, h=h, par=par, mp=mp, r=r):
                wt(ve, "tE", ck[("scores", c, h)])
                wt(ve, "dM", ck[("msk", r)])
                last = None
                for pair in mp:
                    off = 0 if pair == "01" else 1024
                    msk = M01 if pair == "01" else M45
                    last = ve.tensor_tensor(psc[par][:, off:off + 512],
                                            psc[par][:, off:off + 512], msk[:], ALU.add)
                inc(last, "vE")
            ck[("mask", c, h)] = cnt["vE"]

        def exp_emit(c, h):
            par = h % 2
            subs = subtiles_of_chunk(c)
            pvk = ("pv", c, h - 2) if h >= 2 else ("pv", c - 1, H - 2 + h)

            @blk.scalar
            def _(sa, c=c, h=h, par=par, subs=subs, pvk=pvk):
                wt(sa, "vE", ck[("mask", c, h)])
                wt(sa, "tE", ck[("scores", c, h)])
                wt(sa, "tE", ck.get(pvk, 0))
                w0 = min(subs) * 256
                w1 = (max(subs) + 1) * 256
                i = sa.activation(PT[:, par * 1536 + w0: par * 1536 + w1],
                                  psc[par][:, w0:w1], AF.Exp, bias=NSH[:, 0:1], scale=1.0)
                inc(i, "sE")
            ck[("exp", c, h)] = cnt["sE"]

        def pv_emit(c, h):
            par = h % 2
            pairs = pv_pairs(c)

            @blk.tensor
            def _(te, c=c, h=h, par=par, pairs=pairs, r=r):
                wt(te, "sE", ck[("exp", c, h)])
                tk = min(2 * c + 3 if h < 4 else (2 * c + 4 if h < 8 else 2 * c + 5), TT - 1)
                wt(te, "dV", ck[("x0v", r, tk)])
                wt(te, "vE", ck["x0v_ones_setup"])
                k = pool_acquire(te)
                np_ = {q_: sum(1 for s2, q2 in pairs if q2 == q_) for q_ in (0, 1)}
                done = {0: 0, 1: 0}
                lastq = {}
                for qh in (0, 1):
                    for s_, q_ in pairs:
                        if q_ != qh:
                            continue
                        ktile = 2 * c - 2 + s_
                        sl = ktile % 8
                        lhs = PT[:, par * 1536 + s_ * 256 + qh * 128:
                                 par * 1536 + s_ * 256 + qh * 128 + 128]
                        rhs = X0V[:, sl * H * VW + h * VW: sl * H * VW + h * VW + VW]
                        done[qh] += 1
                        lastq[qh] = te.matmul(pbank[k][:, qh * 128:qh * 128 + VW], lhs, rhs,
                                              start=(done[qh] == 1), stop=(done[qh] == np_[qh]))
                for qh in (0, 1):
                    inc(lastq[qh], "tE")
                    ck[("pvq", c, h, qh)] = cnt["tE"]
                ck[("pvk", c, h)] = k
            ck[("pv", c, h)] = cnt["tE"]

        def pvevac_emit(c, h):
            k = ck[("pvk", c, h)]
            for qh in (0, 1):
                @blk.vector
                def _(ve, c=c, h=h, qh=qh, k=k):
                    wt(ve, "tE", ck[("pvq", c, h, 1)])
                    i = ve.reciprocal(RSC[:, qh:qh + 1], pbank[k][:, qh * 128 + 64:qh * 128 + 65])
                    inc(i, "vE")
                    ve.wait_ge(sems["vE"], cnt["vE"])
                    t_abs = 2 * c + qh
                    bsl = Bt[:, t_abs * D + h * 64: t_abs * D + h * 64 + 64]
                    i = ve.scalar_tensor_tensor(bsl, pbank[k][:, qh * 128:qh * 128 + 64],
                                                RSC[:, qh:qh + 1],
                                                bsl, ALU.mult, ALU.add)
                    inc(i, "vE")
                ck[("pvadd", c, h, qh)] = cnt["vE"]
            pool_release(k, "vE", ck[("pvadd", c, h, 1)])

        for c in range(NCH):
            ck[("ring_rdy", c)] = ck[("rxt", r, min(2 * c + 3, TT - 1))]
            for h in range(H):
                if h == 0:
                    scores_emit(c, 0)
                    mask_emit(c, 0)
                    exp_emit(c, 0)
                if h + 1 < H:
                    scores_emit(c, h + 1)
                    mask_emit(c, h + 1)
                    exp_emit(c, h + 1)
                if h in (4, 8):
                    t_new = 2 * c + 4 + (1 if h == 8 else 0)
                    if t_new < TT:
                        emit_x0v(t_new, ck.get(("pv", c, 3 if h == 4 else 7), 0))
                        emit_xt_ring(t_new, [("dX", ck[("x0", r)])], XTA)
                        ck[("rxt", r, t_new)] = cnt["vE"]
                pv_emit(c, h)
                pvevac_emit(c, h)
        ck[("pv_all", r)] = cnt["tE"]
        ck[("attn_v", r)] = cnt["vE"]
        ck["prev_exp_s"] = cnt["sE"]
        ck["prev_yev_v"] = cnt["vE"]
        ck["prev_hev"] = ("sE", cnt["sE"])
        if taps_on:
            @blk.gpsimd
            def _(ge, r=r):
                wt(ge, "vE", ck[("attn_v", r)])
                i = ge.dma_start(out=taps["t_attn"][r].rearrange("(t p) c -> p t c", p=128),
                                 in_=Bt[:].rearrange("p (t c) -> p t c", c=D))
                inc(i, "dT", 16)
            ck["hold"] = cnt["dT"]

        @blk.sync
        def _(se, r=r):
            wt(se, "vE", ck[("attn_v", r)])
            wt(se, "tE", ck[("pv_all", r)])
            if taps_on:
                wt(se, "dT", ck.get("hold", 0))
            i = se.dma_start(out=FLN[0:1, 0:D], in_=ln1_g[None, :])
            inc(i, "dG", 16)
            i = se.dma_start(out=FLN[0:1, D:2 * D], in_=ln1_b[None, :])
            inc(i, "dG", 16)
        ck["gbrow"] = cnt["dG"]
        emit_gb_bcast(0, GB0, ck["gbrow"])
        emit_gb_bcast(D, GB1, ck["gbrow"])
        ck["gb_v"] = cnt["vE"]
        ck["gb1_t"] = cnt["tE"]

        for g4 in range(4):
            emit_ln_group(range(g4 * 4, g4 * 4 + 4), "ln_aff", g4 % 2,
                          gates=[("vE", ck["gb_v"]), ("vE", ck[("attn_v", r)])])
        ck["ln1_all_v"] = cnt["vE"]
        for t in range(8):
            emit_xt_ring(t, [("vE", ck[("ln_v", t)]), ("tE", ck.get("ffn_prev_t", 0))], XTF)
        ck[("ring_ok", 0)] = cnt["vE"]
        ck[("ring_ok", 1)] = cnt["vE"]

        @blk.sync
        def _(se, r=r):
            wt(se, "tE", ck["gb1_t"])
            wt(se, "vE", ck[("attn_v", r)])
            if taps_on:
                wt(se, "dT", ck.get("hold", 0))
            i = se.dma_start(out=FLN[0:1, 0:D], in_=ln2_g[None, :])
            inc(i, "dG", 16)
            i = se.dma_start(out=FLN[0:1, D:2 * D], in_=ln2_b[None, :])
            inc(i, "dG", 16)
        ck["gbrow2"] = cnt["dG"]
        emit_gb_bcast(0, GB0, ck["gbrow2"], v_gate=ck["ln1_all_v"])
        emit_gb_bcast(D, GB1, ck["gbrow2"], v_gate=ck["ln1_all_v"])
        ck["gb2_v"] = cnt["vE"]
        ck["gb2_t"] = cnt["tE"]

        def p4_chunk(tcq):
            ts4 = list(range(tcq * 4, tcq * 4 + 4))
            gpar = tcq % 2
            emit_ln_group(ts4, "ln_aff", gpar,
                          gates=[("vE", ck["gb2_v"]), ("vE", ck[("yev_all", "a", tcq)])])

            for t in ts4:
                par = t % 2

                @blk.sync
                def _(se, t=t, r=r, par=par, tcq=tcq):
                    wt(se, "vE", ck.get(("resid", t - 2), 0))
                    wt(se, "vE", ck.get(("sdot", tcq * 4 - 1), 0))
                    semr = "dRa" if par == 0 else "dRb"
                    i = se.dma_start(out=X0S[:, par * D:(par + 1) * D],
                                     in_=x_in[r][t * 128:(t + 1) * 128, :])
                    inc(i, semr, 16)
                ck[("x0rr", t)] = ("dRa" if par == 0 else "dRb",
                                  cnt["dRa" if par == 0 else "dRb"])

                @blk.vector
                def _(ve, t=t, par=par):
                    wt(ve, ck[("x0rr", t)][0], ck[("x0rr", t)][1])
                    wt(ve, "vE", ck[("ln_v", t)])
                    bsl = Bt[:, t * D:(t + 1) * D]
                    i = ve.tensor_add(bsl, bsl, X0S[:, par * D:(par + 1) * D])
                    inc(i, "vE")
                ck[("resid", t)] = cnt["vE"]

            emit_ln_group(ts4, "plain", gpar,
                          tile_gates=lambda t: [("vE", ck[("resid", t)])])

            c0g = gpar * 16
            @blk.vector
            def _(ve, tcq=tcq, c0g=c0g, gpar=gpar):
                wt(ve, "sE", ck[("lng_s7", gpar)])
                g1 = G16[:, tcq * 4:tcq * 4 + 4]
                i = ve.tensor_tensor(g1, STAT[:, c0g + 8:c0g + 12],
                                     STAT[:, c0g + 8:c0g + 12], ALU.mult)
                inc(i, "vE")
                ve.wait_ge(sems["vE"], cnt["vE"])
                i = ve.tensor_scalar(NRM[:, tcq * 4:tcq * 4 + 4], g1,
                                     -float(D) * EPS_LN, float(D), ALU.mult, ALU.add)
                inc(i, "vE")
                ve.wait_ge(sems["vE"], cnt["vE"])
                i = ve.tensor_scalar(NRM[:, tcq * 4:tcq * 4 + 4], NRM[:, tcq * 4:tcq * 4 + 4],
                                     1e-24, None, ALU.max)
                inc(i, "vE")
            ck[("nq0", tcq)] = cnt["vE"]
            ck[("lng_gate_v", gpar)] = cnt["vE"]

            for t in ts4:
                @blk.vector
                def _(ve, t=t, tcq=tcq):
                    wt(ve, "sE", ck[("ln_s", t)])
                    wt(ve, "vE", ck["rtb_v"])
                    wt(ve, ck[("x0rr", tcq * 4 + 3)][0], ck[("x0rr", tcq * 4 + 3)][1])
                    ve.wait_ge(sems["vE"], cnt["vE"])
                    i = ve.scalar_tensor_tensor(GTS[:, 0:512], Bt[:, t * D:t * D + 512], 1.0,
                                                RTB[:, 0:512], ALU.mult, ALU.mult,
                                                accum_out=SC16[:, t:t + 1])
                    inc(i, "vE")
                    i = ve.scalar_tensor_tensor(GTS2[:, 0:512], Bt[:, t * D + 512:t * D + D], 1.0,
                                                RTB[:, 512:D], ALU.mult, ALU.mult,
                                                accum_out=A16[:, t:t + 1])
                    inc(i, "vE")
                ck[("sdot", t)] = cnt["vE"]

            @blk.scalar
            def _(sa, tcq=tcq):
                wt(sa, "vE", ck[("nq0", tcq)])
                g1 = G16[:, tcq * 4:tcq * 4 + 4]
                i = sa.activation(g1, NRM[:, tcq * 4:tcq * 4 + 4], AF.Ln)
                inc(i, "sE")
                sa.wait_ge(sems["sE"], cnt["sE"])
                i = sa.activation(g1, g1, AF.Exp, bias=0.0, scale=-0.5)
                inc(i, "sE")
            ck[("nq1", tcq)] = cnt["sE"]

            @blk.vector
            def _(ve, tcq=tcq):
                wt(ve, "sE", ck[("nq1", tcq)])
                i = ve.tensor_scalar(NRM[:, tcq * 4:tcq * 4 + 4],
                                     G16[:, tcq * 4:tcq * 4 + 4], SQRT_D, None, ALU.mult)
                inc(i, "vE")
            ck[("nq2", tcq)] = cnt["vE"]

            @blk.tensor
            def _(te, tcq=tcq):
                wt(te, "vE", ck[("nq2", tcq)])
                k = pool_acquire(te)
                ck["_nqk"] = k
                i = te.transpose(pbank[k][0:4, 0:128], NRM[:, tcq * 4:tcq * 4 + 4], IDT[:])
                inc(i, "tE")
            ck[("nq3", tcq)] = cnt["tE"]
            k = ck["_nqk"]

            @blk.vector
            def _(ve, tcq=tcq, k=k):
                wt(ve, "tE", ck[("nq3", tcq)])
                i = ve.tensor_copy(FL2N[0:4, tcq * 128:tcq * 128 + 128], pbank[k][0:4, 0:128])
                inc(i, "vE")
            pool_release(k, "vE", cnt["vE"])
            ck[("nq4", tcq)] = cnt["vE"]

            @blk.sync
            def _(se, tcq=tcq):
                wt(se, "vE", ck[("nq4", tcq)])
                wt(se, "tE", ck["gb2_t"])
                for j4 in range(4):
                    i = se.dma_start(out=FLN[0:1, tcq * 512 + j4 * 128:tcq * 512 + j4 * 128 + 128],
                                     in_=FL2N[j4:j4 + 1, tcq * 128:tcq * 128 + 128])
                    inc(i, "dN", 16)
            ck[("nq5", tcq)] = cnt["dN"]

        ffn_pass(0, W1bf, B1C, W2bf, B2ROW, "relu", False, FT, "a")
        p4_chunk(0)
        for t in range(8, 12):
            emit_xt_ring(t, [("vE", ck[("ln_v", t)]), ("tE", ck[("hmm", "a", 0, FT - 1)])], XTF)
        ck[("ring_ok", 2)] = cnt["vE"]
        ffn_pass(1, W1bf, B1C, W2bf, B2ROW, "relu", False, FT, "a")
        p4_chunk(1)
        for t in range(12, 16):
            emit_xt_ring(t, [("vE", ck[("ln_v", t)]), ("tE", ck[("hmm", "a", 1, FT - 1)])], XTF)
        ck[("ring_ok", 3)] = cnt["vE"]
        ffn_pass(2, W1bf, B1C, W2bf, B2ROW, "relu", False, FT, "a")
        p4_chunk(2)
        for t in range(0, 4):
            emit_xt_ring(t, [("sE", ck[("ln_s", t)]), ("tE", ck[("hmm", "a", 2, FT - 1)])], XTF)
            ck[("yrg", t)] = cnt["vE"]
        ck[("ring_ok", 0)] = ck[("yrg", 3)]
        emit_nrmb(0)
        ffn_pass(0, w1l_f, B1LC, None, B2LROW, "gelu", False, LFT, "l", hbase=0)
        ffn_pass(3, W1bf, B1C, W2bf, B2ROW, "relu", False, FT, "a")
        p4_chunk(3)
        for t in range(4, 8):
            emit_xt_ring(t, [("sE", ck[("ln_s", t)]), ("tE", ck[("hmm", "a", 3, FT - 1)])], XTF)
            ck[("yrg", t)] = cnt["vE"]
        ck[("ring_ok", 1)] = ck[("yrg", 7)]
        emit_nrmb(1)
        ffn_pass(1, w1l_f, B1LC, None, B2LROW, "gelu", False, LFT, "l", hbase=4)
        ck["affn_v"] = cnt["vE"]
        if taps_on:
            @blk.gpsimd
            def _(ge, r=r):
                wt(ge, "vE", ck["affn_v"])
                i = ge.dma_start(out=taps["t_ff"][r].rearrange("(t p) c -> p t c", p=128),
                                 in_=Bt[:].rearrange("p (t c) -> p t c", c=D))
                inc(i, "dT", 16)
            ck["hold"] = cnt["dT"]

        @blk.vector
        def _(ve):
            ve.wait_ge(sems["vE"], ck[("sdot", TT - 1)])
            i = ve.tensor_tensor(SC16[:], SC16[:], A16[:], ALU.add)
            inc(i, "vE")
        ck["s_p"] = cnt["vE"]
        if taps_on:
            @blk.gpsimd
            def _(ge, r=r):
                wt(ge, "sE", ck[("ln_s", TT - 1)])
                wt(ge, "vE", ck[("resid", TT - 1)])
                i = ge.dma_start(out=taps["t_y"][r].rearrange("(t p) c -> p t c", p=128),
                                 in_=Bt[:].rearrange("p (t c) -> p t c", c=D))
                inc(i, "dT", 16)
            ck["hold"] = cnt["dT"]
            ck["bt_hold_dT"] = cnt["dT"]


        ROUT = []

        def _first_step():
            @blk.vector
            def _(ve):
                wt(ve, "vE", ck["s_p"])
                i = ve.tensor_scalar(RL16[:], SC16[:], 0.0, None, ALU.max)
                inc(i, "vE")
                ve.wait_ge(sems["vE"], cnt["vE"])
                i = ve.memset(ACOL[:], 0.0)
                inc(i, "vE")
                i = ve.memset(MXC[:], 0.0)
                inc(i, "vE")
                i = ve.memset(BIASC[:], RCONST)
                inc(i, "vE")
                i = ve.memset(RCONC[:], RCONST)
                inc(i, "vE")
            ck["dsc_v"] = cnt["vE"]
            ck["dsc_rl"] = 0
            ck["dsc_acol_v"] = cnt["vE"]
        ROUT.append(_first_step)

        def _iter_step(it):
            @blk.vector
            def _(ve, it=it):
                wt(ve, "vE", ck["dsc_v"] if it == 0 else ck["dsc_acol_v"])
                if it > 0:
                    wt(ve, "sE", ck["dsc_rl"])
                i = ve.scalar_tensor_tensor(A16[:], SC16[:], 1.0, RL16[:], ALU.mult, ALU.subtract)
                inc(i, "vE")
                ck["_d1"] = cnt["vE"]

            @blk.scalar
            def _(sa, it=it):
                wt(sa, "vE", ck["_d1"])
                sa.wait_ge(sems["sE"], cnt["sE"])
                i = sa.activation(GTS[:, 0:TT], A16[:], AF.Exp, bias=MXC[:, 0:1], scale=10.0,
                                  accum_out=STAT[:, 33:34])
                inc(i, "sE")
                ck["_d8"] = cnt["sE"]

            @blk.tensor
            def _(te):
                wt(te, "sE", ck["_d8"])
                k3 = pool_acquire(te)
                ck["_dk3"] = k3
                i = te.matmul(pbank[k3][0:1, 0:1], STAT[:, 33:34], ONE_C[:, 0:1],
                              start=True, stop=True)
                inc(i, "tE")
                ck["_d9"] = cnt["tE"]
            k3 = ck["_dk3"]

            @blk.scalar
            def _(sa, k3=k3):
                wt(sa, "tE", ck["_d9"])
                i = sa.activation(SEQS[0:1, 1:2], pbank[k3][0:1, 0:1], AF.Ln)
                inc(i, "sE")
                ck["_d10"] = cnt["sE"]
            pool_release(k3, "sE", ck["_d10"])

            @blk.tensor
            def _(te):
                wt(te, "sE", ck["_d10"])
                k4 = pool_acquire(te)
                ck["_dk4"] = k4
                i = te.matmul(pbank[k4][:, 0:1], ONE_R[:], SEQS[0:1, 1:2], start=True, stop=True)
                inc(i, "tE")
                ck["_d13"] = cnt["tE"]
            k4 = ck["_dk4"]

            @blk.scalar
            def _(sa, it=it, k4=k4):
                wt(sa, "tE", ck["_d13"])
                i = sa.activation(ACOL[:], pbank[k4][:, 0:1], AF.Identity,
                                  bias=BIASC[:, 0:1], scale=-0.1)
                inc(i, "sE")
                sa.wait_ge(sems["sE"], cnt["sE"])
                i = sa.activation(BIASC[:], ACOL[:], AF.Identity, bias=RCONC[:, 0:1], scale=1.0)
                inc(i, "sE")
                sa.wait_ge(sems["sE"], cnt["sE"])
                i = sa.activation(MXC[:], ACOL[:], AF.Identity, bias=0.0, scale=10.0)
                inc(i, "sE")
                sa.wait_ge(sems["sE"], cnt["sE"])
                i = sa.activation(RL16[:], SC16[:], AF.Relu, bias=ACOL[:, 0:1], scale=1.0)
                inc(i, "sE")
                ck["dsc_rl"] = cnt["sE"]
            pool_release(k4, "sE", ck["dsc_rl"])
        for it in range(N_ITERS):
            ROUT.append(lambda it=it: _iter_step(it))

        def _scores_step():
            @blk.vector
            def _(ve):
                wt(ve, "sE", ck["dsc_rl"])
                i = ve.scalar_tensor_tensor(A16[:], SC16[:], ACOL[:, 0:1], RL16[:],
                                            ALU.add, ALU.subtract)
                inc(i, "vE")
            ck["r1"] = cnt["vE"]

            @blk.scalar
            def _(sa):
                wt(sa, "vE", ck["r1"])
                i = sa.activation(R16[:], A16[:], AF.Exp, bias=0.0, scale=10.0)
                inc(i, "sE")
            ck["r2"] = cnt["sE"]

            @blk.vector
            def _(ve):
                wt(ve, "sE", ck["r2"])
                i = ve.tensor_scalar(R16[:], R16[:], FTZ_DN, FTZ_UP, ALU.mult, ALU.mult)
                inc(i, "vE")
            ck["r3"] = cnt["vE"]

            @blk.tensor
            def _(te):
                wt(te, "vE", ck["r3"])
                k = pool_acquire(te)
                ck["_rk"] = k
                i = te.transpose(pbank[k][0:16, 0:128], R16[:], IDT[:])
                inc(i, "tE")
            ck["r4"] = cnt["tE"]
            k = ck["_rk"]

            @blk.vector
            def _(ve, k=k):
                wt(ve, "tE", ck["r4"])
                i = ve.tensor_copy(FL2[:], pbank[k][0:16, 0:128])
                inc(i, "vE")
            pool_release(k, "vE", cnt["vE"])
            ck["r5"] = cnt["vE"]

            @blk.sync
            def _(se, r=r):
                wt(se, "vE", ck["r5"])
                wt(se, "tE", ck[("pv_all", r)])
                for i16 in range(16):
                    i = se.dma_start(out=FLS[0:1, i16 * 128:(i16 + 1) * 128],
                                     in_=FL2[i16:i16 + 1, :])
                    inc(i, "dF", 16)
            ck["r6"] = cnt["dF"]
        ROUT.append(_scores_step)

        def _rank_prep(c4):
            @blk.tensor
            def _(te, c4=c4):
                wt(te, "dF", ck["r6"])
                wt(te, "vE", ck.get(("rbu", c4 - 1), 0))
                k = pool_acquire(te)
                ck["_rbk"] = k
                i = te.matmul(pbank[k][:, 0:512], ONE_R[:], FLS[0:1, c4 * 512:(c4 + 1) * 512],
                              start=True, stop=True)
                inc(i, "tE")
            ck[("rb", c4)] = cnt["tE"]
            k = ck["_rbk"]

            @blk.vector
            def _(ve, c4=c4, k=k):
                wt(ve, "tE", ck[("rb", c4)])
                xr = ck.get(("x0rr", TT - 1), None)
                if xr is not None:
                    wt(ve, xr[0], xr[1])
                wt(ve, "vE", ck[("sdot", TT - 1)])
                i = ve.tensor_copy(RBC[:], pbank[k][:, 0:512])
                inc(i, "vE")
            pool_release(k, "vE", cnt["vE"])
            ck[("rbc", c4)] = cnt["vE"]

            @blk.sync
            def _(se, c4=c4):
                iu = ck.get(("iotu", c4 - 1), None)
                if iu is not None:
                    wt(se, iu[0], iu[1])
                wt(se, "vE", ck[("sdot", TT - 1)])
                i = se.dma_start(out=IOTC[:], in_=iota_d[:, c4 * 512:(c4 + 1) * 512])
                inc(i, "dI", 16)
            ck[("iot", c4)] = cnt["dI"]

        def _rank_step(c4, it):
            @blk.vector
            def _(ve, c4=c4, it=it):
                wt(ve, "dI", ck[("iot", c4)])
                wt(ve, "vE", ck[("rbc", c4)])
                wt(ve, "vE", ck.get(("eqv", c4, it - 1),
                                    ck.get(("eqv", c4 - 1, TT - 1), 0)))
                i = ve.tensor_scalar(LTM[:], IOTC[:], JIDX[:, it:it + 1], None, ALU.is_lt)
                inc(i, "vE")
                i = ve.tensor_scalar(GTS[:], RBC[:], R16[:, it:it + 1], None, ALU.is_gt,
                                     ALU.add,
                                     accum_out=GPART[:, it * 4 + c4:it * 4 + c4 + 1])
                inc(i, "vE")
                ve.wait_ge(sems["vE"], cnt["vE"])
                i = ve.scalar_tensor_tensor(GTS2[:], RBC[:], R16[:, it:it + 1], LTM[:],
                                            ALU.is_equal, ALU.mult,
                                            accum_out=EPART[:, it * 4 + c4:it * 4 + c4 + 1])
                inc(i, "vE")
            ck[("eqv", c4, it)] = cnt["vE"]
            if it == TT - 1:
                ck[("rbu", c4)] = ck[("eqv", c4, TT - 1)]
                ck[("iotu", c4)] = ("vE", ck[("eqv", c4, TT - 1)])

        for c4 in range(4):
            ROUT.append(lambda c4=c4: _rank_prep(c4))
            for it2 in range(0, TT, 2):
                ROUT.append(lambda c4=c4, it2=it2: (_rank_step(c4, it2), _rank_step(c4, it2 + 1)))

        def _mask_step():
            @blk.vector
            def _(ve):
                wt(ve, "vE", ck[("eqv", 3, TT - 1)])
                i = ve.reduce_sum(G16[:], GPART[:].rearrange("p (a b) -> p a b", b=4), axis=AX.X)
                inc(i, "vE")
                i = ve.reduce_sum(RNK[:], EPART[:].rearrange("p (a b) -> p a b", b=4), axis=AX.X)
                inc(i, "vE")
            ck["m1"] = cnt["vE"]

            @blk.vector
            def _(ve):
                ve.wait_ge(sems["vE"], ck["m1"])
                i = ve.tensor_tensor(RNK[:], RNK[:], G16[:], ALU.add)
                inc(i, "vE")
                ve.wait_ge(sems["vE"], cnt["vE"])
                i = ve.tensor_scalar(M16[:], RNK[:], 1024.0, None, ALU.is_lt)
                inc(i, "vE")
            ck["mask_v"] = cnt["vE"]
        ROUT.append(_mask_step)
        ck["_ROUT"] = ROUT

        ck["mask_v"] = cnt["vE"]
        if taps_on:
            pump_all_ref[0]()
            @blk.gpsimd
            def _(ge, r=r):
                wt(ge, "vE", ck["mask_v"])
                for tn, tt2 in (("t_m", M16), ("t_r", R16), ("t_nrm", NRM), ("t_s", SC16)):
                    i = ge.dma_start(out=taps[tn][r], in_=tt2[:])
                    inc(i, "dT", 16)
            ck["hold"] = cnt["dT"]

        ck[("ring_ok", 0)] = ck[("yrg", 3)]
        ck[("ring_ok", 1)] = ck[("yrg", 7)]
        ffn_pass(0, w1h_f, B1HC, w2hbf, B2HROW, "gelu", True, FT, "h")
        ck[("nrmbu", 0)] = cnt["vE"]
        for t in range(8, 12):
            emit_xt_ring(t, [("sE", ck[("ln_s", t)]),
                             ("tE", ck[("hmm", "h", 0, FT - 1)])], XTF)
        ck[("ring_ok", 2)] = cnt["vE"]
        emit_nrmb(2)
        ffn_pass(2, w1l_f, B1LC, None, B2LROW, "gelu", False, LFT, "l", hbase=8)
        ffn_pass(1, w1h_f, B1HC, w2hbf, B2HROW, "gelu", True, FT, "h")
        ck[("nrmbu", 1)] = cnt["vE"]
        for t in range(12, 16):
            emit_xt_ring(t, [("sE", ck[("ln_s", t)]),
                             ("tE", ck[("hmm", "h", 1, FT - 1)])], XTF)
        ck[("ring_ok", 3)] = cnt["vE"]
        emit_nrmb(3)
        ffn_pass(3, w1l_f, B1LC, None, B2LROW, "gelu", False, LFT, "l", hbase=12)
        ffn_pass(2, w1h_f, B1HC, w2hbf, B2HROW, "gelu", True, FT, "h")
        ck[("nrmbu", 2)] = cnt["vE"]
        ffn_pass(3, w1h_f, B1HC, w2hbf, B2HROW, "gelu", True, FT, "h")
        ck[("nrmbu", 3)] = cnt["vE"]
        pump_all()
        ck[("hv_done", r)] = cnt["tE"]
        ck["z_v"] = cnt["vE"]
        if taps_on:
            @blk.gpsimd
            def _(ge, r=r):
                wt(ge, "vE", ck["z_v"])
                i = ge.dma_start(out=taps["t_z"][r].rearrange("(t p) c -> p t c", p=128),
                                 in_=Bt[:].rearrange("p (t c) -> p t c", c=D))
                inc(i, "dT", 16)
            ck["hold"] = cnt["dT"]

        for _p in range(2):
            ck[("lng_gate_v", _p)] = 0
            ck[("lng_gate_s", _p)] = 0
        ck[("osl_free", 0)] = 0
        ck[("osl_free", 1)] = 0
        for tcq in range(4):
            ts4 = list(range(tcq * 4, tcq * 4 + 4))
            gpar = tcq % 2
            gts = [("vE", ck["z_v"])]
            if taps_on:
                gts.append(("dT", ck.get("hold", 0)))
            emit_ln_group(ts4, "plain_out", gpar, gates=gts, applies=False)
            for i_, t in enumerate(ts4):
                par = t % 2
                sto = ck.get(("st", t - 2), None)
                emit_ln_apply(t, gpar, i_, "plain_out",
                              extra_gates=([sto] if sto is not None else []))

                @blk.scalar
                def _(sa, t=t, r=r, par=par):
                    wt(sa, "sE", ck[("ln_s", t)])
                    semo = "dOa" if par == 0 else "dOb"
                    i = sa.dma_start(out=out_d[r][t * 128:(t + 1) * 128, :],
                                     in_=OUTST[:, par * D:(par + 1) * D])
                    inc(i, semo, 16)
                ck[("st", t)] = ("dOa" if par == 0 else "dOb",
                                cnt["dOa" if par == 0 else "dOb"])
        ck[("row_st_a", r)] = cnt["dOa"]
        ck[("row_st_b", r)] = cnt["dOb"]
        ck[("row_lastln", r)] = ck[("ln_s", TT - 1)]

    @blk.sync
    def _(se):
        for s_ in ("dOa", "dOb", "dX", "dM", "dG", "dW1a", "dW1b", "dW2a", "dW2b",
                   "dRa", "dRb", "dF", "dN", "dI", "dInit", "dInitP", "dT", "dV"):
            wt(se, s_)
    es.close()
    return nc


_N_CORES = 8
_ROWS_PER_CORE = 4
_FEED = {"ln1_g": "L0_attn_ln1_g", "ln1_b": "L0_attn_ln1_b", "ln2_g": "L0_attn_ln2_g",
         "ln2_b": "L0_attn_ln2_b", "W1": "L0_attn_W1", "b1": "L0_attn_b1",
         "W2": "L0_attn_W2", "b2": "L0_attn_b2", "rt": "L0_ffn_routing_token",
         "gam_l": "L0_ffn_gamma_light", "w1l": "L0_ffn_w1_light", "b1l": "L0_ffn_b1_light",
         "w2l": "L0_ffn_w2_light", "b2l": "L0_ffn_b2_light",
         "gam_h": "L0_ffn_gamma_heavy", "w1h": "L0_ffn_w1_heavy", "b1h": "L0_ffn_b1_heavy",
         "w2h": "L0_ffn_w2_heavy", "b2h": "L0_ffn_b2_heavy"}


def kernel(**inputs):
    from concourse.bass_utils import run_bass_kernel_spmd
    x = np.ascontiguousarray(np.asarray(inputs["L0_attn_src"], dtype=np.float32))
    w = {k: np.ascontiguousarray(np.asarray(inputs[v], dtype=np.float32))
         for k, v in _FEED.items()}
    nc = build(_ROWS_PER_CORE, taps_on=False)
    in_maps = []
    for c in range(_N_CORES):
        m = {"x": x[c * _ROWS_PER_CORE:(c + 1) * _ROWS_PER_CORE]}
        m.update(w)
        in_maps.append(m)
    res = run_bass_kernel_spmd(nc, in_maps, core_ids=list(range(_N_CORES)))
    out = np.concatenate([np.asarray(r["out"]) for r in res.results], axis=0)
    return out.astype(np.float32, copy=False)


# version 11 (final)
# speedup vs baseline: 1.2341x; 1.0021x over previous
import numpy as np
from contextlib import ExitStack
import concourse.bass as bass
import concourse.mybir as mybir
import ml_dtypes

F32 = mybir.dt.float32
F32R = mybir.dt.float32r
BF16 = mybir.dt.bfloat16
AF = mybir.ActivationFunctionType
ALU = mybir.AluOpType
AX = mybir.AxisListType

S, D, H, DH, WIN = 2048, 1024, 16, 64, 256
NCH = S // WIN
DFF, LH = 4096, 512
TT, DT, FT, LFT = 16, 8, 32, 4
TOKC, NTC = 512, 4
EPS_LN = 1e-5
SHIFT = 80.0
NEG = -1.0e9
N_ITERS = 50
RCONST = float(np.float32(0.1) * np.log(np.float32(8.0), dtype=np.float32))
FTZ_DN = float(2.0 ** -24)
FTZ_UP = float(2.0 ** 24)
SQRT_D = float(np.sqrt(np.float32(1024.0)))
SBASE = 16512
VW = 65


def _np_masks():
    j = np.arange(128)[:, None]
    q = np.arange(256)[None, :]
    out = {}
    for s in (0, 1, 4, 5):
        jj = j + 128 * s
        valid = (q <= jj) & (jj <= q + 512)
        out[s] = np.where(valid, np.float32(0.0), np.float32(NEG)).astype(np.float32)
    return out


def subtiles_of_chunk(c):
    if c == 0:
        return [2, 3, 4, 5]
    if c == NCH - 1:
        return [0, 1, 2, 3]
    return [0, 1, 2, 3, 4, 5]


def masked_pairs(c):
    subs = subtiles_of_chunk(c)
    out = []
    if 0 in subs:
        out.append("01")
    if 4 in subs:
        out.append("45")
    return out


def pv_pairs(c):
    subs = subtiles_of_chunk(c)
    out = []
    for s_ in subs:
        for qh in (0, 1):
            if 128 * s_ + 127 >= 128 * qh and 128 * s_ <= 128 * qh + 127 + 512:
                out.append((s_, qh))
    return out


def build(n_rows, taps_on=False):
    nc = bass.Bass("TRN2", target_bir_lowering=False)
    es = ExitStack()

    semn = ("dX", "dM", "dG", "dW1a", "dW1b", "dW2a", "dW2b", "dRa", "dRb",
            "dOa", "dOb", "dF", "dN", "dI", "dInit", "dInitP", "dT", "dV",
            "dInitB", "dInitC", "dInitD",
            "tE", "vE", "sE", "pE")
    sems = {s_: es.enter_context(nc.semaphore(s_)) for s_ in semn}
    cnt = {s_: 0 for s_ in semn}

    def inc(instr, name, n=1):
        instr.then_inc(sems[name], n)
        cnt[name] += n
        return cnt[name]

    def wt(eng, name, val=None):
        v = cnt[name] if val is None else val
        if v > 0:
            eng.wait_ge(sems[name], v)

    x_in = nc.dram_tensor("x", [n_rows, S, D], F32, kind="ExternalInput").ap()
    out_d = nc.dram_tensor("out", [n_rows, S, D], F32, kind="ExternalOutput").ap()

    def din(name, shape):
        return nc.dram_tensor(name, shape, F32, kind="ExternalInput").ap()
    ln1_g, ln1_b = din("ln1_g", [D]), din("ln1_b", [D])
    ln2_g, ln2_b = din("ln2_g", [D]), din("ln2_b", [D])
    W1d, b1d = din("W1", [D, DFF]), din("b1", [DFF])
    W2d, b2d = din("W2", [DFF, D]), din("b2", [D])
    rtd = din("rt", [D])
    gam_l, w1ld, b1ld = din("gam_l", [D]), din("w1l", [D, LH]), din("b1l", [LH])
    w2ld, b2ld = din("w2l", [LH, D]), din("b2l", [D])
    gam_h, w1hd, b1hd = din("gam_h", [D]), din("w1h", [D, DFF]), din("b1h", [DFF])
    w2hd, b2hd = din("w2h", [DFF, D]), din("b2h", [D])

    W1bf = nc.dram_tensor("W1bf", [D, DFF], BF16, kind="Internal").ap()
    W2bf = nc.dram_tensor("W2bf", [DFF, D], BF16, kind="Internal").ap()
    w2hbf = nc.dram_tensor("w2hbf", [DFF, D], BF16, kind="Internal").ap()
    w2lbf = nc.dram_tensor("w2lbf", [LH, D], BF16, kind="Internal").ap()
    w1l_f = nc.dram_tensor("w1l_f", [D, LH], BF16, kind="Internal").ap()
    w1h_f = nc.dram_tensor("w1h_f", [D, DFF], BF16, kind="Internal").ap()

    taps = {}
    if taps_on:
        for tname in ("t_attn", "t_x1", "t_ff", "t_y", "t_z"):
            taps[tname] = nc.dram_tensor(tname, [n_rows, S, D], F32, kind="ExternalOutput").ap()
        for tname in ("t_s", "t_r", "t_m", "t_nrm"):
            taps[tname] = nc.dram_tensor(tname, [n_rows, 128, TT], F32, kind="ExternalOutput").ap()

    mk = _np_masks()
    ident_d = nc.inline_tensor(np.eye(128, dtype=np.float32), "ident_c").ap()
    mask01_d = nc.inline_tensor(np.concatenate([mk[0], mk[1]], 1), "mask01_c").ap()
    mask45_d = nc.inline_tensor(np.concatenate([mk[4], mk[5]], 1), "mask45_c").ap()
    ones_row_d = nc.inline_tensor(np.ones((1, 128), np.float32), "onesr_c").ap()
    ones_rowbf_d = nc.inline_tensor(np.ones((1, 128), ml_dtypes.bfloat16), "onesrb_c").ap()
    ones_col_d = nc.inline_tensor(np.ones((128, 1), np.float32), "onesc_c").ap()
    iota_d = nc.inline_tensor(
        np.broadcast_to(np.arange(S, dtype=np.float32)[None, :], (128, S)).copy(), "iota_c").ap()
    nshift_d = nc.inline_tensor(np.full((128, 1), -SHIFT, np.float32), "nshift_c").ap()
    jidx_d = nc.inline_tensor(
        (np.arange(TT, dtype=np.float32)[None, :] * 128
         + np.arange(128, dtype=np.float32)[:, None]).astype(np.float32), "jidx_c").ap()

    off = [SBASE]

    def sba(name, shape, dtype, at=None):
        per = int(np.prod(shape[1:])) * mybir.dt.size(dtype)
        if at is None:
            at = off[0]
            off[0] = at + ((per + 31) // 32) * 32
        h = nc.alloc_sbuf_tensor_at(name, shape, dtype, offset=at)
        return h.ap(), at

    Bt, _ = sba("Bt", [128, TT * D], F32)
    XT_at = off[0]
    XTA, _ = sba("XTA", [128, DT * 8 * 128], F32R)
    XTF, _ = sba("XTF", [128, DT * 8 * 128], BF16, at=XT_at)
    HT_at = off[0]
    HT, _ = sba("HT", [128, FT * TOKC], BF16)
    SCW, _ = sba("SCW", [128, 2048], F32, at=HT_at)
    SCWB, _ = sba("SCWB", [128, 2048], BF16, at=HT_at + 8192)
    CC_at = off[0]
    X0V, _ = sba("X0V", [128, 8 * H * VW], BF16, at=CC_at)
    FLS, _ = sba("FLS", [1, S], F32, at=XT_at + 16384)
    FLN, _ = sba("FLN", [1, S], F32, at=XT_at + 24576)
    PT, _ = sba("PT", [128, 2 * 6 * 256], BF16, at=CC_at + 16640)
    SQJ, _ = sba("SQJ", [128, D], F32, at=CC_at + 16640)
    NRMB0, _ = sba("NRMB0", [128, TOKC], F32, at=CC_at + 16640 + 4096)
    off[0] = CC_at + 16640 + 6144
    W1S, _ = sba("W1S", [128, 2 * DT * 128], BF16)
    W2S, _ = sba("W2S", [128, 2 * D], BF16)
    W2SL, _ = sba("W2SL", [128, (LH // 128) * D], BF16)
    M01, _ = sba("M01", [128, 512], F32, at=HT_at + 8192)
    M45, _ = sba("M45", [128, 512], F32, at=HT_at + 12288)
    IDT, _ = sba("IDT", [128, 128], F32)
    ONE_R, _ = sba("ONE_R", [1, 128], F32)
    ONE_RB, _ = sba("ONE_RB", [1, 128], BF16)
    ONE_C, _ = sba("ONE_C", [128, 1], F32)
    B2ROW, _ = sba("B2ROW", [1, D], BF16)
    B2LROW, _ = sba("B2LROW", [1, D], BF16)
    B2HROW, _ = sba("B2HROW", [1, D], BF16)
    B1C, _ = sba("B1C", [128, FT], F32)
    B1LC, _ = sba("B1LC", [128, LFT], F32)
    B1HC, _ = sba("B1HC", [128, FT], F32)
    RTB, _ = sba("RTB", [128, D], F32)
    GAML, _ = sba("GAML", [128, DT], F32)
    GAMH, _ = sba("GAMH", [128, DT], F32)
    GB0, _ = sba("GB0", [128, D], F32)
    GB1, _ = sba("GB1", [128, D], F32)
    X0S_at = off[0]
    X0S, _ = sba("X0S", [128, 2 * D], F32)
    RBC, _ = sba("RBC", [128, 512], F32, at=X0S_at)
    IOTC, _ = sba("IOTC", [128, 512], F32, at=X0S_at + 2048)
    GTS, _ = sba("GTS", [128, 512], F32, at=X0S_at + 4096)
    GTS2, _ = sba("GTS2", [128, 512], F32, at=X0S_at + 6144)
    OUT_at = off[0]
    OUTST, _ = sba("OUTST", [128, 2 * D], F32)
    LTM, _ = sba("LTM", [128, 512], F32, at=OUT_at)
    FL2, _ = sba("FL2", [16, 128], F32)
    FL2N, _ = sba("FL2N", [4, 4 * 128], F32)
    NRMB1, _ = sba("NRMB1", [128, TOKC], F32)
    SC16, _ = sba("SC16", [128, TT], F32)
    A16, _ = sba("A16", [128, TT], F32)
    RL16, _ = sba("RL16", [128, TT], F32)
    R16, _ = sba("R16", [128, TT], F32)
    M16, _ = sba("M16", [128, TT], F32)
    NRM, _ = sba("NRM", [128, TT], F32)
    G16, _ = sba("G16", [128, TT], F32)
    RNK, _ = sba("RNK", [128, TT], F32)
    GPART, _ = sba("GPART", [128, TT * 4], F32)
    EPART, _ = sba("EPART", [128, TT * 4], F32)
    JIDX, _ = sba("JIDX", [128, TT], F32)
    STAT, _ = sba("STAT", [128, 40], F32)
    ACOL, _ = sba("ACOL", [128, 1], F32)
    MXC, _ = sba("MXC", [128, 1], F32)
    RSC, _ = sba("RSC", [128, 4], F32)
    SEQS, _ = sba("SEQS", [1, 16], F32)
    NSH, _ = sba("NSH", [128, 1], F32)
    BIASC, _ = sba("BIASC", [128, 1], F32)
    RCONC, _ = sba("RCONC", [128, 1], F32)
    assert off[0] <= 229344, f"sbuf layout overflow: {off[0]}"

    pbank = [es.enter_context(nc.psum_tensor(f"pb{i}", [128, 512], F32)) for i in range(2)]
    psc = [es.enter_context(nc.psum_tensor(f"psc{i}", [128, 1536], F32)) for i in range(2)]
    pool_free = {0: None, 1: None}
    pool_next = [0]

    def pool_acquire(eng):
        k = pool_next[0]
        pool_next[0] = 1 - k
        fa = pool_free[k]
        if fa is not None:
            wt(eng, fa[0], fa[1])
            pool_free[k] = None
        return k

    def pool_release(k, semname, c_):
        pool_free[k] = (semname, c_)

    blk = es.enter_context(nc.Block())
    ck = {}
    pump_all_ref = [None]

    @blk.sync
    def _(se):
        for dst, src in ((IDT[:], ident_d),
                         (ONE_R[:], ones_row_d), (ONE_RB[:], ones_rowbf_d),
                         (ONE_C[:], ones_col_d), (JIDX[:], jidx_d), (NSH[:], nshift_d)):
            inc(se.dma_start(out=dst, in_=src), "dInit", 16)
        inc(se.dma_start(out=SCW[0:32, 0:128], in_=b1d.rearrange("(a b) -> a b", b=128)), "dInit", 16)
        inc(se.dma_start(out=SCW[0:32, 128:256], in_=b1hd.rearrange("(a b) -> a b", b=128)), "dInit", 16)
        inc(se.dma_start(out=SCW[0:4, 256:384], in_=b1ld.rearrange("(a b) -> a b", b=128)), "dInit", 16)
        inc(se.dma_start(out=SCW[0:8, 384:512], in_=gam_l.rearrange("(a b) -> a b", b=128)), "dInit", 16)
        inc(se.dma_start(out=SCW[0:8, 512:640], in_=gam_h.rearrange("(a b) -> a b", b=128)), "dInit", 16)
        inc(se.dma_start(out=SCW[0:1, 640:640 + D], in_=rtd[None, :]), "dInit", 16)

    @blk.gpsimd
    def _(ge):
        inc(ge.dma_start(out=W1bf[:], in_=W1d[:]), "dInitP", 16)
        inc(ge.dma_start(out=W2bf[:], in_=W2d[:]), "dInitP", 16)
        inc(ge.dma_start(out=w2lbf[:], in_=w2ld[:]), "dInitP", 16)
        inc(ge.dma_start(out=w2hbf[:], in_=w2hd[:]), "dInitP", 16)
        inc(ge.dma_start(out=B2ROW[:], in_=b2d[None, :]), "dInitP", 16)
        inc(ge.dma_start(out=B2LROW[:], in_=b2ld[None, :]), "dInitP", 16)
        inc(ge.dma_start(out=B2HROW[:], in_=b2hd[None, :]), "dInitP", 16)
    ck["precast_dP"] = cnt["dInitP"]

    def emit_w2sl_load():
        @blk.sync
        def _(se):
            wt(se, "dInitP", ck["precast_dP"])
            src = w2lbf.rearrange("(fs p) c -> p fs c", p=128)
            inc(se.dma_start(out=W2SL[:].rearrange("p (fs c) -> p fs c", c=D), in_=src),
                "dInit", 16)
        ck["w2sl_dS"] = cnt["dInit"]

    @blk.tensor
    def _(te):
        wt(te, "dInit")
        pt = pbank[0]
        te.transpose(pt[:, 0:32], SCW[0:32, 0:128], IDT[0:32, 0:32])
        te.transpose(pt[:, 32:64], SCW[0:32, 128:256], IDT[0:32, 0:32])
        te.transpose(pt[:, 64:68], SCW[0:4, 256:384], IDT[0:4, 0:4])
        te.transpose(pt[:, 68:76], SCW[0:8, 384:512], IDT[0:8, 0:8])
        i = te.transpose(pt[:, 76:84], SCW[0:8, 512:640], IDT[0:8, 0:8])
        inc(i, "tE")
    ck["setup_t1"] = cnt["tE"]

    @blk.vector
    def _(ve):
        wt(ve, "tE", ck["setup_t1"])
        pt = pbank[0]
        ve.tensor_copy(B1C[:], pt[:, 0:32])
        ve.tensor_copy(B1HC[:], pt[:, 32:64])
        ve.tensor_copy(B1LC[:], pt[:, 64:68])
        ve.tensor_copy(GAML[:], pt[:, 68:76])
        i = ve.tensor_copy(GAMH[:], pt[:, 76:84])
        inc(i, "vE")
        i = ve.memset(X0V[:].rearrange("p (sh e) -> p sh e", e=VW)[:, :, 64:65], 1.0)
        inc(i, "vE")
    ck["setup_v"] = cnt["vE"]
    ck["x0v_ones_setup"] = cnt["vE"]
    pool_free[0] = ("vE", ck["setup_v"])

    for half in range(2):
        @blk.tensor
        def _(te, half=half):
            k = pool_acquire(te)
            ck["_bk"] = k
            i = te.matmul(pbank[k][:, 0:512], ONE_R[:],
                          SCW[0:1, 640 + half * 512:640 + half * 512 + 512], start=True, stop=True)
            inc(i, "tE")
        k = ck["_bk"]

        @blk.vector
        def _(ve, half=half, k=k):
            wt(ve, "tE")
            i = ve.tensor_copy(RTB[:, half * 512:half * 512 + 512], pbank[k][:, 0:512])
            inc(i, "vE")
        pool_release(k, "vE", cnt["vE"])
    ck["rtb_v"] = cnt["vE"]

    fold_jobs = [(w1ld, w1l_f, GAML, dt_i, 0, LH) for dt_i in range(DT)]
    for dt_i in range(DT):
        for cc in range(4):
            fold_jobs.append((w1hd, w1h_f, GAMH, dt_i, cc * 1024, 1024))
    for jj, (wsrc, wdst, gt, dt_i, c0, wdt) in enumerate(fold_jobs):
        sl = jj % 2
        sin = "dInit" if sl == 0 else "dInitB"
        sout = "dInitC" if sl == 0 else "dInitD"

        @blk.scalar
        def _(se, wsrc=wsrc, dt_i=dt_i, c0=c0, wdt=wdt, sl=sl, sin=sin, sout=sout, jj=jj):
            wt(se, sout)
            wt(se, "vE", ck["rtb_v" if jj < 2 else "setup_v"])
            i = se.dma_start(out=SCW[:, sl * 1024:sl * 1024 + wdt],
                             in_=wsrc[dt_i * 128:(dt_i + 1) * 128, c0:c0 + wdt])
            inc(i, sin, 16)

        @blk.vector
        def _(ve, gt=gt, dt_i=dt_i, wdt=wdt, sl=sl, sin=sin):
            wt(ve, sin)
            ve.wait_ge(sems["vE"], cnt["vE"])
            i = ve.tensor_scalar(SCWB[:, sl * 1024:sl * 1024 + wdt],
                                 SCW[:, sl * 1024:sl * 1024 + wdt],
                                 gt[:, dt_i:dt_i + 1], None, ALU.mult)
            inc(i, "vE")
            ck["_fv"] = cnt["vE"]

        @blk.scalar
        def _(sa, wdst=wdst, dt_i=dt_i, c0=c0, wdt=wdt, sl=sl, sout=sout):
            wt(sa, "vE", ck["_fv"])
            i = sa.dma_start(out=wdst[dt_i * 128:(dt_i + 1) * 128, c0:c0 + wdt],
                             in_=SCWB[:, sl * 1024:sl * 1024 + wdt])
            inc(i, sout, 16)
    ck["fold_dP"] = cnt["dInitP"]
    ck["fold_dC"] = cnt["dInitC"]
    ck["fold_dD"] = cnt["dInitD"]

    def rc(ds, tile):
        return ds * (8 * 128) + (tile % 8) * 128

    def emit_xt_ring(t, gates, dst):
        for half in range(2):
            @blk.tensor
            def _(te, t=t, half=half, gates=gates):
                for g in gates:
                    wt(te, g[0], g[1])
                k = pool_acquire(te)
                last = None
                for j in range(4):
                    ds = half * 4 + j
                    last = te.transpose(pbank[k][:, j * 128:j * 128 + 128],
                                        Bt[:, t * D + ds * 128: t * D + ds * 128 + 128],
                                        IDT[:])
                inc(last, "tE")
                ck["_tpk"] = k
            k = ck["_tpk"]

            @blk.vector
            def _(ve, t=t, half=half, k=k, dst=dst):
                wt(ve, "tE")
                dv = dst[:].rearrange("p (a c) -> p a c", c=8 * 128)[
                    :, half * 4:half * 4 + 4, (t % 8) * 128:(t % 8) * 128 + 128]
                sv = pbank[k][:, 0:512].rearrange("p (a c) -> p a c", c=128)
                i = ve.tensor_copy(dv, sv)
                inc(i, "vE")
            pool_release(k, "vE", cnt["vE"])

    def emit_gb_bcast(col0, dst, gate_ds, v_gate=0):
        for half in range(2):
            @blk.tensor
            def _(te, half=half, col0=col0, gate_ds=gate_ds):
                wt(te, "dG", gate_ds)
                k = pool_acquire(te)
                i = te.matmul(pbank[k][:, 0:512], ONE_R[:],
                              FLN[0:1, col0 + half * 512:col0 + half * 512 + 512],
                              start=True, stop=True)
                inc(i, "tE")
                ck["_bk"] = k
            k = ck["_bk"]

            @blk.vector
            def _(ve, half=half, dst=dst, k=k, v_gate=v_gate):
                wt(ve, "tE")
                if v_gate:
                    wt(ve, "vE", v_gate)
                i = ve.tensor_copy(dst[:, half * 512:half * 512 + 512], pbank[k][:, 0:512])
                inc(i, "vE")
            pool_release(k, "vE", cnt["vE"])

    def emit_nrmb(tcq):
        @blk.tensor
        def _(te, tcq=tcq, dn_now=None):
            wt(te, "dN", cnt["dN"])
            wt(te, "vE", ck.get(("nrmbu", tcq - 2), ck.get(("nrmbu", tcq + 2), 0)))
            kk = pool_acquire(te)
            ck["_nbk"] = kk
            i = te.matmul(pbank[kk][:, 0:512], ONE_R[:],
                          FLN[0:1, tcq * 512:(tcq + 1) * 512], start=True, stop=True)
            inc(i, "tE")
        ck[("nrmbp", tcq)] = cnt["tE"]
        kk = ck["_nbk"]

        @blk.vector
        def _(ve, tcq=tcq, kk=kk):
            wt(ve, "tE", ck[("nrmbp", tcq)])
            i = ve.tensor_copy((NRMB0 if tcq % 2 == 0 else NRMB1)[:],
                               pbank[kk][:, 0:512])
            inc(i, "vE")
        pool_release(kk, "vE", cnt["vE"])
        ck[("nrmb_v", tcq)] = cnt["vE"]

    def emit_ln_apply(t, gpar, i_, kind, extra_gates=(), osl=None, osl_key="osl_free"):
        c0 = gpar * 16
        bsl = Bt[:, t * D:(t + 1) * D]
        if osl is None:
            osl = OUTST[:, (t % 2) * D:(t % 2) * D + D]
        sc2c = STAT[:, c0 + 12 + i_:c0 + 13 + i_]
        rsdc = STAT[:, c0 + 8 + i_:c0 + 9 + i_]
        if kind == "ln_aff":
            @blk.scalar
            def _(sa, t=t, sc2c=sc2c, rsdc=rsdc, bsl=bsl, osl=osl, gpar=gpar):
                wt(sa, "vE", ck[("lng_v8", gpar)])
                wt(sa, "vE", ck.get(("osl_free", t % 2), 0))
                for g in extra_gates:
                    wt(sa, g[0], g[1])
                i = sa.activation(osl, bsl, AF.Identity, bias=sc2c, scale=rsdc)
                inc(i, "sE")
                ck["_l9"] = cnt["sE"]

            @blk.vector
            def _(ve, t=t, osl=osl, bsl=bsl):
                wt(ve, "sE", ck["_l9"])
                i = ve.tensor_mul(osl, osl, GB0[:])
                inc(i, "vE")
                ve.wait_ge(sems["vE"], cnt["vE"])
                i = ve.tensor_add(bsl, osl, GB1[:])
                inc(i, "vE")
            ck[("ln_v", t)] = cnt["vE"]
            ck[("osl_free", t % 2)] = cnt["vE"]
            ck[("lng_gate_s", gpar)] = ck["_l9"]
        else:
            dst = bsl if kind == "plain" else osl

            @blk.scalar
            def _(sa, t=t, dst=dst, kind=kind, sc2c=sc2c, rsdc=rsdc, bsl=bsl, gpar=gpar,
                  osl_key=osl_key):
                wt(sa, "vE", ck[("lng_v8", gpar)])
                if kind == "plain_out":
                    wt(sa, "vE", ck.get((osl_key, t % 2), 0))
                for g in extra_gates:
                    wt(sa, g[0], g[1])
                i = sa.activation(dst, bsl, AF.Identity, bias=sc2c, scale=rsdc)
                inc(i, "sE")
            ck[("ln_s", t)] = cnt["sE"]
            ck[("lng_gate_s", gpar)] = cnt["sE"]

    def emit_ln_group(tlist, kind, gpar, gates=(), tile_gates=None, applies=True):
        tlist = list(tlist)
        n = len(tlist)
        c0 = gpar * 16
        ngm = STAT[:, c0:c0 + n]
        var = STAT[:, c0 + 4:c0 + 4 + n]
        rsd = STAT[:, c0 + 8:c0 + 8 + n]
        sc2 = STAT[:, c0 + 12:c0 + 12 + n]

        @blk.vector
        def _(ve, gates=gates, gpar=gpar, tlist=tlist):
            wt(ve, "vE", ck.get(("lng_gate_v", gpar), 0))
            wt(ve, "sE", ck.get(("lng_gate_s", gpar), 0))
            for g in gates:
                wt(ve, g[0], g[1])
            for i_, t in enumerate(tlist):
                if tile_gates is not None:
                    for g in tile_gates(t):
                        wt(ve, g[0], g[1])
                bsl = Bt[:, t * D:(t + 1) * D]
                i = ve.reduce_sum(STAT[:, c0 + i_:c0 + i_ + 1], bsl, axis=AX.X, negate=True)
                inc(i, "vE")
                ve.wait_ge(sems["vE"], cnt["vE"])
                i = ve.scalar_tensor_tensor(SQJ[:], bsl, 1.0, bsl, ALU.mult, ALU.mult,
                                            accum_out=STAT[:, c0 + 4 + i_:c0 + 5 + i_])
                inc(i, "vE")
                ve.wait_ge(sems["vE"], cnt["vE"])
            i = ve.tensor_scalar(ngm, ngm, 1.0 / D, None, ALU.mult)
            inc(i, "vE")
            ve.wait_ge(sems["vE"], cnt["vE"])
            i = ve.tensor_scalar(var, var, 1.0 / D, EPS_LN, ALU.mult, ALU.add)
            inc(i, "vE")
            ve.wait_ge(sems["vE"], cnt["vE"])
            i = ve.tensor_tensor(sc2, ngm, ngm, ALU.mult)
            inc(i, "vE")
            ve.wait_ge(sems["vE"], cnt["vE"])
            i = ve.tensor_tensor(var, var, sc2, ALU.subtract)
            inc(i, "vE")
            ck["_g4"] = cnt["vE"]

        @blk.scalar
        def _(sa, gpar=gpar):
            wt(sa, "vE", ck["_g4"])
            i = sa.activation(rsd, var, AF.Ln)
            inc(i, "sE")
            sa.wait_ge(sems["sE"], cnt["sE"])
            i = sa.activation(rsd, rsd, AF.Exp, bias=0.0, scale=-0.5)
            inc(i, "sE")
            ck[("lng_s7", gpar)] = cnt["sE"]

        @blk.vector
        def _(ve, gpar=gpar):
            wt(ve, "sE", ck[("lng_s7", gpar)])
            i = ve.tensor_tensor(sc2, ngm, rsd, ALU.mult)
            inc(i, "vE")
        ck[("lng_v8", gpar)] = cnt["vE"]
        ck[("lng_gate_v", gpar)] = cnt["vE"]

        if applies:
            for i_, t in enumerate(tlist):
                emit_ln_apply(t, gpar, i_, kind)

    def pump(n=1):
        q = ck.get("_ROUT")
        for _ in range(n):
            if q:
                q.pop(0)()

    def pump_all():
        q = ck.get("_ROUT")
        while q:
            q.pop(0)()
    pump_all_ref[0] = pump_all

    def ffn_pass(tc, w1src, b1col, w2src, b2row, act, masked, nft, tag, hbase=0):
        slot0 = (4 * tc) % 8
        vE0 = ck.get("prev_yev_v", 0)
        sE0 = ck.get("prev_exp_s", 0)
        pheng, phval = ck.get("prev_hev", ("sE", 0))
        heng = "vE" if act == "relu" else "sE"
        for ft in range(nft):
            @blk.sync
            def _(se, ft=ft, tc=tc, w1src=w1src, tag=tag):
                wt(se, "tE", ck.get(("hmm", tag, tc, ft - 2), ck.get("ffn_prev_t", 0)))
                wt(se, "dInitP", ck["fold_dP"])
                wt(se, "dInitC", ck["fold_dC"])
                wt(se, "dInitD", ck["fold_dD"])
                src = w1src[:, ft * 128:(ft + 1) * 128].rearrange("(ds p) c -> p ds c", p=128)
                sem1 = "dW1a" if ft % 2 == 0 else "dW1b"
                i = se.dma_start(out=W1S[:, (ft % 2) * 1024:(ft % 2) * 1024 + 1024]
                                 .rearrange("p (ds c) -> p ds c", c=128), in_=src)
                inc(i, sem1, 16)
            ck[("w1s", tag, tc, ft)] = ("dW1a" if ft % 2 == 0 else "dW1b",
                                        cnt["dW1a" if ft % 2 == 0 else "dW1b"])

            @blk.tensor
            def _(te, ft=ft, tc=tc, slot0=slot0, tag=tag, sE0=sE0, vE0=vE0, act=act, hbase=hbase,
                  pheng=pheng, phval=phval, heng=heng):
                wt(te, ck[("w1s", tag, tc, ft)][0], ck[("w1s", tag, tc, ft)][1])
                wt(te, "vE", ck.get(("ring_ok", tc), 0))
                if ft < 2:
                    wt(te, "vE", vE0)
                    wt(te, pheng, phval)
                wt(te, "sE", sE0)
                wt(te, heng, ck.get(("hev", tag, tc, ft - 2), 0))
                if act == "gelu":
                    wt(te, "vE", max(ck.get("prev_hpre_v", 0) if ft < 2 else 0,
                                     ck.get(("hpre", tag, tc, ft - 2), 0)))
                pt = psc[0][:, (ft % 2) * 512:(ft % 2) * 512 + 512]
                last = None
                for ds in range(DT):
                    last = te.matmul(
                        pt, W1S[:, (ft % 2) * 1024 + ds * 128:(ft % 2) * 1024 + ds * 128 + 128],
                        XTF[:, ds * 1024 + slot0 * 128: ds * 1024 + slot0 * 128 + 512],
                        start=(ds == 0), stop=(ds == DT - 1))
                inc(last, "tE")
            ck[("hmm", tag, tc, ft)] = cnt["tE"]

            if act == "relu":
                @blk.vector
                def _(ve, ft=ft, tc=tc, b1col=b1col, tag=tag, hbase=hbase):
                    wt(ve, "tE", ck[("hmm", tag, tc, ft)])
                    ve.wait_ge(sems["vE"], cnt["vE"])
                    i = ve.tensor_scalar(HT[:, (hbase + ft) * TOKC:(hbase + ft + 1) * TOKC],
                                         psc[0][:, (ft % 2) * 512:(ft % 2) * 512 + 512],
                                         b1col[:, ft:ft + 1], 0.0, ALU.add, ALU.max)
                    inc(i, "vE")
                ck[("hev", tag, tc, ft)] = cnt["vE"]
            else:
                @blk.vector
                def _(ve, ft=ft, tc=tc, tag=tag, sE0=sE0, pheng=pheng, phval=phval):
                    wt(ve, "tE", ck[("hmm", tag, tc, ft)])
                    wt(ve, "vE", ck.get(("nrmb_v", tc), 0))
                    if ft < 2:
                        wt(ve, pheng, phval)
                    wt(ve, "sE", max(sE0, ck.get(("hev", tag, tc, ft - 2), 0)))
                    i = ve.tensor_tensor(OUTST[:, (ft % 2) * 512 + 1024:(ft % 2) * 512 + 1536],
                                         psc[0][:, (ft % 2) * 512:(ft % 2) * 512 + 512],
                                         (NRMB0 if tc % 2 == 0 else NRMB1)[:], ALU.mult)
                    inc(i, "vE")
                ck[("hpre", tag, tc, ft)] = cnt["vE"]

                @blk.scalar
                def _(sa, ft=ft, tc=tc, b1col=b1col, tag=tag, hbase=hbase):
                    wt(sa, "vE", ck[("hpre", tag, tc, ft)])
                    i = sa.activation(HT[:, (hbase + ft) * TOKC:(hbase + ft + 1) * TOKC],
                                      OUTST[:, (ft % 2) * 512 + 1024:(ft % 2) * 512 + 1536],
                                      AF.Gelu, bias=b1col[:, ft:ft + 1], scale=1.0)
                    inc(i, "sE")
                ck[("hev", tag, tc, ft)] = cnt["sE"]
            if ft % 2 == 1:
                pump(1)

        if masked:
            pump_all()
        @blk.tensor
        def _(te, tc=tc, tag=tag, nft=nft, heng=heng):
            wt(te, heng, ck[("hev", tag, tc, nft - 1)])
            wt(te, heng, ck.get(("hev", tag, tc, nft - 2), 0))
            k6 = pool_acquire(te)
            k7 = pool_acquire(te)
            ck["_yk67"] = (k6, k7)
        k6, k7 = ck["_yk67"]
        ybank = {}
        for tt_ in range(4):
            for dh in range(2):
                idx = 2 * tt_ + dh
                if idx < 6:
                    ybank[(tt_, dh)] = psc[idx // 3][:, (idx % 3) * 512:(idx % 3) * 512 + 512]
                else:
                    ybank[(tt_, dh)] = pbank[(k6, k7)[idx - 6]][:]

        use_stream = w2src is not None
        for fs in range(nft):
            if use_stream:
                @blk.sync
                def _(se, fs=fs, tc=tc, tag=tag, w2src=w2src):
                    wt(se, "tE", ck.get(("ymm", tag, tc, fs - 2), 0))
                    wt(se, "dInitP", ck["precast_dP"])
                    src = w2src[fs * 128:(fs + 1) * 128, :]
                    sem2 = "dW2a" if fs % 2 == 0 else "dW2b"
                    i = se.dma_start(out=W2S[:, (fs % 2) * D:(fs % 2) * D + D], in_=src)
                    inc(i, sem2, 16)
                ck[("w2s", tag, tc, fs)] = ("dW2a" if fs % 2 == 0 else "dW2b",
                                            cnt["dW2a" if fs % 2 == 0 else "dW2b"])

            @blk.tensor
            def _(te, fs=fs, tc=tc, tag=tag, use_stream=use_stream, vE0=vE0, ybank=ybank, hbase=hbase):
                if use_stream:
                    wt(te, ck[("w2s", tag, tc, fs)][0], ck[("w2s", tag, tc, fs)][1])
                else:
                    wt(te, "dInit", ck["w2sl_dS"])
                if fs == 0:
                    wt(te, "vE", max(vE0, ck.get(("yev_all", tag, tc), 0)))
                last = None
                for tt_ in range(4):
                    for dh in range(2):
                        pt = ybank[(tt_, dh)]
                        rhs = (W2S[:, (fs % 2) * D + dh * 512:(fs % 2) * D + dh * 512 + 512]
                               if use_stream
                               else W2SL[:, fs * D + dh * 512:fs * D + dh * 512 + 512])
                        last = te.matmul(pt,
                                         HT[:, (hbase + fs) * TOKC + tt_ * 128:
                                            (hbase + fs) * TOKC + tt_ * 128 + 128],
                                         rhs, start=(fs == 0), stop=False)
                inc(last, "tE")
            ck[("ymm", tag, tc, fs)] = cnt["tE"]

        @blk.tensor
        def _(te, tc=tc, tag=tag, b2row=b2row, ybank=ybank):
            last = None
            for tt_ in range(4):
                for dh in range(2):
                    last = te.matmul(ybank[(tt_, dh)], ONE_RB[:],
                                     b2row[0:1, dh * 512:dh * 512 + 512],
                                     start=False, stop=True)
            inc(last, "tE")
        ck[("yb2", tag, tc)] = cnt["tE"]

        for tt_ in range(4):
            t_abs = tc * 4 + tt_
            for dh in range(2):
                @blk.vector
                def _(ve, tt_=tt_, dh=dh, tc=tc, t_abs=t_abs, masked=masked, tag=tag, ybank=ybank):
                    wt(ve, "tE", ck[("yb2", tag, tc)])
                    wt(ve, "dT", ck.get("hold", 0))
                    pt = ybank[(tt_, dh)]
                    bsl = Bt[:, t_abs * D + dh * 512: t_abs * D + dh * 512 + 512]
                    if masked:
                        wt(ve, "vE", ck.get("mask_v", 0))
                        i = ve.scalar_tensor_tensor(bsl, pt, M16[:, t_abs:t_abs + 1], bsl,
                                                    ALU.mult, ALU.add)
                    else:
                        i = ve.tensor_add(bsl, bsl, pt)
                    inc(i, "vE")
                ck[("yev", tag, tc, tt_, dh)] = cnt["vE"]
        ck[("yev_all", tag, tc)] = cnt["vE"]
        pool_release(k6, "vE", cnt["vE"])
        pool_release(k7, "vE", cnt["vE"])
        ck["ffn_prev_t"] = cnt["tE"]
        ck["prev_yev_v"] = cnt["vE"]
        ck["prev_hev"] = (heng, ck[("hev", tag, tc, nft - 1)])
        if act == "gelu":
            ck["prev_hpre_v"] = ck[("hpre", tag, tc, nft - 1)]

    for r in range(n_rows):
        @blk.sync
        def _(se, r=r):
            wt(se, "dOa", ck.get(("row_st_a", r - 1), 0))
            wt(se, "dOb", ck.get(("row_st_b", r - 1), 0))
            wt(se, "sE", ck.get(("row_lastln", r - 1), 0))
            src = x_in[r].rearrange("(t p) c -> p t c", p=128)
            i = se.dma_start(out=Bt[:].rearrange("p (t c) -> p t c", c=D), in_=src)
            inc(i, "dX", 16)
            wt(se, "tE", ck.get(("hv_done", r - 1), 0))
            if r == 0:
                wt(se, "dInitC", ck["fold_dC"])
                wt(se, "dInitD", ck["fold_dD"])
            inc(se.dma_start(out=M01[:], in_=mask01_d), "dM", 16)
            inc(se.dma_start(out=M45[:], in_=mask45_d), "dM", 16)
        ck[("x0", r)] = cnt["dX"]
        ck[("msk", r)] = cnt["dM"]
        if r == 0:
            emit_w2sl_load()

        def emit_x0v(t, extra_gate_t):
            sl = t % 8

            @blk.vector
            def _(ve, t=t, r=r, sl=sl, extra_gate_t=extra_gate_t):
                wt(ve, "dX", ck[("x0", r)])
                if extra_gate_t:
                    wt(ve, "tE", extra_gate_t)
                src = Bt[:, t * D:(t + 1) * D].rearrange("p (h d) -> p h d", d=64)
                blkv = X0V[:, sl * H * VW:(sl + 1) * H * VW].rearrange("p (h e) -> p h e", e=VW)
                ve.wait_ge(sems["vE"], cnt["vE"])
                i = ve.tensor_copy(blkv[:, :, 0:64], src)
                inc(i, "vE")
            ck[("x0v", r, t)] = cnt["vE"]

        for t in range(4):
            emit_x0v(t, ck.get(("hv_done", r - 1), 0) if t == 0 else 0)
        row0_vE = cnt["vE"]
        row0_sE = cnt["sE"]
        for t in range(4):
            emit_xt_ring(t, [("dX", ck[("x0", r)]),
                             ("tE", ck.get(("hv_done", r - 1), 0))], XTA)
            ck[("rxt", r, t)] = cnt["vE"]

        def scores_emit(c, h):
            par = h % 2
            subs = subtiles_of_chunk(c)
            dt_i = h // 2
            r0 = (h % 2) * 64
            pk = ("exp", c, h - 2) if h >= 2 else ("exp", c - 1, H - 2 + h)

            @blk.tensor
            def _(te, c=c, h=h, par=par, subs=subs, dt_i=dt_i, r0=r0, pk=pk, r=r):
                wt(te, "sE", ck.get(pk, 0))
                if c == 0 and h < 2:
                    wt(te, "vE", row0_vE)
                    wt(te, "sE", row0_sE)
                wt(te, "vE", ck.get(("ring_rdy", c), 0))
                last = None
                for s_ in subs:
                    ktile = 2 * c - 2 + s_
                    col = rc(dt_i, ktile)
                    q0 = rc(dt_i, 2 * c)
                    last = te.matmul(psc[par][:, s_ * 256:s_ * 256 + 256],
                                     XTA[r0:r0 + 64, col:col + 128],
                                     XTA[r0:r0 + 64, q0:q0 + 256],
                                     start=True, stop=True, tile_position=(r0, 0))
                inc(last, "tE")
            ck[("scores", c, h)] = cnt["tE"]

        def mask_emit(c, h):
            par = h % 2
            mp = masked_pairs(c)

            @blk.vector
            def _(ve, c=c, h=h, par=par, mp=mp, r=r):
                wt(ve, "tE", ck[("scores", c, h)])
                wt(ve, "dM", ck[("msk", r)])
                last = None
                for pair in mp:
                    off = 0 if pair == "01" else 1024
                    msk = M01 if pair == "01" else M45
                    last = ve.tensor_tensor(psc[par][:, off:off + 512],
                                            psc[par][:, off:off + 512], msk[:], ALU.add)
                inc(last, "vE")
            ck[("mask", c, h)] = cnt["vE"]

        def exp_emit(c, h):
            par = h % 2
            subs = subtiles_of_chunk(c)
            pvk = ("pv", c, h - 2) if h >= 2 else ("pv", c - 1, H - 2 + h)

            @blk.scalar
            def _(sa, c=c, h=h, par=par, subs=subs, pvk=pvk):
                wt(sa, "vE", ck[("mask", c, h)])
                wt(sa, "tE", ck[("scores", c, h)])
                wt(sa, "tE", ck.get(pvk, 0))
                w0 = min(subs) * 256
                w1 = (max(subs) + 1) * 256
                i = sa.activation(PT[:, par * 1536 + w0: par * 1536 + w1],
                                  psc[par][:, w0:w1], AF.Exp, bias=NSH[:, 0:1], scale=1.0)
                inc(i, "sE")
            ck[("exp", c, h)] = cnt["sE"]

        def pv_emit(c, h):
            par = h % 2
            pairs = pv_pairs(c)

            @blk.tensor
            def _(te, c=c, h=h, par=par, pairs=pairs, r=r):
                wt(te, "sE", ck[("exp", c, h)])
                tk = min(2 * c + 3 if h < 4 else (2 * c + 4 if h < 8 else 2 * c + 5), TT - 1)
                wt(te, "vE", ck[("x0v", r, tk)])
                wt(te, "vE", ck["x0v_ones_setup"])
                k = pool_acquire(te)
                np_ = {q_: sum(1 for s2, q2 in pairs if q2 == q_) for q_ in (0, 1)}
                done = {0: 0, 1: 0}
                lastq = {}
                for qh in (0, 1):
                    for s_, q_ in pairs:
                        if q_ != qh:
                            continue
                        ktile = 2 * c - 2 + s_
                        sl = ktile % 8
                        lhs = PT[:, par * 1536 + s_ * 256 + qh * 128:
                                 par * 1536 + s_ * 256 + qh * 128 + 128]
                        rhs = X0V[:, sl * H * VW + h * VW: sl * H * VW + h * VW + VW]
                        done[qh] += 1
                        lastq[qh] = te.matmul(pbank[k][:, qh * 128:qh * 128 + VW], lhs, rhs,
                                              start=(done[qh] == 1), stop=(done[qh] == np_[qh]))
                for qh in (0, 1):
                    inc(lastq[qh], "tE")
                    ck[("pvq", c, h, qh)] = cnt["tE"]
                ck[("pvk", c, h)] = k
            ck[("pv", c, h)] = cnt["tE"]

        def pvevac_emit(c, h):
            k = ck[("pvk", c, h)]
            for qh in (0, 1):
                @blk.vector
                def _(ve, c=c, h=h, qh=qh, k=k):
                    wt(ve, "tE", ck[("pvq", c, h, 1)])
                    i = ve.reciprocal(RSC[:, qh:qh + 1], pbank[k][:, qh * 128 + 64:qh * 128 + 65])
                    inc(i, "vE")
                    ve.wait_ge(sems["vE"], cnt["vE"])
                    t_abs = 2 * c + qh
                    bsl = Bt[:, t_abs * D + h * 64: t_abs * D + h * 64 + 64]
                    i = ve.scalar_tensor_tensor(bsl, pbank[k][:, qh * 128:qh * 128 + 64],
                                                RSC[:, qh:qh + 1],
                                                bsl, ALU.mult, ALU.add)
                    inc(i, "vE")
                ck[("pvadd", c, h, qh)] = cnt["vE"]
            pool_release(k, "vE", ck[("pvadd", c, h, 1)])

        for c in range(NCH):
            ck[("ring_rdy", c)] = ck[("rxt", r, min(2 * c + 3, TT - 1))]
            for h in range(H):
                if h == 0:
                    scores_emit(c, 0)
                    mask_emit(c, 0)
                    exp_emit(c, 0)
                if h + 1 < H:
                    scores_emit(c, h + 1)
                    mask_emit(c, h + 1)
                    exp_emit(c, h + 1)
                if h in (4, 8):
                    t_new = 2 * c + 4 + (1 if h == 8 else 0)
                    if t_new < TT:
                        emit_x0v(t_new, ck.get(("pv", c, 3 if h == 4 else 7), 0))
                        emit_xt_ring(t_new, [("dX", ck[("x0", r)])], XTA)
                        ck[("rxt", r, t_new)] = cnt["vE"]
                pv_emit(c, h)
                pvevac_emit(c, h)
        ck[("pv_all", r)] = cnt["tE"]
        ck[("attn_v", r)] = cnt["vE"]
        ck["prev_exp_s"] = cnt["sE"]
        ck["prev_yev_v"] = cnt["vE"]
        ck["prev_hev"] = ("sE", cnt["sE"])
        if taps_on:
            @blk.gpsimd
            def _(ge, r=r):
                wt(ge, "vE", ck[("attn_v", r)])
                i = ge.dma_start(out=taps["t_attn"][r].rearrange("(t p) c -> p t c", p=128),
                                 in_=Bt[:].rearrange("p (t c) -> p t c", c=D))
                inc(i, "dT", 16)
            ck["hold"] = cnt["dT"]

        @blk.sync
        def _(se, r=r):
            wt(se, "vE", ck[("attn_v", r)])
            wt(se, "tE", ck[("pv_all", r)])
            if taps_on:
                wt(se, "dT", ck.get("hold", 0))
            i = se.dma_start(out=FLN[0:1, 0:D], in_=ln1_g[None, :])
            inc(i, "dG", 16)
            i = se.dma_start(out=FLN[0:1, D:2 * D], in_=ln1_b[None, :])
            inc(i, "dG", 16)
        ck["gbrow"] = cnt["dG"]
        emit_gb_bcast(0, GB0, ck["gbrow"])
        emit_gb_bcast(D, GB1, ck["gbrow"])
        ck["gb_v"] = cnt["vE"]
        ck["gb1_t"] = cnt["tE"]

        for g4 in range(4):
            emit_ln_group(range(g4 * 4, g4 * 4 + 4), "ln_aff", g4 % 2,
                          gates=[("vE", ck["gb_v"]), ("vE", ck[("attn_v", r)])])
        ck["ln1_all_v"] = cnt["vE"]
        for t in range(8):
            emit_xt_ring(t, [("vE", ck[("ln_v", t)]), ("tE", ck.get("ffn_prev_t", 0))], XTF)
        ck[("ring_ok", 0)] = cnt["vE"]
        ck[("ring_ok", 1)] = cnt["vE"]

        @blk.sync
        def _(se, r=r):
            wt(se, "tE", ck["gb1_t"])
            wt(se, "vE", ck[("attn_v", r)])
            if taps_on:
                wt(se, "dT", ck.get("hold", 0))
            i = se.dma_start(out=FLN[0:1, 0:D], in_=ln2_g[None, :])
            inc(i, "dG", 16)
            i = se.dma_start(out=FLN[0:1, D:2 * D], in_=ln2_b[None, :])
            inc(i, "dG", 16)
        ck["gbrow2"] = cnt["dG"]
        emit_gb_bcast(0, GB0, ck["gbrow2"], v_gate=ck["ln1_all_v"])
        emit_gb_bcast(D, GB1, ck["gbrow2"], v_gate=ck["ln1_all_v"])
        ck["gb2_v"] = cnt["vE"]
        ck["gb2_t"] = cnt["tE"]

        def p4_chunk(tcq):
            ts4 = list(range(tcq * 4, tcq * 4 + 4))
            gpar = tcq % 2
            emit_ln_group(ts4, "ln_aff", gpar,
                          gates=[("vE", ck["gb2_v"]), ("vE", ck[("yev_all", "a", tcq)])])

            for t in ts4:
                par = t % 2

                @blk.sync
                def _(se, t=t, r=r, par=par, tcq=tcq):
                    wt(se, "vE", ck.get(("resid", t - 2), 0))
                    wt(se, "vE", ck.get(("sdot", tcq * 4 - 1), 0))
                    semr = "dRa" if par == 0 else "dRb"
                    i = se.dma_start(out=X0S[:, par * D:(par + 1) * D],
                                     in_=x_in[r][t * 128:(t + 1) * 128, :])
                    inc(i, semr, 16)
                ck[("x0rr", t)] = ("dRa" if par == 0 else "dRb",
                                  cnt["dRa" if par == 0 else "dRb"])

                @blk.vector
                def _(ve, t=t, par=par):
                    wt(ve, ck[("x0rr", t)][0], ck[("x0rr", t)][1])
                    wt(ve, "vE", ck[("ln_v", t)])
                    bsl = Bt[:, t * D:(t + 1) * D]
                    i = ve.tensor_add(bsl, bsl, X0S[:, par * D:(par + 1) * D])
                    inc(i, "vE")
                ck[("resid", t)] = cnt["vE"]

            emit_ln_group(ts4, "plain", gpar,
                          tile_gates=lambda t: [("vE", ck[("resid", t)])])

            c0g = gpar * 16
            @blk.vector
            def _(ve, tcq=tcq, c0g=c0g, gpar=gpar):
                wt(ve, "sE", ck[("lng_s7", gpar)])
                g1 = G16[:, tcq * 4:tcq * 4 + 4]
                i = ve.tensor_tensor(g1, STAT[:, c0g + 8:c0g + 12],
                                     STAT[:, c0g + 8:c0g + 12], ALU.mult)
                inc(i, "vE")
                ve.wait_ge(sems["vE"], cnt["vE"])
                i = ve.tensor_scalar(NRM[:, tcq * 4:tcq * 4 + 4], g1,
                                     -float(D) * EPS_LN, float(D), ALU.mult, ALU.add)
                inc(i, "vE")
                ve.wait_ge(sems["vE"], cnt["vE"])
                i = ve.tensor_scalar(NRM[:, tcq * 4:tcq * 4 + 4], NRM[:, tcq * 4:tcq * 4 + 4],
                                     1e-24, None, ALU.max)
                inc(i, "vE")
            ck[("nq0", tcq)] = cnt["vE"]
            ck[("lng_gate_v", gpar)] = cnt["vE"]

            for t in ts4:
                @blk.vector
                def _(ve, t=t, tcq=tcq):
                    wt(ve, "sE", ck[("ln_s", t)])
                    wt(ve, "vE", ck["rtb_v"])
                    wt(ve, ck[("x0rr", tcq * 4 + 3)][0], ck[("x0rr", tcq * 4 + 3)][1])
                    ve.wait_ge(sems["vE"], cnt["vE"])
                    i = ve.scalar_tensor_tensor(GTS[:, 0:512], Bt[:, t * D:t * D + 512], 1.0,
                                                RTB[:, 0:512], ALU.mult, ALU.mult,
                                                accum_out=SC16[:, t:t + 1])
                    inc(i, "vE")
                    i = ve.scalar_tensor_tensor(GTS2[:, 0:512], Bt[:, t * D + 512:t * D + D], 1.0,
                                                RTB[:, 512:D], ALU.mult, ALU.mult,
                                                accum_out=A16[:, t:t + 1])
                    inc(i, "vE")
                ck[("sdot", t)] = cnt["vE"]

            @blk.scalar
            def _(sa, tcq=tcq):
                wt(sa, "vE", ck[("nq0", tcq)])
                g1 = G16[:, tcq * 4:tcq * 4 + 4]
                i = sa.activation(g1, NRM[:, tcq * 4:tcq * 4 + 4], AF.Ln)
                inc(i, "sE")
                sa.wait_ge(sems["sE"], cnt["sE"])
                i = sa.activation(g1, g1, AF.Exp, bias=0.0, scale=-0.5)
                inc(i, "sE")
            ck[("nq1", tcq)] = cnt["sE"]

            @blk.vector
            def _(ve, tcq=tcq):
                wt(ve, "sE", ck[("nq1", tcq)])
                i = ve.tensor_scalar(NRM[:, tcq * 4:tcq * 4 + 4],
                                     G16[:, tcq * 4:tcq * 4 + 4], SQRT_D, None, ALU.mult)
                inc(i, "vE")
            ck[("nq2", tcq)] = cnt["vE"]

            @blk.tensor
            def _(te, tcq=tcq):
                wt(te, "vE", ck[("nq2", tcq)])
                k = pool_acquire(te)
                ck["_nqk"] = k
                i = te.transpose(pbank[k][0:4, 0:128], NRM[:, tcq * 4:tcq * 4 + 4], IDT[:])
                inc(i, "tE")
            ck[("nq3", tcq)] = cnt["tE"]
            k = ck["_nqk"]

            @blk.vector
            def _(ve, tcq=tcq, k=k):
                wt(ve, "tE", ck[("nq3", tcq)])
                i = ve.tensor_copy(FL2N[0:4, tcq * 128:tcq * 128 + 128], pbank[k][0:4, 0:128])
                inc(i, "vE")
            pool_release(k, "vE", cnt["vE"])
            ck[("nq4", tcq)] = cnt["vE"]

            @blk.sync
            def _(se, tcq=tcq):
                wt(se, "vE", ck[("nq4", tcq)])
                wt(se, "tE", ck["gb2_t"])
                for j4 in range(4):
                    i = se.dma_start(out=FLN[0:1, tcq * 512 + j4 * 128:tcq * 512 + j4 * 128 + 128],
                                     in_=FL2N[j4:j4 + 1, tcq * 128:tcq * 128 + 128])
                    inc(i, "dN", 16)
            ck[("nq5", tcq)] = cnt["dN"]

        ffn_pass(0, W1bf, B1C, W2bf, B2ROW, "relu", False, FT, "a")
        p4_chunk(0)
        for t in range(8, 12):
            emit_xt_ring(t, [("vE", ck[("ln_v", t)]), ("tE", ck[("hmm", "a", 0, FT - 1)])], XTF)
        ck[("ring_ok", 2)] = cnt["vE"]
        ffn_pass(1, W1bf, B1C, W2bf, B2ROW, "relu", False, FT, "a")
        p4_chunk(1)
        for t in range(12, 16):
            emit_xt_ring(t, [("vE", ck[("ln_v", t)]), ("tE", ck[("hmm", "a", 1, FT - 1)])], XTF)
        ck[("ring_ok", 3)] = cnt["vE"]
        ffn_pass(2, W1bf, B1C, W2bf, B2ROW, "relu", False, FT, "a")
        p4_chunk(2)
        for t in range(0, 4):
            emit_xt_ring(t, [("sE", ck[("ln_s", t)]), ("tE", ck[("hmm", "a", 2, FT - 1)])], XTF)
            ck[("yrg", t)] = cnt["vE"]
        ck[("ring_ok", 0)] = ck[("yrg", 3)]
        emit_nrmb(0)
        ffn_pass(0, w1l_f, B1LC, None, B2LROW, "gelu", False, LFT, "l", hbase=0)
        ffn_pass(3, W1bf, B1C, W2bf, B2ROW, "relu", False, FT, "a")
        p4_chunk(3)
        for t in range(4, 8):
            emit_xt_ring(t, [("sE", ck[("ln_s", t)]), ("tE", ck[("hmm", "a", 3, FT - 1)])], XTF)
            ck[("yrg", t)] = cnt["vE"]
        ck[("ring_ok", 1)] = ck[("yrg", 7)]
        emit_nrmb(1)
        ffn_pass(1, w1l_f, B1LC, None, B2LROW, "gelu", False, LFT, "l", hbase=4)
        ck["affn_v"] = cnt["vE"]
        if taps_on:
            @blk.gpsimd
            def _(ge, r=r):
                wt(ge, "vE", ck["affn_v"])
                i = ge.dma_start(out=taps["t_ff"][r].rearrange("(t p) c -> p t c", p=128),
                                 in_=Bt[:].rearrange("p (t c) -> p t c", c=D))
                inc(i, "dT", 16)
            ck["hold"] = cnt["dT"]

        @blk.vector
        def _(ve):
            ve.wait_ge(sems["vE"], ck[("sdot", TT - 1)])
            i = ve.tensor_tensor(SC16[:], SC16[:], A16[:], ALU.add)
            inc(i, "vE")
        ck["s_p"] = cnt["vE"]
        if taps_on:
            @blk.gpsimd
            def _(ge, r=r):
                wt(ge, "sE", ck[("ln_s", TT - 1)])
                wt(ge, "vE", ck[("resid", TT - 1)])
                i = ge.dma_start(out=taps["t_y"][r].rearrange("(t p) c -> p t c", p=128),
                                 in_=Bt[:].rearrange("p (t c) -> p t c", c=D))
                inc(i, "dT", 16)
            ck["hold"] = cnt["dT"]
            ck["bt_hold_dT"] = cnt["dT"]


        ROUT = []

        def _first_step():
            @blk.vector
            def _(ve):
                wt(ve, "vE", ck["s_p"])
                i = ve.tensor_scalar(RL16[:], SC16[:], 0.0, None, ALU.max)
                inc(i, "vE")
                ve.wait_ge(sems["vE"], cnt["vE"])
                i = ve.memset(ACOL[:], 0.0)
                inc(i, "vE")
                i = ve.memset(MXC[:], 0.0)
                inc(i, "vE")
                i = ve.memset(BIASC[:], RCONST)
                inc(i, "vE")
                i = ve.memset(RCONC[:], RCONST)
                inc(i, "vE")
            ck["dsc_v"] = cnt["vE"]
            ck["dsc_rl"] = 0
            ck["dsc_acol_v"] = cnt["vE"]
        ROUT.append(_first_step)

        def _iter_step(it):
            @blk.vector
            def _(ve, it=it):
                wt(ve, "vE", ck["dsc_v"] if it == 0 else ck["dsc_acol_v"])
                if it > 0:
                    wt(ve, "sE", ck["dsc_rl"])
                i = ve.scalar_tensor_tensor(A16[:], SC16[:], 1.0, RL16[:], ALU.mult, ALU.subtract)
                inc(i, "vE")
                ck["_d1"] = cnt["vE"]

            @blk.scalar
            def _(sa, it=it):
                wt(sa, "vE", ck["_d1"])
                sa.wait_ge(sems["sE"], cnt["sE"])
                i = sa.activation(GTS[:, 0:TT], A16[:], AF.Exp, bias=MXC[:, 0:1], scale=10.0,
                                  accum_out=STAT[:, 33:34])
                inc(i, "sE")
                ck["_d8"] = cnt["sE"]

            @blk.tensor
            def _(te):
                wt(te, "sE", ck["_d8"])
                k3 = pool_acquire(te)
                ck["_dk3"] = k3
                i = te.matmul(pbank[k3][0:1, 0:1], STAT[:, 33:34], ONE_C[:, 0:1],
                              start=True, stop=True)
                inc(i, "tE")
                ck["_d9"] = cnt["tE"]
            k3 = ck["_dk3"]

            @blk.scalar
            def _(sa, k3=k3):
                wt(sa, "tE", ck["_d9"])
                i = sa.activation(SEQS[0:1, 1:2], pbank[k3][0:1, 0:1], AF.Ln)
                inc(i, "sE")
                ck["_d10"] = cnt["sE"]
            pool_release(k3, "sE", ck["_d10"])

            @blk.tensor
            def _(te):
                wt(te, "sE", ck["_d10"])
                k4 = pool_acquire(te)
                ck["_dk4"] = k4
                i = te.matmul(pbank[k4][:, 0:1], ONE_R[:], SEQS[0:1, 1:2], start=True, stop=True)
                inc(i, "tE")
                ck["_d13"] = cnt["tE"]
            k4 = ck["_dk4"]

            @blk.scalar
            def _(sa, it=it, k4=k4):
                wt(sa, "tE", ck["_d13"])
                i = sa.activation(ACOL[:], pbank[k4][:, 0:1], AF.Identity,
                                  bias=BIASC[:, 0:1], scale=-0.1)
                inc(i, "sE")
                sa.wait_ge(sems["sE"], cnt["sE"])
                i = sa.activation(BIASC[:], ACOL[:], AF.Identity, bias=RCONC[:, 0:1], scale=1.0)
                inc(i, "sE")
                sa.wait_ge(sems["sE"], cnt["sE"])
                i = sa.activation(MXC[:], ACOL[:], AF.Identity, bias=0.0, scale=10.0)
                inc(i, "sE")
                sa.wait_ge(sems["sE"], cnt["sE"])
                i = sa.activation(RL16[:], SC16[:], AF.Relu, bias=ACOL[:, 0:1], scale=1.0)
                inc(i, "sE")
                ck["dsc_rl"] = cnt["sE"]
            pool_release(k4, "sE", ck["dsc_rl"])
        for it in range(N_ITERS):
            ROUT.append(lambda it=it: _iter_step(it))

        def _scores_step():
            @blk.vector
            def _(ve):
                wt(ve, "sE", ck["dsc_rl"])
                i = ve.scalar_tensor_tensor(A16[:], SC16[:], ACOL[:, 0:1], RL16[:],
                                            ALU.add, ALU.subtract)
                inc(i, "vE")
            ck["r1"] = cnt["vE"]

            @blk.scalar
            def _(sa):
                wt(sa, "vE", ck["r1"])
                i = sa.activation(R16[:], A16[:], AF.Exp, bias=0.0, scale=10.0)
                inc(i, "sE")
            ck["r2"] = cnt["sE"]

            @blk.vector
            def _(ve):
                wt(ve, "sE", ck["r2"])
                i = ve.tensor_scalar(R16[:], R16[:], FTZ_DN, FTZ_UP, ALU.mult, ALU.mult)
                inc(i, "vE")
            ck["r3"] = cnt["vE"]

            @blk.tensor
            def _(te):
                wt(te, "vE", ck["r3"])
                k = pool_acquire(te)
                ck["_rk"] = k
                i = te.transpose(pbank[k][0:16, 0:128], R16[:], IDT[:])
                inc(i, "tE")
            ck["r4"] = cnt["tE"]
            k = ck["_rk"]

            @blk.vector
            def _(ve, k=k):
                wt(ve, "tE", ck["r4"])
                i = ve.tensor_copy(FL2[:], pbank[k][0:16, 0:128])
                inc(i, "vE")
            pool_release(k, "vE", cnt["vE"])
            ck["r5"] = cnt["vE"]

            @blk.sync
            def _(se, r=r):
                wt(se, "vE", ck["r5"])
                wt(se, "tE", ck[("pv_all", r)])
                for i16 in range(16):
                    i = se.dma_start(out=FLS[0:1, i16 * 128:(i16 + 1) * 128],
                                     in_=FL2[i16:i16 + 1, :])
                    inc(i, "dF", 16)
            ck["r6"] = cnt["dF"]
        ROUT.append(_scores_step)

        def _rank_prep(c4):
            @blk.tensor
            def _(te, c4=c4):
                wt(te, "dF", ck["r6"])
                wt(te, "vE", ck.get(("rbu", c4 - 1), 0))
                k = pool_acquire(te)
                ck["_rbk"] = k
                i = te.matmul(pbank[k][:, 0:512], ONE_R[:], FLS[0:1, c4 * 512:(c4 + 1) * 512],
                              start=True, stop=True)
                inc(i, "tE")
            ck[("rb", c4)] = cnt["tE"]
            k = ck["_rbk"]

            @blk.vector
            def _(ve, c4=c4, k=k):
                wt(ve, "tE", ck[("rb", c4)])
                xr = ck.get(("x0rr", TT - 1), None)
                if xr is not None:
                    wt(ve, xr[0], xr[1])
                wt(ve, "vE", ck[("sdot", TT - 1)])
                i = ve.tensor_copy(RBC[:], pbank[k][:, 0:512])
                inc(i, "vE")
            pool_release(k, "vE", cnt["vE"])
            ck[("rbc", c4)] = cnt["vE"]

            @blk.sync
            def _(se, c4=c4):
                iu = ck.get(("iotu", c4 - 1), None)
                if iu is not None:
                    wt(se, iu[0], iu[1])
                wt(se, "vE", ck[("sdot", TT - 1)])
                i = se.dma_start(out=IOTC[:], in_=iota_d[:, c4 * 512:(c4 + 1) * 512])
                inc(i, "dI", 16)
            ck[("iot", c4)] = cnt["dI"]

        def _rank_step(c4, it):
            @blk.vector
            def _(ve, c4=c4, it=it):
                wt(ve, "dI", ck[("iot", c4)])
                wt(ve, "vE", ck[("rbc", c4)])
                wt(ve, "vE", ck.get(("eqv", c4, it - 1),
                                    ck.get(("eqv", c4 - 1, TT - 1), 0)))
                i = ve.tensor_scalar(LTM[:], IOTC[:], JIDX[:, it:it + 1], None, ALU.is_lt)
                inc(i, "vE")
                i = ve.tensor_scalar(GTS[:], RBC[:], R16[:, it:it + 1], None, ALU.is_gt,
                                     ALU.add,
                                     accum_out=GPART[:, it * 4 + c4:it * 4 + c4 + 1])
                inc(i, "vE")
                ve.wait_ge(sems["vE"], cnt["vE"])
                i = ve.scalar_tensor_tensor(GTS2[:], RBC[:], R16[:, it:it + 1], LTM[:],
                                            ALU.is_equal, ALU.mult,
                                            accum_out=EPART[:, it * 4 + c4:it * 4 + c4 + 1])
                inc(i, "vE")
            ck[("eqv", c4, it)] = cnt["vE"]
            if it == TT - 1:
                ck[("rbu", c4)] = ck[("eqv", c4, TT - 1)]
                ck[("iotu", c4)] = ("vE", ck[("eqv", c4, TT - 1)])

        for c4 in range(4):
            ROUT.append(lambda c4=c4: _rank_prep(c4))
            for it2 in range(0, TT, 2):
                ROUT.append(lambda c4=c4, it2=it2: (_rank_step(c4, it2), _rank_step(c4, it2 + 1)))

        def _mask_step():
            @blk.vector
            def _(ve):
                wt(ve, "vE", ck[("eqv", 3, TT - 1)])
                i = ve.reduce_sum(G16[:], GPART[:].rearrange("p (a b) -> p a b", b=4), axis=AX.X)
                inc(i, "vE")
                i = ve.reduce_sum(RNK[:], EPART[:].rearrange("p (a b) -> p a b", b=4), axis=AX.X)
                inc(i, "vE")
            ck["m1"] = cnt["vE"]

            @blk.vector
            def _(ve):
                ve.wait_ge(sems["vE"], ck["m1"])
                i = ve.tensor_tensor(RNK[:], RNK[:], G16[:], ALU.add)
                inc(i, "vE")
                ve.wait_ge(sems["vE"], cnt["vE"])
                i = ve.tensor_scalar(M16[:], RNK[:], 1024.0, None, ALU.is_lt)
                inc(i, "vE")
            ck["mask_v"] = cnt["vE"]
        ROUT.append(_mask_step)
        ck["_ROUT"] = ROUT

        ck["mask_v"] = cnt["vE"]
        if taps_on:
            pump_all_ref[0]()
            @blk.gpsimd
            def _(ge, r=r):
                wt(ge, "vE", ck["mask_v"])
                for tn, tt2 in (("t_m", M16), ("t_r", R16), ("t_nrm", NRM), ("t_s", SC16)):
                    i = ge.dma_start(out=taps[tn][r], in_=tt2[:])
                    inc(i, "dT", 16)
            ck["hold"] = cnt["dT"]

        ck[("ring_ok", 0)] = ck[("yrg", 3)]
        ck[("ring_ok", 1)] = ck[("yrg", 7)]
        ffn_pass(0, w1h_f, B1HC, w2hbf, B2HROW, "gelu", True, FT, "h")
        ck[("nrmbu", 0)] = cnt["vE"]
        for t in range(8, 12):
            emit_xt_ring(t, [("sE", ck[("ln_s", t)]),
                             ("tE", ck[("hmm", "h", 0, FT - 1)])], XTF)
        ck[("ring_ok", 2)] = cnt["vE"]
        emit_nrmb(2)
        ffn_pass(2, w1l_f, B1LC, None, B2LROW, "gelu", False, LFT, "l", hbase=8)
        ffn_pass(1, w1h_f, B1HC, w2hbf, B2HROW, "gelu", True, FT, "h")
        ck[("nrmbu", 1)] = cnt["vE"]
        for t in range(12, 16):
            emit_xt_ring(t, [("sE", ck[("ln_s", t)]),
                             ("tE", ck[("hmm", "h", 1, FT - 1)])], XTF)
        ck[("ring_ok", 3)] = cnt["vE"]
        emit_nrmb(3)
        ffn_pass(3, w1l_f, B1LC, None, B2LROW, "gelu", False, LFT, "l", hbase=12)
        ffn_pass(2, w1h_f, B1HC, w2hbf, B2HROW, "gelu", True, FT, "h")
        ck[("nrmbu", 2)] = cnt["vE"]
        ffn_pass(3, w1h_f, B1HC, w2hbf, B2HROW, "gelu", True, FT, "h")
        ck[("nrmbu", 3)] = cnt["vE"]
        pump_all()
        ck[("hv_done", r)] = cnt["tE"]
        ck["z_v"] = cnt["vE"]
        if taps_on:
            @blk.gpsimd
            def _(ge, r=r):
                wt(ge, "vE", ck["z_v"])
                i = ge.dma_start(out=taps["t_z"][r].rearrange("(t p) c -> p t c", p=128),
                                 in_=Bt[:].rearrange("p (t c) -> p t c", c=D))
                inc(i, "dT", 16)
            ck["hold"] = cnt["dT"]

        for _p in range(2):
            ck[("lng_gate_v", _p)] = 0
            ck[("lng_gate_s", _p)] = 0
        ck[("osl_free", 0)] = 0
        ck[("osl_free", 1)] = 0
        for tcq in range(4):
            ts4 = list(range(tcq * 4, tcq * 4 + 4))
            gpar = tcq % 2
            gts = [("vE", ck["z_v"])]
            if taps_on:
                gts.append(("dT", ck.get("hold", 0)))
            emit_ln_group(ts4, "plain_out", gpar, gates=gts, applies=False)
            for i_, t in enumerate(ts4):
                par = t % 2
                sto = ck.get(("st", t - 2), None)
                emit_ln_apply(t, gpar, i_, "plain_out",
                              extra_gates=([sto] if sto is not None else []))

                @blk.scalar
                def _(sa, t=t, r=r, par=par):
                    wt(sa, "sE", ck[("ln_s", t)])
                    semo = "dOa" if par == 0 else "dOb"
                    i = sa.dma_start(out=out_d[r][t * 128:(t + 1) * 128, :],
                                     in_=OUTST[:, par * D:(par + 1) * D])
                    inc(i, semo, 16)
                ck[("st", t)] = ("dOa" if par == 0 else "dOb",
                                cnt["dOa" if par == 0 else "dOb"])
        ck[("row_st_a", r)] = cnt["dOa"]
        ck[("row_st_b", r)] = cnt["dOb"]
        ck[("row_lastln", r)] = ck[("ln_s", TT - 1)]

    @blk.sync
    def _(se):
        for s_ in ("dOa", "dOb", "dX", "dM", "dG", "dW1a", "dW1b", "dW2a", "dW2b",
                   "dRa", "dRb", "dF", "dN", "dI", "dInit", "dInitP", "dT", "dV"):
            wt(se, s_)
    es.close()
    return nc


_N_CORES = 8
_ROWS_PER_CORE = 4
_FEED = {"ln1_g": "L0_attn_ln1_g", "ln1_b": "L0_attn_ln1_b", "ln2_g": "L0_attn_ln2_g",
         "ln2_b": "L0_attn_ln2_b", "W1": "L0_attn_W1", "b1": "L0_attn_b1",
         "W2": "L0_attn_W2", "b2": "L0_attn_b2", "rt": "L0_ffn_routing_token",
         "gam_l": "L0_ffn_gamma_light", "w1l": "L0_ffn_w1_light", "b1l": "L0_ffn_b1_light",
         "w2l": "L0_ffn_w2_light", "b2l": "L0_ffn_b2_light",
         "gam_h": "L0_ffn_gamma_heavy", "w1h": "L0_ffn_w1_heavy", "b1h": "L0_ffn_b1_heavy",
         "w2h": "L0_ffn_w2_heavy", "b2h": "L0_ffn_b2_heavy"}


def kernel(**inputs):
    from concourse.bass_utils import run_bass_kernel_spmd
    x = np.ascontiguousarray(np.asarray(inputs["L0_attn_src"], dtype=np.float32))
    w = {k: np.ascontiguousarray(np.asarray(inputs[v], dtype=np.float32))
         for k, v in _FEED.items()}
    nc = build(_ROWS_PER_CORE, taps_on=False)
    in_maps = []
    for c in range(_N_CORES):
        m = {"x": x[c * _ROWS_PER_CORE:(c + 1) * _ROWS_PER_CORE]}
        m.update(w)
        in_maps.append(m)
    res = run_bass_kernel_spmd(nc, in_maps, core_ids=list(range(_N_CORES)))
    out = np.concatenate([np.asarray(r["out"]) for r in res.results], axis=0)
    return out.astype(np.float32, copy=False)
